# version 01
# baseline (speedup 1.0000x reference)
import numpy as np
from contextlib import ExitStack

import concourse.bass as bass
import concourse.mybir as mybir
from concourse.bass_utils import run_bass_kernel_spmd

F32 = mybir.dt.float32
BF16 = mybir.dt.bfloat16
AF = mybir.ActivationFunctionType
ALU = mybir.AluOpType
AX = mybir.AxisListType.X

D = 2048
S = 4096
SQ = 2048
H = 16
DH = 128
DC = 16
INTER = 8192
NT = S // 128
NQT = SQ // 128
QSCALE = 1.0 / float(np.sqrt(128.0))
EPS = 1e-5

_DEBUG_TAPS = []


def build_program(debug=False):
    nc = bass.Bass("TRN2", target_bir_lowering=False)
    SCR = "ExternalOutput" if debug else "Internal"

    x_in = nc.dram_tensor("x", [S, D], F32, kind="ExternalInput").ap()
    w_in_t = nc.dram_tensor("w_in_t", [48, 128, DC * 128], F32, kind="ExternalInput").ap()
    w_out_in = nc.dram_tensor("w_out", [D, D], F32, kind="ExternalInput").ap()
    wgu_t = nc.dram_tensor("wgu_t", [128, 128, DC * 128], F32, kind="ExternalInput").ap()
    wdn_t = nc.dram_tensor("wdn_t", [64, 128, D], F32, kind="ExternalInput").ap()
    lnw_in = nc.dram_tensor("lnw_c", [128, DC], F32, kind="ExternalInput").ap()
    lnb_in = nc.dram_tensor("lnb_c", [128, DC], F32, kind="ExternalInput").ap()
    cos_in = nc.dram_tensor("cos_t", [64, S], F32, kind="ExternalInput").ap()
    sin_in = nc.dram_tensor("sin_t", [64, S], F32, kind="ExternalInput").ap()
    perm_in = nc.dram_tensor("perm", [64, 64], F32, kind="ExternalInput").ap()
    idb_in = nc.dram_tensor("id_bf", [128, 128], BF16, kind="ExternalInput").ap()
    idf_in = nc.dram_tensor("id_f32", [128, 128], F32, kind="ExternalInput").ap()
    onesr_in = nc.dram_tensor("ones_r", [1, 128], F32, kind="ExternalInput").ap()
    onesc_in = nc.dram_tensor("ones_cb", [128, 1], BF16, kind="ExternalInput").ap()

    out_t = nc.dram_tensor("out", [SQ, D], F32, kind="ExternalOutput").ap()

    q_scr = nc.dram_tensor("q_scr", [H, 128, SQ], BF16, kind=SCR).ap()
    k_scr = nc.dram_tensor("k_scr", [H, 128, S], BF16, kind=SCR).ap()
    vt_scr = nc.dram_tensor("vt_scr", [H, 128, S], BF16, kind=SCR).ap()
    x1_scr = nc.dram_tensor("x1_scr", [NQT, 128, D], F32, kind=SCR).ap()
    wgu_bf = nc.dram_tensor("wgu_bf", [128, 128, DC * 128], BF16, kind="Internal").ap()
    wdn_bf = nc.dram_tensor("wdn_bf", [64, 128, D], BF16, kind="Internal").ap()

    with ExitStack() as ctx:
        E = ctx.enter_context

        A = E(nc.sbuf_tensor("arA", [128, 65536], BF16)).ap()
        B = E(nc.sbuf_tensor("arB", [128, 28672], BF16)).ap()
        C = E(nc.sbuf_tensor("arC", [128, 4096], F32)).ap()
        COLS = E(nc.sbuf_tensor("cols", [128, 64], F32)).ap()
        perm_s = E(nc.sbuf_tensor("perm_s", [64, 64], F32)).ap()
        idb = E(nc.sbuf_tensor("idb", [128, 128], BF16)).ap()
        idf = E(nc.sbuf_tensor("idf", [128, 128], F32)).ap()
        ones_r = E(nc.sbuf_tensor("ones_r_s", [1, 128], F32)).ap()
        ones_cb = E(nc.sbuf_tensor("ones_cb_s", [128, 1], BF16)).ap()
        lnw_s = E(nc.sbuf_tensor("lnw_s", [128, DC], F32)).ap()
        lnbb = E(nc.sbuf_tensor("lnbb", [128, DC], BF16)).ap()

        PS = E(nc.psum_tensor("ps", [128, 4096], F32)).ap()

        def bank(i, n=512):
            return PS[:, 512 * i: 512 * i + n]

        def bank_bf(i, cols=128, off=0):
            return PS[:, 512 * i + off // 2: 512 * i + (off + cols) // 2].bitcast(BF16)

        XnT = A
        Ctx = A[:, 0:32768]
        P_T = A[:, 32768:49152]
        WOB = A[:, 32768:65536]
        X1nT = A[:, 0:32768]
        HT = A[:, 32768:65536]

        xb = B[:, 0:12288].bitcast(F32)
        ub = B[:, 12288:16384]
        jnk1 = B[:, 16384:20480].bitcast(F32)

        wbuf = B[:, 0:12288]
        qbuf = B[:, 12288:14336]
        kbuf = B[:, 14336:18432]
        vfb = B[:, 18432:22528]
        xr0 = B[:, 22528:23552].bitcast(F32)
        xr1 = B[:, 23552:24576].bitcast(F32)
        t1b = B[:, 24576:25600].bitcast(F32)
        t2b = B[:, 25600:26624].bitcast(F32)

        def qh(s):
            return B[:, s * 10240: s * 10240 + 2048]

        def kh(s):
            return B[:, s * 10240 + 2048: s * 10240 + 6144]

        def vth(s):
            return B[:, s * 10240 + 6144: s * 10240 + 10240]

        x0b = B[:, 0:8192].bitcast(F32)
        x1f = B[:, 8192:16384].bitcast(F32)
        jnk3 = B[:, 16384:20480].bitcast(F32)

        x1rb = B[:, 0:8192].bitcast(F32)
        u2b = B[:, 8192:12288]
        jnk35 = B[:, 12288:16384].bitcast(F32)

        wgb = B[:, 0:4096]
        wub = B[:, 4096:8192]
        wdb = B[:, 8192:12288]
        x1r4 = B[:, 12288:20480].bitcast(F32)
        x2f = B[:, 20480:28672].bitcast(F32)

        cosb = C[:, 0:2048].bitcast(BF16)
        sinb = C[:, 2048:4096].bitcast(BF16)
        r_bc = C[:, 0:512]
        lrow = C[0:1, 512:1024]
        rrow = C[0:1, 1024:1536]
        nmrow = C[0:1, 1536:1664]
        ncsc = C[0:1, 1664:1665]
        sgb = C[:, 0:1024]

        def col(i, n=1):
            return COLS[:, i:i + n]

        def sem(name):
            return E(nc.semaphore(name))

        pb1, pb2a, pb2b, pb3, pb35 = sem("pb1"), sem("pb2a"), sem("pb2b"), sem("pb3"), sem("pb35")
        x1q, x1c, x1a, x1u, x1t, x1v = (sem(n) for n in
            ("x1q", "x1c", "x1a", "x1u", "x1t", "x1v"))
        xl = [sem(f"xl{i}") for i in range(3)]
        ahd, apb, avb, apq, aev, art, arp, avt, avc, acache = (sem(n) for n in
            ("ahd", "apb", "avb", "apq", "aev", "art", "arp", "avt", "avc", "acache"))
        aw2 = [sem(f"aw{i}") for i in range(2)]
        aqsq, aqsk, aqsv = sem("aqsq"), sem("aqsk"), sem("aqsv")
        b1m, b1x, b1q, bnm, bt1, bv1, bt2, bv2, b2m, bpt, bpv, blv, bt3, bv3, bcx, bhd = (
            sem(n) for n in ("b1m", "b1x", "b1q", "bnm", "bt1", "bv1", "bt2",
                             "bv2", "b2m", "bpt", "bpv", "blv", "bt3", "bv3", "bcx", "bhd"))
        bql2 = [sem(f"bql{i}") for i in range(2)]
        c3w, c3q, c3c, c3a, c3g, c3m, c3e = (sem(n) for n in
            ("c3w", "c3q", "c3c", "c3a", "c3g", "c3m", "c3e"))
        c3x2 = [sem(f"c3x{i}") for i in range(2)]
        c3st2 = [sem(f"c3st{i}") for i in range(2)]
        d5q, d5c, d5a, d5u, d5t, d5v = (sem(n) for n in
            ("d5q", "d5c", "d5a", "d5u", "d5t", "d5v"))
        d5x2 = [sem(f"d5x{i}") for i in range(2)]
        e4g, e4u, e4s, e4h, e4bm, e4e = (sem(n) for n in
            ("e4g", "e4u", "e4s", "e4h", "e4bm", "e4e"))
        e4w2 = [sem(f"e4w{i}") for i in range(2)]
        e4d2 = [sem(f"e4d{i}") for i in range(2)]
        e4x2 = [sem(f"e4x{i}") for i in range(2)]
        e4o2 = [sem(f"e4o{i}") for i in range(2)]
        s_const = sem("s_const")
        s_constg = sem("s_constg")

        block = E(nc.Block())

        CH = []
        for r, n in ((0, 4), (1, 8), (2, 8)):
            for c in range(n):
                CH.append((r, c, n))
        NCH = len(CH)
        NQK = 12

        def fcol(h, r):
            return r * 16 + h

        @block.sync
        def _(s):
            s.dma_start(out=perm_s, in_=perm_in).then_inc(s_const, 16)
            s.dma_start(out=idb, in_=idb_in).then_inc(s_const, 16)
            s.dma_start(out=idf, in_=idf_in).then_inc(s_const, 16)
            s.dma_start(out=ones_r, in_=onesr_in).then_inc(s_const, 16)
            s.dma_start(out=ones_cb, in_=onesc_in).then_inc(s_const, 16)
            s.dma_start(out=lnw_s, in_=lnw_in).then_inc(s_const, 16)

            for i in range(NT):
                if i >= 3:
                    s.wait_ge(x1u, i - 2)
                s.dma_start(out=xb[:, (i % 3) * 2048:(i % 3 + 1) * 2048],
                            in_=x_in[i * 128:(i + 1) * 128, :]).then_inc(xl[i % 3], 16)

            s.nop().then_inc(pb1, 1)
            s.wait_ge(pb1, 5)
            for h in range(H):
                s.wait_ge(arp, h * NQK + 4)
                s.wait_ge(aev, h * NCH + 4)
                s.dma_start(out=q_scr[h], in_=qbuf).then_inc(aqsq, 16)
                s.wait_ge(arp, h * NQK + 12)
                s.wait_ge(aev, h * NCH + 12)
                s.dma_start(out=k_scr[h], in_=kbuf).then_inc(aqsk, 16)
                for g in range(8):
                    s.wait_ge(avc, h * 32 + 4 * (g + 1))
                    s.dma_start(out=vt_scr[h][:, g * 512:(g + 1) * 512],
                                in_=vfb[:, g * 512:(g + 1) * 512]).then_inc(aqsv, 16)

            s.nop().then_inc(pb2a, 1)
            s.wait_ge(pb2a, 5)
            s.nop().then_inc(pb2b, 1)
            s.wait_ge(pb2b, 5)

            s.wait_ge(bpv, 64)
            for qt in range(NQT):
                if qt >= 2:
                    s.wait_ge(c3e, qt - 1)
                s.dma_start(out=x0b[:, (qt % 2) * 2048:(qt % 2 + 1) * 2048],
                            in_=x_in[qt * 128:(qt + 1) * 128, :]).then_inc(c3x2[qt % 2], 16)
                s.wait_ge(c3e, qt + 1)
                s.dma_start(out=x1_scr[qt],
                            in_=x1f[:, (qt % 2) * 2048:(qt % 2 + 1) * 2048]).then_inc(c3st2[qt % 2], 16)

            s.nop().then_inc(pb3, 1)
            s.wait_ge(pb3, 5)

            s.wait_ge(c3st2[0], 16 * 8)
            s.wait_ge(c3st2[1], 16 * 8)
            for qt in range(NQT):
                if qt >= 2:
                    s.wait_ge(d5u, qt - 1)
                s.dma_start(out=x1rb[:, (qt % 2) * 2048:(qt % 2 + 1) * 2048],
                            in_=x1_scr[qt]).then_inc(d5x2[qt % 2], 16)

            s.nop().then_inc(pb35, 1)
            s.wait_ge(pb35, 5)

            s.wait_ge(d5u, NQT)
            for g in range(4):
                for oc in range(64):
                    gi = g * 64 + oc
                    if gi >= 2:
                        s.wait_ge(e4g, gi - 1)
                        s.wait_ge(e4u, gi - 1)
                    s.wait_ge(acache, 16 * 24)
                    s.dma_start(out=wgb[:, (gi % 2) * 2048:(gi % 2 + 1) * 2048],
                                in_=wgu_bf[oc]).then_inc(e4w2[gi % 2], 16)
                    s.dma_start(out=wub[:, (gi % 2) * 2048:(gi % 2 + 1) * 2048],
                                in_=wgu_bf[64 + oc]).then_inc(e4w2[gi % 2], 16)
                for p in range(2):
                    for qs in range(2):
                        qt = g * 4 + p * 2 + qs
                        if qt == 0:
                            s.wait_ge(c3st2[0], 16 * 8)
                            s.wait_ge(c3st2[1], 16 * 8)
                        if qt >= 2:
                            s.wait_ge(e4e, qt - 1)
                        s.dma_start(out=x1r4[:, (qt % 2) * 2048:(qt % 2 + 1) * 2048],
                                    in_=x1_scr[qt]).then_inc(e4x2[qt % 2], 16)
                        s.wait_ge(e4e, qt + 1)
                        s.dma_start(out=out_t[qt * 128:(qt + 1) * 128, :],
                                    in_=x2f[:, (qt % 2) * 2048:(qt % 2 + 1) * 2048]
                                    ).then_inc(e4o2[qt % 2], 16)
            s.wait_ge(e4o2[0], 16 * 8)
            s.wait_ge(e4o2[1], 16 * 8)

        @block.gpsimd
        def _(g):
            g.dma_start(out=lnbb, in_=lnb_in).then_inc(s_constg, 16)
            g.dma_start(out=cosb[0:64, :], in_=cos_in).then_inc(s_constg, 16)
            g.dma_start(out=sinb[0:64, :], in_=sin_in).then_inc(s_constg, 16)
            g.nop().then_inc(pb1, 1)
            g.wait_ge(pb1, 5)
            g.wait_ge(x1u, NT)
            g.wait_ge(x1c, 2 * NT)

            cache_jobs = []
            for i in range(16):
                cache_jobs.append((wgu_bf[i * 8:(i + 1) * 8], wgu_t[i * 8:(i + 1) * 8]))
            for i in range(8):
                cache_jobs.append((wdn_bf[i * 8:(i + 1) * 8], wdn_t[i * 8:(i + 1) * 8]))
            cj = 0
            for h in range(H):
                if h >= 2:
                    g.wait_ge(apq, (h - 1) * NCH)
                for r in range(3):
                    g.dma_start(out=wbuf[:, ((h % 2) * 3 + r) * 2048:((h % 2) * 3 + r + 1) * 2048],
                                in_=w_in_t[fcol(h, r)]).then_inc(aw2[h % 2], 16)
                for _ in range(2):
                    if cj < len(cache_jobs):
                        dst, src = cache_jobs[cj]
                        g.dma_start(out=dst, in_=src).then_inc(acache, 16)
                        cj += 1
            while cj < len(cache_jobs):
                dst, src = cache_jobs[cj]
                g.dma_start(out=dst, in_=src).then_inc(acache, 16)
                cj += 1

            g.nop().then_inc(pb2a, 1)
            g.wait_ge(pb2a, 5)
            g.nop().then_inc(pb2b, 1)
            g.wait_ge(pb2b, 5)

            g.wait_ge(bpv, 64)
            for dc in range(DC):
                g.dma_start(out=WOB[:, dc * 2048:(dc + 1) * 2048],
                            in_=w_out_in[dc * 128:(dc + 1) * 128, :]).then_inc(c3w, 16)
            g.nop().then_inc(pb3, 1)
            g.wait_ge(pb3, 5)
            g.nop().then_inc(pb35, 1)
            g.wait_ge(pb35, 5)

            g.wait_ge(d5t, NQT * DC)
            for gi in range(4):
                for p in range(2):
                    for oc in range(64):
                        di = (gi * 2 + p) * 64 + oc
                        if di >= 2:
                            g.wait_ge(e4bm, di - 1)
                        g.wait_ge(acache, 16 * 24)
                        g.dma_start(out=wdb[:, (di % 2) * 2048:(di % 2 + 1) * 2048],
                                    in_=wdn_bf[oc]).then_inc(e4d2[di % 2], 16)

        @block.tensor
        def _(t):
            t.wait_ge(s_const, 16 * 6)
            t.wait_ge(s_constg, 16 * 3)
            for i in range(NT):
                t.wait_ge(x1u, i + 1)
                for dc in range(DC):
                    k = i * DC + dc
                    if k >= 2:
                        t.wait_ge(x1v, k - 1)
                    tp = bank_bf(6 + (k % 2), 128)
                    usl = ub[:, (i % 2) * 2048 + dc * 128:(i % 2) * 2048 + (dc + 1) * 128]
                    t.transpose(tp, usl, idb).then_inc(x1t, 1)
            t.nop().then_inc(pb1, 1)
            t.wait_ge(pb1, 5)
            t.wait_ge(x1v, 512)

            for h in range(H):
                t.wait_ge(aw2[h % 2], 16 * 3 * (h // 2 + 1))
                for r in range(3):
                    pb_ = PS[:, 512 * 6:512 * 6 + 1]
                    if h > 0 or r > 0:
                        t.wait_ge(avb, h * 3 + r)
                    if h > 0 and r == 0:
                        t.wait_ge(avc, 32 * h)
                    ws = wbuf[:, ((h % 2) * 3 + r) * 2048:((h % 2) * 3 + r + 1) * 2048]
                    for dc in range(DC):
                        ins = t.matmul(pb_, ws[:, dc * 128:(dc + 1) * 128], lnbb[:, dc:dc + 1],
                                       start=(dc == 0), stop=(dc == DC - 1))
                        if dc == DC - 1:
                            ins.then_inc(apb, 1)

                rope_pending = []
                for ci, (r, c, n) in enumerate(CH):
                    gcid = h * NCH + ci
                    slot = gcid % 4
                    if gcid >= 4:
                        t.wait_ge(aev, gcid - 3)
                    ws = wbuf[:, ((h % 2) * 3 + r) * 2048:((h % 2) * 3 + r + 1) * 2048]
                    dst = bank(slot)
                    toff = c * 512 if r == 0 else c * 512
                    for dc in range(DC):
                        ins = t.matmul(dst, ws[:, dc * 128:(dc + 1) * 128],
                                       XnT[:, dc * 4096 + toff:dc * 4096 + toff + 512],
                                       start=(dc == 0), stop=(dc == DC - 1))
                        if dc == DC - 1:
                            ins.then_inc(apq, 1)
                    if rope_pending:
                        rj, rslot = rope_pending.pop(0)
                        t.wait_ge(aev, rj + 1)
                        if (h * NQK + rslot) >= 2:
                            t.wait_ge(arp, h * NQK + rslot - 1)
                        xr = xr0 if (rslot % 2 == 0) else xr1
                        t.matmul(PS[0:64, 512 * (4 + rslot % 2):512 * (4 + rslot % 2) + 512],
                                 perm_s, xr[0:64, :], start=True, stop=True).then_inc(art, 1)
                    if r in (0, 1):
                        rqk = c if r == 0 else 4 + c
                        rope_pending.append((gcid, rqk))
                while rope_pending:
                    rj, rslot = rope_pending.pop(0)
                    t.wait_ge(aev, rj + 1)
                    if (h * NQK + rslot) >= 2:
                        t.wait_ge(arp, h * NQK + rslot - 1)
                    xr = xr0 if (rslot % 2 == 0) else xr1
                    t.matmul(PS[0:64, 512 * (4 + rslot % 2):512 * (4 + rslot % 2) + 512],
                             perm_s, xr[0:64, :], start=True, stop=True).then_inc(art, 1)

                t.wait_ge(avb, h * 3 + 3)
                for kc in range(32):
                    t.wait_ge(aev, h * NCH + 12 + (kc // 4) + 1)
                    gk = h * 32 + kc
                    if gk >= 2:
                        t.wait_ge(avc, gk - 1)
                    tp = bank_bf(6 + (kc % 2), 128)
                    t.transpose(tp, vfb[:, kc * 128:(kc + 1) * 128], idb).then_inc(avt, 1)

            t.nop().then_inc(pb2a, 1)
            t.wait_ge(pb2a, 5)

            t.wait_ge(aev, H * NCH)
            t.wait_ge(avc, H * 32)
            t.wait_ge(arp, H * NQK)
            gsw_c = 0
            gkc2 = 0
            for h in range(H):
                t.wait_ge(bql2[h % 2], 16 * 3 * (h // 2 + 1))
                sl = h % 2

                def p1_j(jj):
                    nonlocal gsw_c
                    for ql in range(4):
                        qt = jj * 4 + ql
                        for sw in range(4):
                            gsw_c += 1
                            if gsw_c > 2:
                                t.wait_ge(b1x, gsw_c - 2)
                            pbk = 0 if (gsw_c % 2 == 1) else 2
                            for half in range(2):
                                ins = t.matmul(bank(pbk + half),
                                               qh(sl)[:, qt * 128:(qt + 1) * 128],
                                               kh(sl)[:, sw * 1024 + half * 512:sw * 1024 + half * 512 + 512],
                                               start=True, stop=True)
                                if half == 1:
                                    ins.then_inc(b1m, 1)

                p1_j(0)
                p1_j(1)
                for j in range(4):
                    gj = h * 4 + j
                    t.wait_ge(bnm, gj + 1)
                    if gj > 0:
                        t.wait_ge(bcx, gj)
                    t.transpose(PS[0:1, 512 * 6:512 * 6 + 128], col(32), idf
                                ).then_inc(bt1, 1)
                    t.wait_ge(bv1, gj + 1)
                    t.matmul(PS[:, 512 * 6:512 * 6 + 1], ones_r, ncsc,
                             start=True, stop=True).then_inc(bt2, 1)
                    for kc in range(32):
                        gkc2 += 1
                        if gkc2 > 2:
                            t.wait_ge(bpt, gkc2 - 2)
                        t.matmul(bank(4 + (kc % 2)),
                                 kh(sl)[:, kc * 128:(kc + 1) * 128],
                                 qh(sl)[:, j * 512:(j + 1) * 512],
                                 start=True, stop=True).then_inc(b2m, 1)
                    for kc in range(32):
                        t.wait_ge(bpt, gj * 32 + kc + 1)
                        if kc == 0:
                            t.wait_ge(bv2, gj + 1)
                        t.matmul(bank(6), vth(sl)[:, kc * 128:(kc + 1) * 128],
                                 P_T[:, kc * 512:(kc + 1) * 512],
                                 start=(kc == 0), stop=(kc == 31))
                        if kc == 0 and gj > 0:
                            t.wait_ge(bv3, gj)
                        ins = t.matmul(PS[0:1, 512 * 7:512 * 7 + 512], ones_cb,
                                       P_T[:, kc * 512:(kc + 1) * 512],
                                       start=(kc == 0), stop=(kc == 31))
                        if kc == 31:
                            ins.then_inc(bpv, 1)
                    t.wait_ge(blv, gj + 1)
                    t.matmul(bank(7), ones_r, rrow, start=True, stop=True).then_inc(bt3, 1)
                    if j + 2 < 4:
                        p1_j(j + 2)

            t.nop().then_inc(pb2b, 1)
            t.wait_ge(pb2b, 5)

            t.wait_ge(bcx, 64)
            t.wait_ge(bv3, 64)
            t.wait_ge(bpt, 2048)
            t.wait_ge(b1x, 1024)
            t.wait_ge(c3w, 16 * DC)
            for qt in range(NQT):
                pp = qt % 2
                if qt >= 2:
                    t.wait_ge(c3e, qt - 1)
                for oc in range(4):
                    dst = PS[:, pp * 2048 + oc * 512: pp * 2048 + oc * 512 + 512]
                    for dcc in range(DC):
                        ins = t.matmul(dst,
                                       Ctx[:, dcc * 2048 + qt * 128: dcc * 2048 + (qt + 1) * 128],
                                       WOB[:, dcc * 2048 + oc * 512: dcc * 2048 + oc * 512 + 512],
                                       start=(dcc == 0), stop=(dcc == DC - 1))
                        if dcc == DC - 1:
                            ins.then_inc(c3m, 1)
            t.nop().then_inc(pb3, 1)
            t.wait_ge(pb3, 5)

            t.wait_ge(c3e, NQT)
            for qt in range(NQT):
                t.wait_ge(d5u, qt + 1)
                for dc in range(DC):
                    k = qt * DC + dc
                    if k >= 2:
                        t.wait_ge(d5v, k - 1)
                    tp = bank_bf(0 + (k % 2), 128)
                    usl = u2b[:, (qt % 2) * 2048 + dc * 128:(qt % 2) * 2048 + (dc + 1) * 128]
                    t.transpose(tp, usl, idb).then_inc(d5t, 1)
            t.nop().then_inc(pb35, 1)
            t.wait_ge(pb35, 5)

            t.wait_ge(d5v, NQT * DC)
            for g4 in range(4):
                if g4 > 0:
                    t.wait_ge(e4e, g4 * 4)
                for oc in range(64):
                    gi = g4 * 64 + oc
                    t.wait_ge(e4w2[gi % 2], 16 * 2 * (gi // 2 + 1))
                    if gi >= 2:
                        t.wait_ge(e4s, gi - 1)
                        t.wait_ge(e4h, gi - 1)
                    pg = bank(0 + 2 * (gi % 2))
                    pu = bank(1 + 2 * (gi % 2))
                    for hc in range(DC):
                        ins = t.matmul(pg, wgb[:, (gi % 2) * 2048 + hc * 128:(gi % 2) * 2048 + (hc + 1) * 128],
                                       X1nT[:, hc * 2048 + g4 * 512:hc * 2048 + (g4 + 1) * 512],
                                       start=(hc == 0), stop=(hc == DC - 1))
                        if hc == DC - 1:
                            ins.then_inc(e4g, 1)
                    for hc in range(DC):
                        ins = t.matmul(pu, wub[:, (gi % 2) * 2048 + hc * 128:(gi % 2) * 2048 + (hc + 1) * 128],
                                       X1nT[:, hc * 2048 + g4 * 512:hc * 2048 + (g4 + 1) * 512],
                                       start=(hc == 0), stop=(hc == DC - 1))
                        if hc == DC - 1:
                            ins.then_inc(e4u, 1)
                t.wait_ge(e4h, g4 * 64 + 64)
                for p in range(2):
                    if p == 1:
                        t.wait_ge(e4e, g4 * 4 + 2)
                    for oc in range(64):
                        di = (g4 * 2 + p) * 64 + oc
                        t.wait_ge(e4d2[di % 2], 16 * (di // 2 + 1))
                        for qs in range(2):
                            qloc = p * 2 + qs
                            for ho in range(4):
                                ins = t.matmul(PS[:, qs * 2048 + ho * 512: qs * 2048 + ho * 512 + 512],
                                               HT[:, oc * 512 + qloc * 128: oc * 512 + qloc * 128 + 128],
                                               wdb[:, (di % 2) * 2048 + ho * 512:(di % 2) * 2048 + ho * 512 + 512],
                                               start=(oc == 0), stop=(oc == 63))
                                if qs == 1 and ho == 3:
                                    ins.then_inc(e4bm, 1)

        @block.scalar
        def _(sc):
            sc.wait_ge(s_const, 16 * 6)
            sc.wait_ge(s_constg, 16 * 3)
            for i in range(NT):
                xs = xb[:, (i % 3) * 2048:(i % 3 + 1) * 2048]
                sc.wait_ge(xl[i % 3], 16 * (i // 3 + 1))
                sc.wait_ge(x1c, 2 * i + 1)
                sc.activation(col(8), col(7), AF.Ln)
                sc.activation(col(9 + i % 2), col(8), AF.Exp, scale=-0.5).then_inc(x1a, 1)
                sc.wait_ge(x1c, 2 * i + 2)
                if i >= 2:
                    sc.wait_ge(x1t, 16 * (i - 1))
                sc.activation(ub[:, (i % 2) * 2048:(i % 2 + 1) * 2048], xs, AF.Identity,
                              bias=col(11 + i % 2), scale=col(9 + i % 2)).then_inc(x1u, 1)
            sc.nop().then_inc(pb1, 1)
            sc.wait_ge(pb1, 5)

            for h in range(H):
                for ci, (r, c, n) in enumerate(CH):
                    gcid = h * NCH + ci
                    slot = gcid % 4
                    sc.wait_ge(apq, gcid + 1)
                    sc.wait_ge(avb, h * 3 + r + 1)
                    bcol = col(16 + (h % 2) * 3 + r)
                    scl = QSCALE if r == 0 else 1.0
                    if r == 0:
                        if h > 0 and c == 0:
                            sc.wait_ge(aqsq, 16 * h)
                        rqk = c
                        if h * NQK + rqk >= 2:
                            sc.wait_ge(arp, h * NQK + rqk - 1)
                        sc.activation(qbuf[:, c * 512:(c + 1) * 512], bank(slot),
                                      AF.Identity, bias=bcol, scale=scl)
                        sc.activation(xr0[0:64, :] if (c % 2 == 0) else xr1[0:64, :],
                                      bank(slot)[0:64, :], AF.Identity,
                                      bias=bcol[0:64, :], scale=scl).then_inc(aev, 1)
                    elif r == 1:
                        if h > 0 and c == 0:
                            sc.wait_ge(aqsk, 16 * h)
                        rqk = 4 + c
                        if h * NQK + rqk >= 2:
                            sc.wait_ge(arp, h * NQK + rqk - 1)
                        sc.activation(kbuf[:, c * 512:(c + 1) * 512], bank(slot),
                                      AF.Identity, bias=bcol, scale=scl)
                        sc.activation(xr0[0:64, :] if (rqk % 2 == 0) else xr1[0:64, :],
                                      bank(slot)[0:64, :], AF.Identity,
                                      bias=bcol[0:64, :], scale=scl).then_inc(aev, 1)
                    else:
                        if h > 0 and c == 0:
                            sc.wait_ge(avt, 32 * h)
                            sc.wait_ge(aqsv, 16 * 8 * h)
                        sc.activation(vfb[:, c * 512:(c + 1) * 512], bank(slot),
                                      AF.Identity, bias=bcol, scale=1.0).then_inc(aev, 1)
            sc.nop().then_inc(pb2a, 1)
            sc.wait_ge(pb2a, 5)

            sc.wait_ge(aqsq, 16 * 16)
            sc.wait_ge(aqsk, 16 * 16)
            sc.wait_ge(aqsv, 16 * 128)
            for h in range(2):
                sc.dma_start(out=qh(h % 2), in_=q_scr[h]).then_inc(bql2[h % 2], 16)
                sc.dma_start(out=kh(h % 2), in_=k_scr[h]).then_inc(bql2[h % 2], 16)
                sc.dma_start(out=vth(h % 2), in_=vt_scr[h]).then_inc(bql2[h % 2], 16)
            for h in range(H):
                for j in range(4):
                    gj = h * 4 + j
                    sc.wait_ge(bv2, gj + 1)
                    if gj > 0:
                        sc.wait_ge(bpv, gj)
                    for kc in range(32):
                        sc.wait_ge(b2m, gj * 32 + kc + 1)
                        sc.activation(P_T[:, kc * 512:(kc + 1) * 512], bank(4 + (kc % 2)),
                                      AF.Exp, bias=col(33)).then_inc(bpt, 1)
                if h + 2 < H:
                    sc.wait_ge(bcx, 4 * (h + 1))
                    sc.dma_start(out=qh((h + 2) % 2), in_=q_scr[h + 2]).then_inc(bql2[h % 2], 16)
                    sc.dma_start(out=kh((h + 2) % 2), in_=k_scr[h + 2]).then_inc(bql2[h % 2], 16)
                    sc.dma_start(out=vth((h + 2) % 2), in_=vt_scr[h + 2]).then_inc(bql2[h % 2], 16)
            sc.nop().then_inc(pb2b, 1)
            sc.wait_ge(pb2b, 5)

            for qt in range(NQT):
                sc.wait_ge(c3c, qt + 1)
                sc.activation(col(8), col(4), AF.Ln)
                sc.activation(col(9 + qt % 2), col(8), AF.Exp, scale=-0.5).then_inc(c3a, 1)
            sc.nop().then_inc(pb3, 1)
            sc.wait_ge(pb3, 5)

            sc.wait_ge(c3st2[0], 16 * 8)
            sc.wait_ge(c3st2[1], 16 * 8)
            for qt in range(NQT):
                xs = x1rb[:, (qt % 2) * 2048:(qt % 2 + 1) * 2048]
                sc.wait_ge(d5x2[qt % 2], 16 * (qt // 2 + 1))
                sc.wait_ge(d5c, 2 * qt + 1)
                sc.activation(col(8), col(7), AF.Ln)
                sc.activation(col(9 + qt % 2), col(8), AF.Exp, scale=-0.5).then_inc(d5a, 1)
                sc.wait_ge(d5c, 2 * qt + 2)
                if qt >= 2:
                    sc.wait_ge(d5t, 16 * (qt - 1))
                sc.activation(u2b[:, (qt % 2) * 2048:(qt % 2 + 1) * 2048], xs, AF.Identity,
                              bias=col(11 + qt % 2), scale=col(9 + qt % 2)).then_inc(d5u, 1)
            sc.nop().then_inc(pb35, 1)
            sc.wait_ge(pb35, 5)

            for g4 in range(4):
                for oc in range(64):
                    gi = g4 * 64 + oc
                    sc.wait_ge(e4g, gi + 1)
                    if gi >= 2:
                        sc.wait_ge(e4h, gi - 1)
                    sc.activation(sgb[:, (gi % 2) * 512:(gi % 2 + 1) * 512],
                                  bank(0 + 2 * (gi % 2)), AF.Silu).then_inc(e4s, 1)

        fsem = sem("fence_v")
        _fc = [0]

        @block.vector
        def _(v):
            def F(ins):
                _fc[0] += 1
                ins.then_inc(fsem, 1)
                v.wait_ge(fsem, _fc[0])
            v.wait_ge(s_const, 16 * 6)
            v.wait_ge(s_constg, 16 * 3)
            for i in range(NT):
                xs = xb[:, (i % 3) * 2048:(i % 3 + 1) * 2048]
                v.wait_ge(xl[i % 3], 16 * (i // 3 + 1))
                v.reduce_sum(col(0), xs, axis=AX)
                F(v.scalar_tensor_tensor(jnk1, xs, 1.0, xs, ALU.bypass, ALU.mult,
                                         accum_out=col(1)))
                F(v.tensor_scalar_mul(col(2), col(0), 1.0 / D))
                F(v.tensor_tensor(col(46), col(2), col(2), ALU.mult))
                F(v.tensor_scalar(col(3), col(1), 1.0 / D, col(46), ALU.mult, ALU.subtract))
                F(v.tensor_scalar_add(col(4), col(3), EPS))
                F(v.reciprocal(col(5), col(4)))
                F(v.tensor_scalar(col(6), col(3), col(5), EPS, ALU.mult, ALU.add))
                v.tensor_tensor(col(7), col(4), col(6), ALU.mult).then_inc(x1c, 1)
                v.wait_ge(x1a, i + 1)
                v.tensor_scalar(col(11 + i % 2), col(2), col(9 + i % 2), -1.0,
                                ALU.mult, ALU.mult).then_inc(x1c, 1)
                for dc in range(DC):
                    k = i * DC + dc
                    v.wait_ge(x1t, k + 1)
                    tp = bank_bf(6 + (k % 2), 128)
                    v.tensor_scalar(XnT[:, dc * 4096 + i * 128: dc * 4096 + (i + 1) * 128],
                                    tp, lnw_s[:, dc:dc + 1], None, ALU.mult).then_inc(x1v, 1)
            v.nop().then_inc(pb1, 1)
            v.wait_ge(pb1, 5)

            for h in range(H):
                for r in range(3):
                    v.wait_ge(apb, h * 3 + r + 1)
                    scl = QSCALE if r == 0 else 1.0
                    v.tensor_scalar(col(16 + (h % 2) * 3 + r), PS[:, 512 * 6:512 * 6 + 1],
                                    scl, None, ALU.mult).then_inc(avb, 1)
                for rqk in range(NQK):
                    grp = h * NQK + rqk
                    v.wait_ge(art, grp + 1)
                    xr = xr0 if (rqk % 2 == 0) else xr1
                    rot = PS[0:64, 512 * (4 + rqk % 2):512 * (4 + rqk % 2) + 512]
                    if rqk < 4:
                        pos = rqk * 512
                        tgt = qbuf[0:64, rqk * 512:(rqk + 1) * 512]
                    else:
                        pos = (rqk - 4) * 512
                        tgt = kbuf[0:64, (rqk - 4) * 512:(rqk - 4 + 1) * 512]
                    v.tensor_tensor(t1b[0:64, :], xr[0:64, :], cosb[0:64, pos:pos + 512], ALU.mult)
                    v.tensor_tensor(t2b[0:64, :], rot, sinb[0:64, pos:pos + 512], ALU.mult)
                    v.tensor_add(tgt, t1b[0:64, :], t2b[0:64, :]).then_inc(arp, 1)
                for kc in range(32):
                    gk = h * 32 + kc
                    v.wait_ge(avt, gk + 1)
                    tp = bank_bf(6 + (kc % 2), 128)
                    v.tensor_copy(vfb[:, kc * 128:(kc + 1) * 128], tp).then_inc(avc, 1)
            v.nop().then_inc(pb2a, 1)
            v.wait_ge(pb2a, 5)

            for h in range(H):
                def dve_maxes(jj):
                    base = 28 if (jj % 2 == 0) else 34
                    gj_ = h * 4 + jj
                    for ql in range(4):
                        for sw in range(4):
                            gsw = gj_ * 16 + ql * 4 + sw
                            v.wait_ge(b1m, gsw + 1)
                            pbk = 0 if (gsw % 2 == 0) else 2
                            ins = v.reduce_max(col(24 + sw), PS[:, pbk * 512:pbk * 512 + 1024],
                                               axis=AX, negate=True)
                            ins.then_inc(b1x, 1)
                            if sw == 3:
                                _fc[0] += 1
                                v.wait_ge(b1x, gsw + 1)
                                _fc[0] -= 1
                        v.tensor_reduce(col(base + ql), COLS[:, 24:28], AX, ALU.min
                                        ).then_inc(b1q, 1)

                dve_maxes(0)
                dve_maxes(1)
                for j in range(4):
                    gj = h * 4 + j
                    base = 28 if (j % 2 == 0) else 34
                    v.wait_ge(b1q, gj * 4 + 4)
                    v.tensor_reduce(col(32), COLS[:, base:base + 4], AX, ALU.min
                                    ).then_inc(bnm, 1)
                    v.wait_ge(bt1, gj + 1)
                    F(v.tensor_copy(nmrow, PS[0:1, 512 * 6:512 * 6 + 128]))
                    v.tensor_reduce(ncsc, nmrow, AX, ALU.min).then_inc(bv1, 1)
                    v.wait_ge(bt2, gj + 1)
                    v.tensor_copy(col(33), PS[:, 512 * 6:512 * 6 + 1]).then_inc(bv2, 1)
                    v.wait_ge(bpv, gj + 1)
                    F(v.tensor_copy(lrow, PS[0:1, 512 * 7:512 * 7 + 512]))
                    v.reciprocal(rrow, lrow).then_inc(blv, 1)
                    v.wait_ge(bt3, gj + 1)
                    v.tensor_copy(r_bc, bank(7)).then_inc(bv3, 1)
                    v.tensor_tensor(Ctx[:, h * 2048 + j * 512: h * 2048 + (j + 1) * 512],
                                    bank(6), r_bc, ALU.mult).then_inc(bcx, 1)
                    if j + 2 < 4:
                        dve_maxes(j + 2)
            v.nop().then_inc(pb2b, 1)
            v.wait_ge(pb2b, 5)

            for qt in range(NQT):
                xs = x0b[:, (qt % 2) * 2048:(qt % 2 + 1) * 2048]
                v.wait_ge(c3x2[qt % 2], 16 * (qt // 2 + 1))
                v.reduce_sum(col(0), xs, axis=AX)
                F(v.scalar_tensor_tensor(jnk3, xs, 1.0, xs, ALU.bypass, ALU.mult,
                                         accum_out=col(1)))
                F(v.tensor_scalar_mul(col(2), col(0), 1.0 / D))
                F(v.tensor_tensor(col(46), col(2), col(2), ALU.mult))
                F(v.tensor_scalar(col(3), col(1), 1.0 / D, col(46), ALU.mult, ALU.subtract))
                v.tensor_scalar_add(col(4), col(3), EPS).then_inc(c3c, 1)
                v.wait_ge(c3a, qt + 1)
                al = col(9 + qt % 2)
                F(v.tensor_scalar(col(11 + qt % 2), col(2), al, -1.0, ALU.mult, ALU.mult))
                F(v.tensor_scalar_add(col(13), al, 1.0))
                v.wait_ge(c3m, qt * 4 + 4)
                if qt >= 2:
                    v.wait_ge(c3st2[qt % 2], 16 * (qt // 2))
                pp = qt % 2
                xf = x1f[:, pp * 2048:(pp + 1) * 2048]
                v.scalar_tensor_tensor(xf, xs, col(13), PS[:, pp * 2048:(pp + 1) * 2048],
                                       ALU.mult, ALU.add)
                v.tensor_scalar_add(xf, xf, col(11 + qt % 2)).then_inc(c3e, 1)
            v.nop().then_inc(pb3, 1)
            v.wait_ge(pb3, 5)

            for qt in range(NQT):
                xs = x1rb[:, (qt % 2) * 2048:(qt % 2 + 1) * 2048]
                v.wait_ge(d5x2[qt % 2], 16 * (qt // 2 + 1))
                v.reduce_sum(col(0), xs, axis=AX)
                F(v.scalar_tensor_tensor(jnk35, xs, 1.0, xs, ALU.bypass, ALU.mult,
                                         accum_out=col(1)))
                F(v.tensor_scalar_mul(col(2), col(0), 1.0 / D))
                F(v.tensor_tensor(col(46), col(2), col(2), ALU.mult))
                F(v.tensor_scalar(col(3), col(1), 1.0 / D, col(46), ALU.mult, ALU.subtract))
                v.tensor_scalar_add(col(7), col(3), EPS).then_inc(d5c, 1)
                v.wait_ge(d5a, qt + 1)
                v.tensor_scalar(col(11 + qt % 2), col(2), col(9 + qt % 2), -1.0,
                                ALU.mult, ALU.mult).then_inc(d5c, 1)
                for dc in range(DC):
                    k = qt * DC + dc
                    v.wait_ge(d5t, k + 1)
                    tp = bank_bf(0 + (k % 2), 128)
                    v.tensor_copy(X1nT[:, dc * 2048 + qt * 128: dc * 2048 + (qt + 1) * 128],
                                  tp).then_inc(d5v, 1)
            v.nop().then_inc(pb35, 1)
            v.wait_ge(pb35, 5)

            for g4 in range(4):
                for oc in range(64):
                    gi = g4 * 64 + oc
                    v.wait_ge(e4u, gi + 1)
                    v.wait_ge(e4s, gi + 1)
                    v.tensor_tensor(HT[:, oc * 512:(oc + 1) * 512],
                                    bank(1 + 2 * (gi % 2)),
                                    sgb[:, (gi % 2) * 512:(gi % 2 + 1) * 512],
                                    ALU.mult).then_inc(e4h, 1)
                for p in range(2):
                    for qs in range(2):
                        qt = g4 * 4 + p * 2 + qs
                        v.wait_ge(e4bm, (g4 * 2 + p) * 64 + 64)
                        v.wait_ge(e4x2[qt % 2], 16 * (qt // 2 + 1))
                        if qt >= 2:
                            v.wait_ge(e4o2[qt % 2], 16 * (qt // 2))
                        v.scalar_tensor_tensor(
                            x2f[:, (qt % 2) * 2048:(qt % 2 + 1) * 2048],
                            PS[:, qs * 2048:(qs + 1) * 2048], 1.0,
                            x1r4[:, (qt % 2) * 2048:(qt % 2 + 1) * 2048],
                            ALU.mult, ALU.add).then_inc(e4e, 1)

    return nc


_NC_CACHE = None


def _get_nc():
    global _NC_CACHE
    if _NC_CACHE is None:
        _NC_CACHE = build_program()
    return _NC_CACHE


def _host_tables():
    import ml_dtypes
    rot = 64
    inv_freq = (1.0 / 10000.0 ** (np.arange(0, rot, 2, dtype=np.float32) / rot)).astype(np.float32)
    ang = np.arange(S, dtype=np.float32)[:, None] * inv_freq[None, :]
    ang = np.repeat(ang, 2, axis=-1)
    cos_g = np.cos(ang).T.astype(np.float32)
    sin_g = np.sin(ang).T.astype(np.float32)
    perm = np.zeros((64, 64), dtype=np.float32)
    for i in range(32):
        perm[2 * i + 1, 2 * i] = -1.0
        perm[2 * i, 2 * i + 1] = 1.0
    idb = np.eye(128, dtype=ml_dtypes.bfloat16)
    idf = np.eye(128, dtype=np.float32)
    ones_r = np.ones((1, 128), dtype=np.float32)
    ones_cb = np.ones((128, 1), dtype=ml_dtypes.bfloat16)
    return cos_g, sin_g, perm, idb, idf, ones_r, ones_cb


def kernel(L0_attn_X, L0_attn_ln_w, L0_attn_ln_b, L0_attn_W_in, L0_attn_W_out,
           L0_ffn_w_gate_up, L0_ffn_w_down):
    X = np.ascontiguousarray(np.asarray(L0_attn_X, dtype=np.float32))
    ln_w = np.asarray(L0_attn_ln_w, dtype=np.float32)
    ln_b = np.asarray(L0_attn_ln_b, dtype=np.float32)
    W_in = np.asarray(L0_attn_W_in, dtype=np.float32)
    W_out = np.ascontiguousarray(np.asarray(L0_attn_W_out, dtype=np.float32))
    w_gu = np.asarray(L0_ffn_w_gate_up, dtype=np.float32)
    w_dn = np.asarray(L0_ffn_w_down, dtype=np.float32)

    w_in_t = np.ascontiguousarray(
        W_in.reshape(DC, 128, 48, 128).transpose(2, 1, 0, 3).reshape(48, 128, DC * 128))
    wgu_t = np.ascontiguousarray(
        w_gu.T.reshape(DC, 128, 128, 128).transpose(2, 1, 0, 3).reshape(128, 128, DC * 128))
    wdn_t = np.ascontiguousarray(w_dn.T.reshape(64, 128, D))
    lnw_c = np.ascontiguousarray(ln_w.reshape(DC, 128).T)
    lnb_c = np.ascontiguousarray(ln_b.reshape(DC, 128).T)

    cos_g, sin_g, perm, idb, idf, ones_r, ones_cb = _host_tables()

    nc = _get_nc()
    in_maps = []
    for c in range(8):
        b, half = c // 2, c % 2
        off = half * SQ
        in_maps.append({
            "x": np.roll(X[b], -off, axis=0) if off else X[b],
            "w_in_t": w_in_t,
            "w_out": W_out,
            "wgu_t": wgu_t,
            "wdn_t": wdn_t,
            "lnw_c": lnw_c,
            "lnb_c": lnb_c,
            "cos_t": np.roll(cos_g, -off, axis=1) if off else cos_g,
            "sin_t": np.roll(sin_g, -off, axis=1) if off else sin_g,
            "perm": perm,
            "id_bf": idb,
            "id_f32": idf,
            "ones_r": ones_r,
            "ones_cb": ones_cb,
        })
    res = run_bass_kernel_spmd(nc, in_maps, core_ids=list(range(8)))
    out = np.empty((4, S, D), dtype=np.float32)
    for c in range(8):
        b, half = c // 2, c % 2
        out[b, half * SQ:(half + 1) * SQ, :] = np.asarray(res.results[c]["out"],
                                                          dtype=np.float32)
    return out


# version 02
# speedup vs baseline: 1.0188x; 1.0188x over previous
import numpy as np
from contextlib import ExitStack

import concourse.bass as bass
import concourse.mybir as mybir
from concourse.bass_utils import run_bass_kernel_spmd

F32 = mybir.dt.float32
BF16 = mybir.dt.bfloat16
AF = mybir.ActivationFunctionType
ALU = mybir.AluOpType
AX = mybir.AxisListType.X

D = 2048
S = 4096
SQ = 2048
H = 16
DH = 128
DC = 16
INTER = 8192
NT = S // 128
NQT = SQ // 128
QSCALE = 1.0 / float(np.sqrt(128.0))
EPS = 1e-5

_DEBUG_TAPS = []


def build_program(debug=False):
    nc = bass.Bass("TRN2", target_bir_lowering=False)
    SCR = "ExternalOutput" if debug else "Internal"

    x_in = nc.dram_tensor("x", [S, D], F32, kind="ExternalInput").ap()
    w_in_t = nc.dram_tensor("w_in_t", [48, 128, DC * 128], F32, kind="ExternalInput").ap()
    w_out_in = nc.dram_tensor("w_out", [D, D], F32, kind="ExternalInput").ap()
    wgu_t = nc.dram_tensor("wgu_t", [128, 128, DC * 128], F32, kind="ExternalInput").ap()
    wdn_t = nc.dram_tensor("wdn_t", [64, 128, D], F32, kind="ExternalInput").ap()
    lnw_in = nc.dram_tensor("lnw_c", [128, DC], F32, kind="ExternalInput").ap()
    lnb_in = nc.dram_tensor("lnb_c", [128, DC], F32, kind="ExternalInput").ap()
    cos_in = nc.dram_tensor("cos_t", [64, S], F32, kind="ExternalInput").ap()
    sin_in = nc.dram_tensor("sin_t", [64, S], F32, kind="ExternalInput").ap()
    perm_in = nc.dram_tensor("perm", [64, 64], F32, kind="ExternalInput").ap()
    idb_in = nc.dram_tensor("id_bf", [128, 128], BF16, kind="ExternalInput").ap()
    idf_in = nc.dram_tensor("id_f32", [128, 128], F32, kind="ExternalInput").ap()
    onesr_in = nc.dram_tensor("ones_r", [1, 128], F32, kind="ExternalInput").ap()
    onesc_in = nc.dram_tensor("ones_cb", [128, 1], BF16, kind="ExternalInput").ap()

    out_t = nc.dram_tensor("out", [SQ, D], F32, kind="ExternalOutput").ap()

    q_scr = nc.dram_tensor("q_scr", [H, 128, SQ], BF16, kind=SCR).ap()
    k_scr = nc.dram_tensor("k_scr", [H, 128, S], BF16, kind=SCR).ap()
    vt_scr = nc.dram_tensor("vt_scr", [H, 128, S], BF16, kind=SCR).ap()
    x1_scr = nc.dram_tensor("x1_scr", [NQT, 128, D], F32, kind=SCR).ap()
    wgu_bf = nc.dram_tensor("wgu_bf", [128, 128, DC * 128], BF16, kind="Internal").ap()
    wdn_bf = nc.dram_tensor("wdn_bf", [64, 128, D], BF16, kind="Internal").ap()

    with ExitStack() as ctx:
        E = ctx.enter_context

        A = E(nc.sbuf_tensor("arA", [128, 65536], BF16)).ap()
        B = E(nc.sbuf_tensor("arB", [128, 28672], BF16)).ap()
        C = E(nc.sbuf_tensor("arC", [128, 4096], F32)).ap()
        COLS = E(nc.sbuf_tensor("cols", [128, 64], F32)).ap()
        perm_s = E(nc.sbuf_tensor("perm_s", [64, 64], F32)).ap()
        idb = E(nc.sbuf_tensor("idb", [128, 128], BF16)).ap()
        idf = E(nc.sbuf_tensor("idf", [128, 128], F32)).ap()
        ones_r = E(nc.sbuf_tensor("ones_r_s", [1, 128], F32)).ap()
        ones_cb = E(nc.sbuf_tensor("ones_cb_s", [128, 1], BF16)).ap()
        lnw_s = E(nc.sbuf_tensor("lnw_s", [128, DC], F32)).ap()
        lnbb = E(nc.sbuf_tensor("lnbb", [128, DC], BF16)).ap()

        PS = E(nc.psum_tensor("ps", [128, 4096], F32)).ap()

        def bank(i, n=512):
            return PS[:, 512 * i: 512 * i + n]

        def bank_bf(i, cols=128, off=0):
            return PS[:, 512 * i + off // 2: 512 * i + (off + cols) // 2].bitcast(BF16)

        XnT = A
        Ctx = A[:, 0:32768]
        P_T = A[:, 32768:49152]
        WOB = A[:, 32768:65536]
        X1nT = A[:, 0:32768]
        HT = A[:, 32768:65536]

        xb = B[:, 0:12288].bitcast(F32)
        ub = B[:, 12288:16384]
        jnk1 = B[:, 16384:20480].bitcast(F32)

        wbuf = B[:, 0:12288]
        qbuf = B[:, 12288:14336]
        kbuf = B[:, 14336:18432]
        vfb = B[:, 18432:22528]
        xr0 = B[:, 22528:23552].bitcast(F32)
        xr1 = B[:, 23552:24576].bitcast(F32)
        t1b = B[:, 24576:25600].bitcast(F32)
        t2b = B[:, 25600:26624].bitcast(F32)

        def qh(s):
            return B[:, s * 10240: s * 10240 + 2048]

        def kh(s):
            return B[:, s * 10240 + 2048: s * 10240 + 6144]

        def vth(s):
            return B[:, s * 10240 + 6144: s * 10240 + 10240]

        x0b = B[:, 0:8192].bitcast(F32)
        x1f = B[:, 8192:16384].bitcast(F32)
        jnk3 = B[:, 16384:20480].bitcast(F32)

        x1rb = B[:, 0:8192].bitcast(F32)
        u2b = B[:, 8192:12288]
        jnk35 = B[:, 12288:16384].bitcast(F32)

        wgb = B[:, 0:4096]
        wub = B[:, 4096:8192]
        wdb = B[:, 8192:12288]
        x1r4 = B[:, 12288:20480].bitcast(F32)
        x2f = B[:, 20480:28672].bitcast(F32)

        cosb = C[:, 0:2048].bitcast(BF16)
        sinb = C[:, 2048:4096].bitcast(BF16)
        r_bc = C[:, 0:512]
        lrow = C[0:1, 512:1024]
        rrow = C[0:1, 1024:1536]
        nmrow = C[0:1, 1536:1664]
        ncsc = C[0:1, 1664:1665]
        sgb = C[:, 0:1024]

        def col(i, n=1):
            return COLS[:, i:i + n]

        def sem(name):
            return E(nc.semaphore(name))

        pb1, pb2a, pb2b, pb3, pb35 = sem("pb1"), sem("pb2a"), sem("pb2b"), sem("pb3"), sem("pb35")
        x1q, x1c, x1a, x1u, x1t, x1v = (sem(n) for n in
            ("x1q", "x1c", "x1a", "x1u", "x1t", "x1v"))
        xl = [sem(f"xl{i}") for i in range(3)]
        ahd, apb, avb, apq, aev, art, arp, avt, avc, acache = (sem(n) for n in
            ("ahd", "apb", "avb", "apq", "aev", "art", "arp", "avt", "avc", "acache"))
        aw2 = [sem(f"aw{i}") for i in range(2)]
        aqsq, aqsk, aqsv = sem("aqsq"), sem("aqsk"), sem("aqsv")
        b1m, b1x, b1q, bnm, bt1, bv1, bt2, bv2, b2m, bpt, bpv, blv, bt3, bv3, bcx, bhd = (
            sem(n) for n in ("b1m", "b1x", "b1q", "bnm", "bt1", "bv1", "bt2",
                             "bv2", "b2m", "bpt", "bpv", "blv", "bt3", "bv3", "bcx", "bhd"))
        bql2 = [sem(f"bql{i}") for i in range(2)]
        c3w, c3q, c3c, c3a, c3g, c3m, c3e = (sem(n) for n in
            ("c3w", "c3q", "c3c", "c3a", "c3g", "c3m", "c3e"))
        c3x2 = [sem(f"c3x{i}") for i in range(2)]
        c3st2 = [sem(f"c3st{i}") for i in range(2)]
        d5q, d5c, d5a, d5u, d5t, d5v = (sem(n) for n in
            ("d5q", "d5c", "d5a", "d5u", "d5t", "d5v"))
        d5x2 = [sem(f"d5x{i}") for i in range(2)]
        e4g, e4u, e4s, e4h, e4bm, e4e = (sem(n) for n in
            ("e4g", "e4u", "e4s", "e4h", "e4bm", "e4e"))
        e4w2 = [sem(f"e4w{i}") for i in range(2)]
        e4d2 = [sem(f"e4d{i}") for i in range(2)]
        e4x2 = [sem(f"e4x{i}") for i in range(2)]
        e4o2 = [sem(f"e4o{i}") for i in range(2)]
        s_const = sem("s_const")
        s_constg = sem("s_constg")

        block = E(nc.Block())

        CH = []
        for r, n in ((0, 4), (1, 8), (2, 8)):
            for c in range(n):
                CH.append((r, c, n))
        NCH = len(CH)
        NQK = 12

        def fcol(h, r):
            return r * 16 + h

        @block.sync
        def _(s):
            s.dma_start(out=perm_s, in_=perm_in).then_inc(s_const, 16)
            s.dma_start(out=idb, in_=idb_in).then_inc(s_const, 16)
            s.dma_start(out=idf, in_=idf_in).then_inc(s_const, 16)
            s.dma_start(out=ones_r, in_=onesr_in).then_inc(s_const, 16)
            s.dma_start(out=ones_cb, in_=onesc_in).then_inc(s_const, 16)
            s.dma_start(out=lnw_s, in_=lnw_in).then_inc(s_const, 16)

            for i in range(NT):
                if i >= 3:
                    s.wait_ge(x1u, i - 2)
                s.dma_start(out=xb[:, (i % 3) * 2048:(i % 3 + 1) * 2048],
                            in_=x_in[i * 128:(i + 1) * 128, :]).then_inc(xl[i % 3], 16)

            s.nop().then_inc(pb1, 1)
            s.wait_ge(pb1, 5)
            for h in range(H):
                s.wait_ge(arp, h * NQK + 4)
                s.wait_ge(aev, h * NCH + 4)
                s.dma_start(out=q_scr[h], in_=qbuf).then_inc(aqsq, 16)
                s.wait_ge(arp, h * NQK + 12)
                s.wait_ge(aev, h * NCH + 12)
                s.dma_start(out=k_scr[h], in_=kbuf).then_inc(aqsk, 16)
                for g in range(8):
                    s.wait_ge(avc, h * 32 + 4 * (g + 1))
                    s.dma_start(out=vt_scr[h][:, g * 512:(g + 1) * 512],
                                in_=vfb[:, g * 512:(g + 1) * 512]).then_inc(aqsv, 16)

            s.nop().then_inc(pb2a, 1)
            s.wait_ge(pb2a, 5)
            s.nop().then_inc(pb2b, 1)
            s.wait_ge(pb2b, 5)

            s.wait_ge(bpv, 64)
            for qt in range(NQT):
                if qt >= 2:
                    s.wait_ge(c3e, qt - 1)
                s.dma_start(out=x0b[:, (qt % 2) * 2048:(qt % 2 + 1) * 2048],
                            in_=x_in[qt * 128:(qt + 1) * 128, :]).then_inc(c3x2[qt % 2], 16)
                s.wait_ge(c3e, qt + 1)
                s.dma_start(out=x1_scr[qt],
                            in_=x1f[:, (qt % 2) * 2048:(qt % 2 + 1) * 2048]).then_inc(c3st2[qt % 2], 16)

            s.nop().then_inc(pb3, 1)
            s.wait_ge(pb3, 5)

            s.wait_ge(c3st2[0], 16 * 8)
            s.wait_ge(c3st2[1], 16 * 8)
            for qt in range(NQT):
                if qt >= 2:
                    s.wait_ge(d5u, qt - 1)
                s.dma_start(out=x1rb[:, (qt % 2) * 2048:(qt % 2 + 1) * 2048],
                            in_=x1_scr[qt]).then_inc(d5x2[qt % 2], 16)

            s.nop().then_inc(pb35, 1)
            s.wait_ge(pb35, 5)

            s.wait_ge(d5u, NQT)
            for g in range(4):
                for oc in range(64):
                    gi = g * 64 + oc
                    if gi >= 2:
                        s.wait_ge(e4g, gi - 1)
                        s.wait_ge(e4u, gi - 1)
                    s.wait_ge(acache, 16 * 24)
                    s.dma_start(out=wgb[:, (gi % 2) * 2048:(gi % 2 + 1) * 2048],
                                in_=wgu_bf[oc]).then_inc(e4w2[gi % 2], 16)
                    s.dma_start(out=wub[:, (gi % 2) * 2048:(gi % 2 + 1) * 2048],
                                in_=wgu_bf[64 + oc]).then_inc(e4w2[gi % 2], 16)
                for p in range(2):
                    for qs in range(2):
                        qt = g * 4 + p * 2 + qs
                        if qt == 0:
                            s.wait_ge(c3st2[0], 16 * 8)
                            s.wait_ge(c3st2[1], 16 * 8)
                        if qt >= 2:
                            s.wait_ge(e4e, qt - 1)
                        s.dma_start(out=x1r4[:, (qt % 2) * 2048:(qt % 2 + 1) * 2048],
                                    in_=x1_scr[qt]).then_inc(e4x2[qt % 2], 16)
                        s.wait_ge(e4e, qt + 1)
                        s.dma_start(out=out_t[qt * 128:(qt + 1) * 128, :],
                                    in_=x2f[:, (qt % 2) * 2048:(qt % 2 + 1) * 2048]
                                    ).then_inc(e4o2[qt % 2], 16)
            s.wait_ge(e4o2[0], 16 * 8)
            s.wait_ge(e4o2[1], 16 * 8)

        @block.gpsimd
        def _(g):
            g.dma_start(out=lnbb, in_=lnb_in).then_inc(s_constg, 16)
            g.dma_start(out=cosb[0:64, :], in_=cos_in).then_inc(s_constg, 16)
            g.dma_start(out=sinb[0:64, :], in_=sin_in).then_inc(s_constg, 16)
            g.nop().then_inc(pb1, 1)
            g.wait_ge(pb1, 5)
            g.wait_ge(x1u, NT)
            g.wait_ge(x1c, 2 * NT)

            cache_jobs = []
            for i in range(16):
                cache_jobs.append((wgu_bf[i * 8:(i + 1) * 8], wgu_t[i * 8:(i + 1) * 8]))
            for i in range(8):
                cache_jobs.append((wdn_bf[i * 8:(i + 1) * 8], wdn_t[i * 8:(i + 1) * 8]))
            cj = 0
            for h in range(H):
                if h >= 2:
                    g.wait_ge(apq, (h - 1) * NCH)
                for r in range(3):
                    g.dma_start(out=wbuf[:, ((h % 2) * 3 + r) * 2048:((h % 2) * 3 + r + 1) * 2048],
                                in_=w_in_t[fcol(h, r)]).then_inc(aw2[h % 2], 16)
                for _ in range(2):
                    if cj < len(cache_jobs):
                        dst, src = cache_jobs[cj]
                        g.dma_start(out=dst, in_=src).then_inc(acache, 16)
                        cj += 1
            while cj < len(cache_jobs):
                dst, src = cache_jobs[cj]
                g.dma_start(out=dst, in_=src).then_inc(acache, 16)
                cj += 1

            g.nop().then_inc(pb2a, 1)
            g.wait_ge(pb2a, 5)
            g.nop().then_inc(pb2b, 1)
            g.wait_ge(pb2b, 5)

            g.wait_ge(bpv, 64)
            for dc in range(DC):
                g.dma_start(out=WOB[:, dc * 2048:(dc + 1) * 2048],
                            in_=w_out_in[dc * 128:(dc + 1) * 128, :]).then_inc(c3w, 16)
            g.nop().then_inc(pb3, 1)
            g.wait_ge(pb3, 5)
            g.nop().then_inc(pb35, 1)
            g.wait_ge(pb35, 5)

            g.wait_ge(d5t, NQT * DC)
            for gi in range(4):
                for p in range(2):
                    for oc in range(64):
                        di = (gi * 2 + p) * 64 + oc
                        if di >= 2:
                            g.wait_ge(e4bm, di - 1)
                        g.wait_ge(acache, 16 * 24)
                        g.dma_start(out=wdb[:, (di % 2) * 2048:(di % 2 + 1) * 2048],
                                    in_=wdn_bf[oc]).then_inc(e4d2[di % 2], 16)

        @block.tensor
        def _(t):
            t.wait_ge(s_const, 16 * 6)
            t.wait_ge(s_constg, 16 * 3)
            for i in range(NT):
                t.wait_ge(x1u, i + 1)
                for dc in range(DC):
                    k = i * DC + dc
                    if k >= 2:
                        t.wait_ge(x1v, k - 1)
                    tp = bank_bf(6 + (k % 2), 128)
                    usl = ub[:, (i % 2) * 2048 + dc * 128:(i % 2) * 2048 + (dc + 1) * 128]
                    t.transpose(tp, usl, idb).then_inc(x1t, 1)
            t.nop().then_inc(pb1, 1)
            t.wait_ge(pb1, 5)
            t.wait_ge(x1v, 512)

            for h in range(H):
                t.wait_ge(aw2[h % 2], 16 * 3 * (h // 2 + 1))
                for r in range(3):
                    pb_ = PS[:, 512 * 6:512 * 6 + 1]
                    if h > 0 or r > 0:
                        t.wait_ge(avb, h * 3 + r)
                    if h > 0 and r == 0:
                        t.wait_ge(avc, 32 * h)
                    ws = wbuf[:, ((h % 2) * 3 + r) * 2048:((h % 2) * 3 + r + 1) * 2048]
                    for dc in range(DC):
                        ins = t.matmul(pb_, ws[:, dc * 128:(dc + 1) * 128], lnbb[:, dc:dc + 1],
                                       start=(dc == 0), stop=(dc == DC - 1))
                        if dc == DC - 1:
                            ins.then_inc(apb, 1)

                rope_pending = []
                for ci, (r, c, n) in enumerate(CH):
                    gcid = h * NCH + ci
                    slot = gcid % 4
                    if gcid >= 4:
                        t.wait_ge(aev, gcid - 3)
                    ws = wbuf[:, ((h % 2) * 3 + r) * 2048:((h % 2) * 3 + r + 1) * 2048]
                    dst = bank(slot)
                    toff = c * 512 if r == 0 else c * 512
                    for dc in range(DC):
                        ins = t.matmul(dst, ws[:, dc * 128:(dc + 1) * 128],
                                       XnT[:, dc * 4096 + toff:dc * 4096 + toff + 512],
                                       start=(dc == 0), stop=(dc == DC - 1))
                        if dc == DC - 1:
                            ins.then_inc(apq, 1)
                    if rope_pending:
                        rj, rslot = rope_pending.pop(0)
                        t.wait_ge(aev, rj + 1)
                        if (h * NQK + rslot) >= 2:
                            t.wait_ge(arp, h * NQK + rslot - 1)
                        xr = xr0 if (rslot % 2 == 0) else xr1
                        t.matmul(PS[0:64, 512 * (4 + rslot % 2):512 * (4 + rslot % 2) + 512],
                                 perm_s, xr[0:64, :], start=True, stop=True).then_inc(art, 1)
                    if r in (0, 1):
                        rqk = c if r == 0 else 4 + c
                        rope_pending.append((gcid, rqk))
                while rope_pending:
                    rj, rslot = rope_pending.pop(0)
                    t.wait_ge(aev, rj + 1)
                    if (h * NQK + rslot) >= 2:
                        t.wait_ge(arp, h * NQK + rslot - 1)
                    xr = xr0 if (rslot % 2 == 0) else xr1
                    t.matmul(PS[0:64, 512 * (4 + rslot % 2):512 * (4 + rslot % 2) + 512],
                             perm_s, xr[0:64, :], start=True, stop=True).then_inc(art, 1)

                t.wait_ge(avb, h * 3 + 3)
                for kc in range(32):
                    t.wait_ge(aev, h * NCH + 12 + (kc // 4) + 1)
                    gk = h * 32 + kc
                    if gk >= 2:
                        t.wait_ge(avc, gk - 1)
                    tp = bank_bf(6 + (kc % 2), 128)
                    t.transpose(tp, vfb[:, kc * 128:(kc + 1) * 128], idb).then_inc(avt, 1)

            t.nop().then_inc(pb2a, 1)
            t.wait_ge(pb2a, 5)

            t.wait_ge(aev, H * NCH)
            t.wait_ge(avc, H * 32)
            t.wait_ge(arp, H * NQK)
            sweep_emit = [0]

            def emit_sweep():
                gsw = sweep_emit[0]
                tgj = gsw // 16
                hh, jj = tgj // 4, tgj % 4
                ssl = hh % 2
                qt = jj * 4 + (gsw % 16) // 4
                sw = gsw % 4
                if gsw % 64 == 0:
                    t.wait_ge(bql2[hh % 2], 16 * 3 * (hh // 2 + 1))
                if gsw >= 2:
                    t.wait_ge(b1x, gsw - 1)
                pbk = 0 if (gsw % 2 == 0) else 2
                for half in range(2):
                    ins = t.matmul(bank(pbk + half),
                                   qh(ssl)[:, qt * 128:(qt + 1) * 128],
                                   kh(ssl)[:, sw * 1024 + half * 512:sw * 1024 + half * 512 + 512],
                                   start=True, stop=True)
                    if half == 1:
                        ins.then_inc(b1m, 1)
                sweep_emit[0] += 1

            for _ in range(32):
                emit_sweep()
            for gj in range(64):
                h, j = gj // 4, gj % 4
                sl = h % 2
                if j == 0:
                    t.wait_ge(bql2[h % 2], 16 * 3 * (h // 2 + 1))
                t.wait_ge(bnm, gj + 1)
                if gj > 0:
                    t.wait_ge(bv3, gj)
                t.transpose(PS[0:1, 512 * 7:512 * 7 + 128], col(32), idf
                            ).then_inc(bt1, 1)
                t.wait_ge(bv1, gj + 1)
                t.matmul(PS[:, 512 * 7:512 * 7 + 1], ones_r, ncsc,
                         start=True, stop=True).then_inc(bt2, 1)
                for g in range(16):
                    if gj * 16 + g >= 1:
                        t.wait_ge(bpt, gj * 16 + g)
                    for half in range(2):
                        kc = 2 * g + half
                        t.matmul(PS[:, (4 + half) * 512:(4 + half) * 512 + 512],
                                 kh(sl)[:, kc * 128:(kc + 1) * 128],
                                 qh(sl)[:, j * 512:(j + 1) * 512],
                                 start=True, stop=True).then_inc(b2m, 1)
                    if g >= 1:
                        for kc in (2 * g - 2, 2 * g - 1):
                            if kc == 0:
                                t.wait_ge(bcx, gj)
                                t.wait_ge(bv2, gj + 1)
                            t.matmul(bank(6), vth(sl)[:, kc * 128:(kc + 1) * 128],
                                     P_T[:, kc * 512:(kc + 1) * 512],
                                     start=(kc == 0), stop=False)
                            t.matmul(PS[0:1, 512 * 7:512 * 7 + 512], ones_cb,
                                     P_T[:, kc * 512:(kc + 1) * 512],
                                     start=(kc == 0), stop=False)
                    if sweep_emit[0] < min((gj + 3) * 16, 1024):
                        emit_sweep()
                t.wait_ge(bpt, gj * 16 + 16)
                for kc in (30, 31):
                    t.matmul(bank(6), vth(sl)[:, kc * 128:(kc + 1) * 128],
                             P_T[:, kc * 512:(kc + 1) * 512],
                             start=False, stop=(kc == 31))
                    ins = t.matmul(PS[0:1, 512 * 7:512 * 7 + 512], ones_cb,
                                   P_T[:, kc * 512:(kc + 1) * 512],
                                   start=False, stop=(kc == 31))
                    if kc == 31:
                        ins.then_inc(bpv, 1)
                t.wait_ge(blv, gj + 1)
                t.matmul(bank(7), ones_r, rrow, start=True, stop=True).then_inc(bt3, 1)
            while sweep_emit[0] < 1024:
                emit_sweep()

            t.nop().then_inc(pb2b, 1)
            t.wait_ge(pb2b, 5)

            t.wait_ge(bcx, 64)
            t.wait_ge(bv3, 64)
            t.wait_ge(bpt, 1024)
            t.wait_ge(b1x, 1024)
            t.wait_ge(c3w, 16 * DC)
            for qt in range(NQT):
                pp = qt % 2
                if qt >= 2:
                    t.wait_ge(c3e, qt - 1)
                for oc in range(4):
                    dst = PS[:, pp * 2048 + oc * 512: pp * 2048 + oc * 512 + 512]
                    for dcc in range(DC):
                        ins = t.matmul(dst,
                                       Ctx[:, dcc * 2048 + qt * 128: dcc * 2048 + (qt + 1) * 128],
                                       WOB[:, dcc * 2048 + oc * 512: dcc * 2048 + oc * 512 + 512],
                                       start=(dcc == 0), stop=(dcc == DC - 1))
                        if dcc == DC - 1:
                            ins.then_inc(c3m, 1)
            t.nop().then_inc(pb3, 1)
            t.wait_ge(pb3, 5)

            t.wait_ge(c3e, NQT)
            for qt in range(NQT):
                t.wait_ge(d5u, qt + 1)
                for dc in range(DC):
                    k = qt * DC + dc
                    if k >= 2:
                        t.wait_ge(d5v, k - 1)
                    tp = bank_bf(0 + (k % 2), 128)
                    usl = u2b[:, (qt % 2) * 2048 + dc * 128:(qt % 2) * 2048 + (dc + 1) * 128]
                    t.transpose(tp, usl, idb).then_inc(d5t, 1)
            t.nop().then_inc(pb35, 1)
            t.wait_ge(pb35, 5)

            t.wait_ge(d5v, NQT * DC)
            for g4 in range(4):
                if g4 > 0:
                    t.wait_ge(e4e, g4 * 4)
                for oc in range(64):
                    gi = g4 * 64 + oc
                    t.wait_ge(e4w2[gi % 2], 16 * 2 * (gi // 2 + 1))
                    if gi >= 2:
                        t.wait_ge(e4s, gi - 1)
                        t.wait_ge(e4h, gi - 1)
                    pg = bank(0 + 2 * (gi % 2))
                    pu = bank(1 + 2 * (gi % 2))
                    for hc in range(DC):
                        ins = t.matmul(pg, wgb[:, (gi % 2) * 2048 + hc * 128:(gi % 2) * 2048 + (hc + 1) * 128],
                                       X1nT[:, hc * 2048 + g4 * 512:hc * 2048 + (g4 + 1) * 512],
                                       start=(hc == 0), stop=(hc == DC - 1))
                        if hc == DC - 1:
                            ins.then_inc(e4g, 1)
                    for hc in range(DC):
                        ins = t.matmul(pu, wub[:, (gi % 2) * 2048 + hc * 128:(gi % 2) * 2048 + (hc + 1) * 128],
                                       X1nT[:, hc * 2048 + g4 * 512:hc * 2048 + (g4 + 1) * 512],
                                       start=(hc == 0), stop=(hc == DC - 1))
                        if hc == DC - 1:
                            ins.then_inc(e4u, 1)
                t.wait_ge(e4h, g4 * 64 + 64)
                for p in range(2):
                    if p == 1:
                        t.wait_ge(e4e, g4 * 4 + 2)
                    for oc in range(64):
                        di = (g4 * 2 + p) * 64 + oc
                        t.wait_ge(e4d2[di % 2], 16 * (di // 2 + 1))
                        for qs in range(2):
                            qloc = p * 2 + qs
                            for ho in range(4):
                                ins = t.matmul(PS[:, qs * 2048 + ho * 512: qs * 2048 + ho * 512 + 512],
                                               HT[:, oc * 512 + qloc * 128: oc * 512 + qloc * 128 + 128],
                                               wdb[:, (di % 2) * 2048 + ho * 512:(di % 2) * 2048 + ho * 512 + 512],
                                               start=(oc == 0), stop=(oc == 63))
                                if qs == 1 and ho == 3:
                                    ins.then_inc(e4bm, 1)

        @block.scalar
        def _(sc):
            sc.wait_ge(s_const, 16 * 6)
            sc.wait_ge(s_constg, 16 * 3)
            for i in range(NT):
                xs = xb[:, (i % 3) * 2048:(i % 3 + 1) * 2048]
                sc.wait_ge(xl[i % 3], 16 * (i // 3 + 1))
                sc.wait_ge(x1c, 2 * i + 1)
                sc.activation(col(8), col(7), AF.Ln)
                sc.activation(col(9 + i % 2), col(8), AF.Exp, scale=-0.5).then_inc(x1a, 1)
                sc.wait_ge(x1c, 2 * i + 2)
                if i >= 2:
                    sc.wait_ge(x1t, 16 * (i - 1))
                sc.activation(ub[:, (i % 2) * 2048:(i % 2 + 1) * 2048], xs, AF.Identity,
                              bias=col(11 + i % 2), scale=col(9 + i % 2)).then_inc(x1u, 1)
            sc.nop().then_inc(pb1, 1)
            sc.wait_ge(pb1, 5)

            for h in range(H):
                for ci, (r, c, n) in enumerate(CH):
                    gcid = h * NCH + ci
                    slot = gcid % 4
                    sc.wait_ge(apq, gcid + 1)
                    sc.wait_ge(avb, h * 3 + r + 1)
                    bcol = col(16 + (h % 2) * 3 + r)
                    scl = QSCALE if r == 0 else 1.0
                    if r == 0:
                        if h > 0 and c == 0:
                            sc.wait_ge(aqsq, 16 * h)
                        rqk = c
                        if h * NQK + rqk >= 2:
                            sc.wait_ge(arp, h * NQK + rqk - 1)
                        sc.activation(qbuf[:, c * 512:(c + 1) * 512], bank(slot),
                                      AF.Identity, bias=bcol, scale=scl)
                        sc.activation(xr0[0:64, :] if (c % 2 == 0) else xr1[0:64, :],
                                      bank(slot)[0:64, :], AF.Identity,
                                      bias=bcol[0:64, :], scale=scl).then_inc(aev, 1)
                    elif r == 1:
                        if h > 0 and c == 0:
                            sc.wait_ge(aqsk, 16 * h)
                        rqk = 4 + c
                        if h * NQK + rqk >= 2:
                            sc.wait_ge(arp, h * NQK + rqk - 1)
                        sc.activation(kbuf[:, c * 512:(c + 1) * 512], bank(slot),
                                      AF.Identity, bias=bcol, scale=scl)
                        sc.activation(xr0[0:64, :] if (rqk % 2 == 0) else xr1[0:64, :],
                                      bank(slot)[0:64, :], AF.Identity,
                                      bias=bcol[0:64, :], scale=scl).then_inc(aev, 1)
                    else:
                        if h > 0 and c == 0:
                            sc.wait_ge(avt, 32 * h)
                            sc.wait_ge(aqsv, 16 * 8 * h)
                        sc.activation(vfb[:, c * 512:(c + 1) * 512], bank(slot),
                                      AF.Identity, bias=bcol, scale=1.0).then_inc(aev, 1)
            sc.nop().then_inc(pb2a, 1)
            sc.wait_ge(pb2a, 5)

            sc.wait_ge(aqsq, 16 * 16)
            sc.wait_ge(aqsk, 16 * 16)
            sc.wait_ge(aqsv, 16 * 128)
            for h in range(2):
                sc.dma_start(out=qh(h % 2), in_=q_scr[h]).then_inc(bql2[h % 2], 16)
                sc.dma_start(out=kh(h % 2), in_=k_scr[h]).then_inc(bql2[h % 2], 16)
                sc.dma_start(out=vth(h % 2), in_=vt_scr[h]).then_inc(bql2[h % 2], 16)
            for h in range(H):
                for j in range(4):
                    gj = h * 4 + j
                    sc.wait_ge(bv2, gj + 1)
                    if gj > 0:
                        sc.wait_ge(bpv, gj)
                    for g in range(16):
                        sc.wait_ge(b2m, gj * 32 + 2 * g + 2)
                        sc.activation(P_T[:, g * 1024:(g + 1) * 1024],
                                      PS[:, 4 * 512:4 * 512 + 1024],
                                      AF.Exp, bias=col(33)).then_inc(bpt, 1)
                if h + 2 < H:
                    sc.wait_ge(bcx, 4 * (h + 1))
                    sc.dma_start(out=qh((h + 2) % 2), in_=q_scr[h + 2]).then_inc(bql2[h % 2], 16)
                    sc.dma_start(out=kh((h + 2) % 2), in_=k_scr[h + 2]).then_inc(bql2[h % 2], 16)
                    sc.dma_start(out=vth((h + 2) % 2), in_=vt_scr[h + 2]).then_inc(bql2[h % 2], 16)
            sc.nop().then_inc(pb2b, 1)
            sc.wait_ge(pb2b, 5)

            for qt in range(NQT):
                sc.wait_ge(c3c, qt + 1)
                sc.activation(col(8), col(4), AF.Ln)
                sc.activation(col(9 + qt % 2), col(8), AF.Exp, scale=-0.5).then_inc(c3a, 1)
            sc.nop().then_inc(pb3, 1)
            sc.wait_ge(pb3, 5)

            sc.wait_ge(c3st2[0], 16 * 8)
            sc.wait_ge(c3st2[1], 16 * 8)
            for qt in range(NQT):
                xs = x1rb[:, (qt % 2) * 2048:(qt % 2 + 1) * 2048]
                sc.wait_ge(d5x2[qt % 2], 16 * (qt // 2 + 1))
                sc.wait_ge(d5c, 2 * qt + 1)
                sc.activation(col(8), col(7), AF.Ln)
                sc.activation(col(9 + qt % 2), col(8), AF.Exp, scale=-0.5).then_inc(d5a, 1)
                sc.wait_ge(d5c, 2 * qt + 2)
                if qt >= 2:
                    sc.wait_ge(d5t, 16 * (qt - 1))
                sc.activation(u2b[:, (qt % 2) * 2048:(qt % 2 + 1) * 2048], xs, AF.Identity,
                              bias=col(11 + qt % 2), scale=col(9 + qt % 2)).then_inc(d5u, 1)
            sc.nop().then_inc(pb35, 1)
            sc.wait_ge(pb35, 5)

            for g4 in range(4):
                for oc in range(64):
                    gi = g4 * 64 + oc
                    sc.wait_ge(e4g, gi + 1)
                    if gi >= 2:
                        sc.wait_ge(e4h, gi - 1)
                    sc.activation(sgb[:, (gi % 2) * 512:(gi % 2 + 1) * 512],
                                  bank(0 + 2 * (gi % 2)), AF.Silu).then_inc(e4s, 1)

        fsem = sem("fence_v")
        _fc = [0]

        @block.vector
        def _(v):
            def F(ins):
                _fc[0] += 1
                ins.then_inc(fsem, 1)
                v.wait_ge(fsem, _fc[0])
            v.wait_ge(s_const, 16 * 6)
            v.wait_ge(s_constg, 16 * 3)
            for i in range(NT):
                xs = xb[:, (i % 3) * 2048:(i % 3 + 1) * 2048]
                v.wait_ge(xl[i % 3], 16 * (i // 3 + 1))
                v.reduce_sum(col(0), xs, axis=AX)
                F(v.scalar_tensor_tensor(jnk1, xs, 1.0, xs, ALU.bypass, ALU.mult,
                                         accum_out=col(1)))
                F(v.tensor_scalar_mul(col(2), col(0), 1.0 / D))
                F(v.tensor_tensor(col(46), col(2), col(2), ALU.mult))
                F(v.tensor_scalar(col(3), col(1), 1.0 / D, col(46), ALU.mult, ALU.subtract))
                F(v.tensor_scalar_add(col(4), col(3), EPS))
                F(v.reciprocal(col(5), col(4)))
                F(v.tensor_scalar(col(6), col(3), col(5), EPS, ALU.mult, ALU.add))
                v.tensor_tensor(col(7), col(4), col(6), ALU.mult).then_inc(x1c, 1)
                v.wait_ge(x1a, i + 1)
                v.tensor_scalar(col(11 + i % 2), col(2), col(9 + i % 2), -1.0,
                                ALU.mult, ALU.mult).then_inc(x1c, 1)
                for dc in range(DC):
                    k = i * DC + dc
                    v.wait_ge(x1t, k + 1)
                    tp = bank_bf(6 + (k % 2), 128)
                    v.tensor_scalar(XnT[:, dc * 4096 + i * 128: dc * 4096 + (i + 1) * 128],
                                    tp, lnw_s[:, dc:dc + 1], None, ALU.mult).then_inc(x1v, 1)
            v.nop().then_inc(pb1, 1)
            v.wait_ge(pb1, 5)

            for h in range(H):
                for r in range(3):
                    v.wait_ge(apb, h * 3 + r + 1)
                    scl = QSCALE if r == 0 else 1.0
                    v.tensor_scalar(col(16 + (h % 2) * 3 + r), PS[:, 512 * 6:512 * 6 + 1],
                                    scl, None, ALU.mult).then_inc(avb, 1)
                for rqk in range(NQK):
                    grp = h * NQK + rqk
                    v.wait_ge(art, grp + 1)
                    xr = xr0 if (rqk % 2 == 0) else xr1
                    rot = PS[0:64, 512 * (4 + rqk % 2):512 * (4 + rqk % 2) + 512]
                    if rqk < 4:
                        pos = rqk * 512
                        tgt = qbuf[0:64, rqk * 512:(rqk + 1) * 512]
                    else:
                        pos = (rqk - 4) * 512
                        tgt = kbuf[0:64, (rqk - 4) * 512:(rqk - 4 + 1) * 512]
                    v.tensor_tensor(t1b[0:64, :], xr[0:64, :], cosb[0:64, pos:pos + 512], ALU.mult)
                    v.tensor_tensor(t2b[0:64, :], rot, sinb[0:64, pos:pos + 512], ALU.mult)
                    v.tensor_add(tgt, t1b[0:64, :], t2b[0:64, :]).then_inc(arp, 1)
                for kc in range(32):
                    gk = h * 32 + kc
                    v.wait_ge(avt, gk + 1)
                    tp = bank_bf(6 + (kc % 2), 128)
                    v.tensor_copy(vfb[:, kc * 128:(kc + 1) * 128], tp).then_inc(avc, 1)
            v.nop().then_inc(pb2a, 1)
            v.wait_ge(pb2a, 5)

            def dve_maxes(tgj):
                base = 28 if (tgj % 2 == 0) else 34
                for ql in range(4):
                    for sw in range(4):
                        gsw = tgj * 16 + ql * 4 + sw
                        v.wait_ge(b1m, gsw + 1)
                        pbk = 0 if (gsw % 2 == 0) else 2
                        ins = v.reduce_max(col(24 + sw), PS[:, pbk * 512:pbk * 512 + 1024],
                                           axis=AX, negate=True)
                        ins.then_inc(b1x, 1)
                        if sw == 3:
                            v.wait_ge(b1x, gsw + 1)
                    v.tensor_reduce(col(base + ql), COLS[:, 24:28], AX, ALU.min
                                    ).then_inc(b1q, 1)

            dve_maxes(0)
            dve_maxes(1)
            for gj in range(64):
                h, j = gj // 4, gj % 4
                base = 28 if (gj % 2 == 0) else 34
                v.wait_ge(b1q, gj * 4 + 4)
                v.tensor_reduce(col(32), COLS[:, base:base + 4], AX, ALU.min
                                ).then_inc(bnm, 1)
                v.wait_ge(bt1, gj + 1)
                F(v.tensor_copy(nmrow, PS[0:1, 512 * 7:512 * 7 + 128]))
                v.tensor_reduce(ncsc, nmrow, AX, ALU.min).then_inc(bv1, 1)
                v.wait_ge(bt2, gj + 1)
                v.tensor_copy(col(33), PS[:, 512 * 7:512 * 7 + 1]).then_inc(bv2, 1)
                if gj + 2 < 64:
                    dve_maxes(gj + 2)
                v.wait_ge(bpv, gj + 1)
                F(v.tensor_copy(lrow, PS[0:1, 512 * 7:512 * 7 + 512]))
                v.reciprocal(rrow, lrow).then_inc(blv, 1)
                v.wait_ge(bt3, gj + 1)
                v.tensor_copy(r_bc, bank(7)).then_inc(bv3, 1)
                v.tensor_tensor(Ctx[:, h * 2048 + j * 512: h * 2048 + (j + 1) * 512],
                                bank(6), r_bc, ALU.mult).then_inc(bcx, 1)
            v.nop().then_inc(pb2b, 1)
            v.wait_ge(pb2b, 5)

            for qt in range(NQT):
                xs = x0b[:, (qt % 2) * 2048:(qt % 2 + 1) * 2048]
                v.wait_ge(c3x2[qt % 2], 16 * (qt // 2 + 1))
                v.reduce_sum(col(0), xs, axis=AX)
                F(v.scalar_tensor_tensor(jnk3, xs, 1.0, xs, ALU.bypass, ALU.mult,
                                         accum_out=col(1)))
                F(v.tensor_scalar_mul(col(2), col(0), 1.0 / D))
                F(v.tensor_tensor(col(46), col(2), col(2), ALU.mult))
                F(v.tensor_scalar(col(3), col(1), 1.0 / D, col(46), ALU.mult, ALU.subtract))
                v.tensor_scalar_add(col(4), col(3), EPS).then_inc(c3c, 1)
                v.wait_ge(c3a, qt + 1)
                al = col(9 + qt % 2)
                F(v.tensor_scalar(col(11 + qt % 2), col(2), al, -1.0, ALU.mult, ALU.mult))
                F(v.tensor_scalar_add(col(13), al, 1.0))
                v.wait_ge(c3m, qt * 4 + 4)
                if qt >= 2:
                    v.wait_ge(c3st2[qt % 2], 16 * (qt // 2))
                pp = qt % 2
                xf = x1f[:, pp * 2048:(pp + 1) * 2048]
                v.scalar_tensor_tensor(xf, xs, col(13), PS[:, pp * 2048:(pp + 1) * 2048],
                                       ALU.mult, ALU.add)
                v.tensor_scalar_add(xf, xf, col(11 + qt % 2)).then_inc(c3e, 1)
            v.nop().then_inc(pb3, 1)
            v.wait_ge(pb3, 5)

            for qt in range(NQT):
                xs = x1rb[:, (qt % 2) * 2048:(qt % 2 + 1) * 2048]
                v.wait_ge(d5x2[qt % 2], 16 * (qt // 2 + 1))
                v.reduce_sum(col(0), xs, axis=AX)
                F(v.scalar_tensor_tensor(jnk35, xs, 1.0, xs, ALU.bypass, ALU.mult,
                                         accum_out=col(1)))
                F(v.tensor_scalar_mul(col(2), col(0), 1.0 / D))
                F(v.tensor_tensor(col(46), col(2), col(2), ALU.mult))
                F(v.tensor_scalar(col(3), col(1), 1.0 / D, col(46), ALU.mult, ALU.subtract))
                v.tensor_scalar_add(col(7), col(3), EPS).then_inc(d5c, 1)
                v.wait_ge(d5a, qt + 1)
                v.tensor_scalar(col(11 + qt % 2), col(2), col(9 + qt % 2), -1.0,
                                ALU.mult, ALU.mult).then_inc(d5c, 1)
                for dc in range(DC):
                    k = qt * DC + dc
                    v.wait_ge(d5t, k + 1)
                    tp = bank_bf(0 + (k % 2), 128)
                    v.tensor_copy(X1nT[:, dc * 2048 + qt * 128: dc * 2048 + (qt + 1) * 128],
                                  tp).then_inc(d5v, 1)
            v.nop().then_inc(pb35, 1)
            v.wait_ge(pb35, 5)

            for g4 in range(4):
                for oc in range(64):
                    gi = g4 * 64 + oc
                    v.wait_ge(e4u, gi + 1)
                    v.wait_ge(e4s, gi + 1)
                    v.tensor_tensor(HT[:, oc * 512:(oc + 1) * 512],
                                    bank(1 + 2 * (gi % 2)),
                                    sgb[:, (gi % 2) * 512:(gi % 2 + 1) * 512],
                                    ALU.mult).then_inc(e4h, 1)
                for p in range(2):
                    for qs in range(2):
                        qt = g4 * 4 + p * 2 + qs
                        v.wait_ge(e4bm, (g4 * 2 + p) * 64 + 64)
                        v.wait_ge(e4x2[qt % 2], 16 * (qt // 2 + 1))
                        if qt >= 2:
                            v.wait_ge(e4o2[qt % 2], 16 * (qt // 2))
                        v.scalar_tensor_tensor(
                            x2f[:, (qt % 2) * 2048:(qt % 2 + 1) * 2048],
                            PS[:, qs * 2048:(qs + 1) * 2048], 1.0,
                            x1r4[:, (qt % 2) * 2048:(qt % 2 + 1) * 2048],
                            ALU.mult, ALU.add).then_inc(e4e, 1)

    return nc


_NC_CACHE = None


def _get_nc():
    global _NC_CACHE
    if _NC_CACHE is None:
        _NC_CACHE = build_program()
    return _NC_CACHE


def _host_tables():
    import ml_dtypes
    rot = 64
    inv_freq = (1.0 / 10000.0 ** (np.arange(0, rot, 2, dtype=np.float32) / rot)).astype(np.float32)
    ang = np.arange(S, dtype=np.float32)[:, None] * inv_freq[None, :]
    ang = np.repeat(ang, 2, axis=-1)
    cos_g = np.cos(ang).T.astype(np.float32)
    sin_g = np.sin(ang).T.astype(np.float32)
    perm = np.zeros((64, 64), dtype=np.float32)
    for i in range(32):
        perm[2 * i + 1, 2 * i] = -1.0
        perm[2 * i, 2 * i + 1] = 1.0
    idb = np.eye(128, dtype=ml_dtypes.bfloat16)
    idf = np.eye(128, dtype=np.float32)
    ones_r = np.ones((1, 128), dtype=np.float32)
    ones_cb = np.ones((128, 1), dtype=ml_dtypes.bfloat16)
    return cos_g, sin_g, perm, idb, idf, ones_r, ones_cb


def kernel(L0_attn_X, L0_attn_ln_w, L0_attn_ln_b, L0_attn_W_in, L0_attn_W_out,
           L0_ffn_w_gate_up, L0_ffn_w_down):
    X = np.ascontiguousarray(np.asarray(L0_attn_X, dtype=np.float32))
    ln_w = np.asarray(L0_attn_ln_w, dtype=np.float32)
    ln_b = np.asarray(L0_attn_ln_b, dtype=np.float32)
    W_in = np.asarray(L0_attn_W_in, dtype=np.float32)
    W_out = np.ascontiguousarray(np.asarray(L0_attn_W_out, dtype=np.float32))
    w_gu = np.asarray(L0_ffn_w_gate_up, dtype=np.float32)
    w_dn = np.asarray(L0_ffn_w_down, dtype=np.float32)

    w_in_t = np.ascontiguousarray(
        W_in.reshape(DC, 128, 48, 128).transpose(2, 1, 0, 3).reshape(48, 128, DC * 128))
    wgu_t = np.ascontiguousarray(
        w_gu.T.reshape(DC, 128, 128, 128).transpose(2, 1, 0, 3).reshape(128, 128, DC * 128))
    wdn_t = np.ascontiguousarray(w_dn.T.reshape(64, 128, D))
    lnw_c = np.ascontiguousarray(ln_w.reshape(DC, 128).T)
    lnb_c = np.ascontiguousarray(ln_b.reshape(DC, 128).T)

    cos_g, sin_g, perm, idb, idf, ones_r, ones_cb = _host_tables()

    nc = _get_nc()
    in_maps = []
    for c in range(8):
        b, half = c // 2, c % 2
        off = half * SQ
        in_maps.append({
            "x": np.roll(X[b], -off, axis=0) if off else X[b],
            "w_in_t": w_in_t,
            "w_out": W_out,
            "wgu_t": wgu_t,
            "wdn_t": wdn_t,
            "lnw_c": lnw_c,
            "lnb_c": lnb_c,
            "cos_t": np.roll(cos_g, -off, axis=1) if off else cos_g,
            "sin_t": np.roll(sin_g, -off, axis=1) if off else sin_g,
            "perm": perm,
            "id_bf": idb,
            "id_f32": idf,
            "ones_r": ones_r,
            "ones_cb": ones_cb,
        })
    res = run_bass_kernel_spmd(nc, in_maps, core_ids=list(range(8)))
    out = np.empty((4, S, D), dtype=np.float32)
    for c in range(8):
        b, half = c // 2, c % 2
        out[b, half * SQ:(half + 1) * SQ, :] = np.asarray(res.results[c]["out"],
                                                          dtype=np.float32)
    return out


# version 03
# speedup vs baseline: 1.1358x; 1.1148x over previous
import numpy as np
from contextlib import ExitStack

import concourse.bass as bass
import concourse.mybir as mybir
from concourse.bass_utils import run_bass_kernel_spmd

F32 = mybir.dt.float32
BF16 = mybir.dt.bfloat16
AF = mybir.ActivationFunctionType
ALU = mybir.AluOpType
AX = mybir.AxisListType.X

D = 2048
S = 4096
SQ = 2048
H = 16
DH = 128
DC = 16
INTER = 8192
NT = S // 128
NQT = SQ // 128
QSCALE = 1.0 / float(np.sqrt(128.0))
EPS = 1e-5

_DEBUG_TAPS = []


def build_program(debug=False):
    nc = bass.Bass("TRN2", target_bir_lowering=False)
    SCR = "ExternalOutput" if debug else "Internal"

    x_in = nc.dram_tensor("x", [S, D], F32, kind="ExternalInput").ap()
    w_in_t = nc.dram_tensor("w_in_t", [48, 128, DC * 128], F32, kind="ExternalInput").ap()
    w_out_in = nc.dram_tensor("w_out", [D, D], F32, kind="ExternalInput").ap()
    wgu_t = nc.dram_tensor("wgu_t", [128, 128, DC * 128], F32, kind="ExternalInput").ap()
    wdn_t = nc.dram_tensor("wdn_t", [64, 128, D], F32, kind="ExternalInput").ap()
    lnw_in = nc.dram_tensor("lnw_c", [128, DC], F32, kind="ExternalInput").ap()
    lnb_in = nc.dram_tensor("lnb_c", [128, DC], F32, kind="ExternalInput").ap()
    cos_in = nc.dram_tensor("cos_t", [64, S], F32, kind="ExternalInput").ap()
    sin_in = nc.dram_tensor("sin_t", [64, S], F32, kind="ExternalInput").ap()
    perm_in = nc.dram_tensor("perm", [64, 64], F32, kind="ExternalInput").ap()
    idb_in = nc.dram_tensor("id_bf", [128, 128], BF16, kind="ExternalInput").ap()
    idf_in = nc.dram_tensor("id_f32", [128, 128], F32, kind="ExternalInput").ap()
    onesr_in = nc.dram_tensor("ones_r", [1, 128], F32, kind="ExternalInput").ap()
    onesc_in = nc.dram_tensor("ones_cb", [128, 1], BF16, kind="ExternalInput").ap()

    out_t = nc.dram_tensor("out", [SQ, D], F32, kind="ExternalOutput").ap()

    q_scr = nc.dram_tensor("q_scr", [H, 128, SQ], BF16, kind=SCR).ap()
    k_scr = nc.dram_tensor("k_scr", [H, 128, S], BF16, kind=SCR).ap()
    vt_scr = nc.dram_tensor("vt_scr", [H, 128, S], BF16, kind=SCR).ap()
    x1_scr = nc.dram_tensor("x1_scr", [NQT, 128, D], F32, kind=SCR).ap()
    wgu_bf = nc.dram_tensor("wgu_bf", [128, 128, DC * 128], BF16, kind="Internal").ap()
    wdn_bf = nc.dram_tensor("wdn_bf", [64, 128, D], BF16, kind="Internal").ap()

    with ExitStack() as ctx:
        E = ctx.enter_context

        A = E(nc.sbuf_tensor("arA", [128, 65536], BF16)).ap()
        B = E(nc.sbuf_tensor("arB", [128, 28672], BF16)).ap()
        C = E(nc.sbuf_tensor("arC", [128, 4096], F32)).ap()
        COLS = E(nc.sbuf_tensor("cols", [128, 64], F32)).ap()
        perm_s = E(nc.sbuf_tensor("perm_s", [64, 64], F32)).ap()
        idb = E(nc.sbuf_tensor("idb", [128, 128], BF16)).ap()
        idf = E(nc.sbuf_tensor("idf", [128, 128], F32)).ap()
        ones_r = E(nc.sbuf_tensor("ones_r_s", [1, 128], F32)).ap()
        ones_cb = E(nc.sbuf_tensor("ones_cb_s", [128, 1], BF16)).ap()
        lnw_s = E(nc.sbuf_tensor("lnw_s", [128, DC], F32)).ap()
        ntab = E(nc.sbuf_tensor("ntab", [1, 128], F32)).ap()
        lnbb = E(nc.sbuf_tensor("lnbb", [128, DC], BF16)).ap()

        PS = E(nc.psum_tensor("ps", [128, 4096], F32)).ap()

        def bank(i, n=512):
            return PS[:, 512 * i: 512 * i + n]

        def bank_bf(i, cols=128, off=0):
            return PS[:, 512 * i + off // 2: 512 * i + (off + cols) // 2].bitcast(BF16)

        XnT = A
        Ctx = A[:, 0:32768]
        P_T = A[:, 32768:49152]
        WOB = A[:, 32768:65536]
        X1nT = A[:, 0:32768]
        HT = A[:, 32768:65536]

        xb = B[:, 0:12288].bitcast(F32)
        ub = B[:, 12288:16384]
        jnk1 = B[:, 16384:20480].bitcast(F32)

        wbuf = B[:, 0:12288]
        qbuf = B[:, 12288:14336]
        kbuf = B[:, 14336:18432]
        vfb = B[:, 18432:22528]
        xr0 = B[:, 22528:23552].bitcast(F32)
        xr1 = B[:, 23552:24576].bitcast(F32)
        t1b = B[:, 24576:25600].bitcast(F32)
        t2b = B[:, 25600:26624].bitcast(F32)
        sqst = B[:, 26624:27648]

        def qh(s):
            return B[:, s * 10240: s * 10240 + 2048]

        def kh(s):
            return B[:, s * 10240 + 2048: s * 10240 + 6144]

        def vth(s):
            return B[:, s * 10240 + 6144: s * 10240 + 10240]

        x0b = B[:, 0:8192].bitcast(F32)
        x1f = B[:, 8192:16384].bitcast(F32)
        jnk3 = B[:, 16384:20480].bitcast(F32)

        x1rb = B[:, 0:8192].bitcast(F32)
        u2b = B[:, 8192:12288]
        jnk35 = B[:, 12288:16384].bitcast(F32)

        wgb = B[:, 0:4096]
        wub = B[:, 4096:8192]
        wdb = B[:, 8192:16384]
        x1r4 = B[:, 16384:24576].bitcast(F32)
        x2fB = B[:, 24576:28672].bitcast(F32)

        cosb = C[:, 0:2048].bitcast(BF16)
        sinb = C[:, 2048:4096].bitcast(BF16)
        r_bc = C[:, 0:512]
        lrow = C[0:1, 512:1024]
        rrow = C[0:1, 1024:1536]
        nmrow = C[0:1, 1536:1664]
        ncsc = C[0:1, 1664:1665]
        ktmp = COLS[0:1, 47:48]
        ztab = C[0:1, 1696:1760]
        nbtab = C[0:1, 1760:1824]
        sgb = C[:, 0:1024]
        x2fC = C[:, 2048:4096]

        def col(i, n=1):
            return COLS[:, i:i + n]

        def sem(name):
            return E(nc.semaphore(name))

        pb1, pb2a, pb2b, pb3, pb35 = sem("pb1"), sem("pb2a"), sem("pb2b"), sem("pb3"), sem("pb35")
        x1q, x1c, x1a, x1u, x1t, x1v = (sem(n) for n in
            ("x1q", "x1c", "x1a", "x1u", "x1t", "x1v"))
        xl = [sem(f"xl{i}") for i in range(3)]
        ahd, apb, avb, apq, aev, art, arp, avt, avc, acache = (sem(n) for n in
            ("ahd", "apb", "avb", "apq", "aev", "art", "arp", "avt", "avc", "acache"))
        aw2 = [sem(f"aw{i}") for i in range(2)]
        aqsq, aqsk, aqsv = sem("aqsq"), sem("aqsk"), sem("aqsv")
        b1m, b1x, b1q, bnm, bt1, bv1, bt2, bv2, b2m, bpt, bpv, blv, bt3, bv3, bcx, bhd = (
            sem(n) for n in ("b1m", "b1x", "b1q", "bnm", "bt1", "bv1", "bt2",
                             "bv2", "b2m", "bpt", "bpv", "blv", "bt3", "bv3", "bcx", "bhd"))
        bql2 = [sem(f"bql{i}") for i in range(2)]
        c3w, c3q, c3c, c3a, c3g, c3m, c3e = (sem(n) for n in
            ("c3w", "c3q", "c3c", "c3a", "c3g", "c3m", "c3e"))
        c3x2 = [sem(f"c3x{i}") for i in range(2)]
        c3st2 = [sem(f"c3st{i}") for i in range(2)]
        d5q, d5c, d5a, d5u, d5t, d5v = (sem(n) for n in
            ("d5q", "d5c", "d5a", "d5u", "d5t", "d5v"))
        d5x2 = [sem(f"d5x{i}") for i in range(2)]
        e4g, e4u, e4s, e4h, e4bm, e4e = (sem(n) for n in
            ("e4g", "e4u", "e4s", "e4h", "e4bm", "e4e"))
        aqn, anm, anr, bz1, bz2 = (sem(n) for n in ("aqn", "anm", "anr", "bz1", "bz2"))
        e4w2 = [sem(f"e4w{i}") for i in range(2)]
        e4d2 = [sem(f"e4d{i}") for i in range(2)]
        e4x2 = [sem(f"e4x{i}") for i in range(2)]
        e4o2 = [sem(f"e4o{i}") for i in range(2)]
        s_const = sem("s_const")
        s_constg = sem("s_constg")

        block = E(nc.Block())

        CH = []
        for r, n in ((0, 4), (1, 8), (2, 8)):
            for c in range(n):
                CH.append((r, c, n))
        NCH = len(CH)
        NQK = 12

        def fcol(h, r):
            return r * 16 + h

        @block.sync
        def _(s):
            s.dma_start(out=perm_s, in_=perm_in).then_inc(s_const, 16)
            s.dma_start(out=idb, in_=idb_in).then_inc(s_const, 16)
            s.dma_start(out=idf, in_=idf_in).then_inc(s_const, 16)
            s.dma_start(out=ones_r, in_=onesr_in).then_inc(s_const, 16)
            s.dma_start(out=ones_cb, in_=onesc_in).then_inc(s_const, 16)
            s.dma_start(out=lnw_s, in_=lnw_in).then_inc(s_const, 16)

            for i in range(NT):
                if i >= 3:
                    s.wait_ge(x1u, i - 2)
                s.dma_start(out=xb[:, (i % 3) * 2048:(i % 3 + 1) * 2048],
                            in_=x_in[i * 128:(i + 1) * 128, :]).then_inc(xl[i % 3], 16)

            s.nop().then_inc(pb1, 1)
            s.wait_ge(pb1, 5)
            for h in range(H):
                s.wait_ge(arp, h * NQK + 4)
                s.wait_ge(aev, h * NCH + 4)
                s.dma_start(out=q_scr[h], in_=qbuf).then_inc(aqsq, 16)
                s.wait_ge(arp, h * NQK + 12)
                s.wait_ge(aev, h * NCH + 12)
                s.dma_start(out=k_scr[h], in_=kbuf).then_inc(aqsk, 16)
                for g in range(8):
                    s.wait_ge(avc, h * 32 + 4 * (g + 1))
                    s.dma_start(out=vt_scr[h][:, g * 512:(g + 1) * 512],
                                in_=vfb[:, g * 512:(g + 1) * 512]).then_inc(aqsv, 16)

            s.nop().then_inc(pb2a, 1)
            s.wait_ge(pb2a, 5)
            s.nop().then_inc(pb2b, 1)
            s.wait_ge(pb2b, 5)

            s.wait_ge(bpv, 64)
            for qt in range(NQT):
                if qt >= 2:
                    s.wait_ge(c3e, qt - 1)
                s.dma_start(out=x0b[:, (qt % 2) * 2048:(qt % 2 + 1) * 2048],
                            in_=x_in[qt * 128:(qt + 1) * 128, :]).then_inc(c3x2[qt % 2], 16)
                s.wait_ge(c3e, qt + 1)
                s.dma_start(out=x1_scr[qt],
                            in_=x1f[:, (qt % 2) * 2048:(qt % 2 + 1) * 2048]).then_inc(c3st2[qt % 2], 16)

            s.nop().then_inc(pb3, 1)
            s.wait_ge(pb3, 5)

            s.wait_ge(c3st2[0], 16 * 8)
            s.wait_ge(c3st2[1], 16 * 8)
            for qt in range(NQT):
                if qt >= 2:
                    s.wait_ge(d5u, qt - 1)
                s.dma_start(out=x1rb[:, (qt % 2) * 2048:(qt % 2 + 1) * 2048],
                            in_=x1_scr[qt]).then_inc(d5x2[qt % 2], 16)

            s.nop().then_inc(pb35, 1)
            s.wait_ge(pb35, 5)

            s.wait_ge(d5u, NQT)
            for g in range(4):
                for oc in range(64):
                    gi = g * 64 + oc
                    if gi >= 2:
                        s.wait_ge(e4g, gi - 1)
                        s.wait_ge(e4u, gi - 1)
                    s.wait_ge(acache, 16 * 24)
                    s.dma_start(out=wgb[:, (gi % 2) * 2048:(gi % 2 + 1) * 2048],
                                in_=wgu_bf[oc]).then_inc(e4w2[gi % 2], 16)
                    s.dma_start(out=wub[:, (gi % 2) * 2048:(gi % 2 + 1) * 2048],
                                in_=wgu_bf[64 + oc]).then_inc(e4w2[gi % 2], 16)
                for p in range(2):
                    for qs in range(2):
                        qt = g * 4 + p * 2 + qs
                        if qt == 0:
                            s.wait_ge(c3st2[0], 16 * 8)
                            s.wait_ge(c3st2[1], 16 * 8)
                        if qt >= 2:
                            s.wait_ge(e4e, qt - 1)
                        s.dma_start(out=x1r4[:, (qt % 2) * 2048:(qt % 2 + 1) * 2048],
                                    in_=x1_scr[qt]).then_inc(e4x2[qt % 2], 16)
                        s.wait_ge(e4e, qt + 1)
                        s.dma_start(out=out_t[qt * 128:(qt + 1) * 128, :],
                                    in_=(x2fB if (qt % 2 == 0) else x2fC.copy())
                                    ).then_inc(e4o2[qt % 2], 16)
            s.wait_ge(e4o2[0], 16 * 8)
            s.wait_ge(e4o2[1], 16 * 8)

        @block.gpsimd
        def _(g):
            g.dma_start(out=lnbb, in_=lnb_in).then_inc(s_constg, 16)
            g.dma_start(out=cosb[0:64, :], in_=cos_in).then_inc(s_constg, 16)
            g.dma_start(out=sinb[0:64, :], in_=sin_in).then_inc(s_constg, 16)
            g.nop().then_inc(pb1, 1)
            g.wait_ge(pb1, 5)
            g.wait_ge(x1u, NT)
            g.wait_ge(x1c, 2 * NT)

            cache_jobs = []
            for i in range(16):
                cache_jobs.append((wgu_bf[i * 8:(i + 1) * 8], wgu_t[i * 8:(i + 1) * 8]))
            for i in range(8):
                cache_jobs.append((wdn_bf[i * 8:(i + 1) * 8], wdn_t[i * 8:(i + 1) * 8]))
            cj = 0
            for h in range(H):
                if h >= 2:
                    g.wait_ge(apq, (h - 1) * NCH)
                for r in range(3):
                    g.dma_start(out=wbuf[:, ((h % 2) * 3 + r) * 2048:((h % 2) * 3 + r + 1) * 2048],
                                in_=w_in_t[fcol(h, r)]).then_inc(aw2[h % 2], 16)
                for _ in range(2):
                    if cj < len(cache_jobs):
                        dst, src = cache_jobs[cj]
                        g.dma_start(out=dst, in_=src).then_inc(acache, 16)
                        cj += 1
            while cj < len(cache_jobs):
                dst, src = cache_jobs[cj]
                g.dma_start(out=dst, in_=src).then_inc(acache, 16)
                cj += 1

            g.nop().then_inc(pb2a, 1)
            g.wait_ge(pb2a, 5)
            g.nop().then_inc(pb2b, 1)
            g.wait_ge(pb2b, 5)

            g.wait_ge(bpv, 64)
            for dc in range(DC):
                g.dma_start(out=WOB[:, dc * 2048:(dc + 1) * 2048],
                            in_=w_out_in[dc * 128:(dc + 1) * 128, :]).then_inc(c3w, 16)
            g.nop().then_inc(pb3, 1)
            g.wait_ge(pb3, 5)
            g.nop().then_inc(pb35, 1)
            g.wait_ge(pb35, 5)

            g.wait_ge(d5t, NQT * DC)
            for gi in range(4):
                for p in range(2):
                    for oc in range(64):
                        di = (gi * 2 + p) * 64 + oc
                        if di >= 4:
                            g.wait_ge(e4bm, di - 3)
                        g.wait_ge(acache, 16 * 24)
                        g.dma_start(out=wdb[:, (di % 4) * 2048:(di % 4 + 1) * 2048],
                                    in_=wdn_bf[oc]).then_inc(e4d2[di % 2], 16)

        @block.tensor
        def _(t):
            t.wait_ge(s_const, 16 * 6)
            t.wait_ge(s_constg, 16 * 3)
            for i in range(NT):
                t.wait_ge(x1u, i + 1)
                for dc in range(DC):
                    k = i * DC + dc
                    if k >= 2:
                        t.wait_ge(x1v, k - 1)
                    tp = bank_bf(6 + (k % 2), 128)
                    usl = ub[:, (i % 2) * 2048 + dc * 128:(i % 2) * 2048 + (dc + 1) * 128]
                    t.transpose(tp, usl, idb).then_inc(x1t, 1)
            t.nop().then_inc(pb1, 1)
            t.wait_ge(pb1, 5)
            t.wait_ge(x1v, 512)

            for h in range(H):
                t.wait_ge(aw2[h % 2], 16 * 3 * (h // 2 + 1))
                for r in range(3):
                    pb_ = PS[:, 512 * 6:512 * 6 + 1]
                    if h > 0 or r > 0:
                        t.wait_ge(avb, h * 3 + r)
                    if h > 0 and r == 0:
                        t.wait_ge(avc, 32 * h)
                    ws = wbuf[:, ((h % 2) * 3 + r) * 2048:((h % 2) * 3 + r + 1) * 2048]
                    for dc in range(DC):
                        ins = t.matmul(pb_, ws[:, dc * 128:(dc + 1) * 128], lnbb[:, dc:dc + 1],
                                       start=(dc == 0), stop=(dc == DC - 1))
                        if dc == DC - 1:
                            ins.then_inc(apb, 1)

                rope_pending = []
                for ci, (r, c, n) in enumerate(CH):
                    gcid = h * NCH + ci
                    slot = gcid % 4
                    if gcid >= 4:
                        t.wait_ge(aev, gcid - 3)
                    ws = wbuf[:, ((h % 2) * 3 + r) * 2048:((h % 2) * 3 + r + 1) * 2048]
                    dst = bank(slot)
                    toff = c * 512 if r == 0 else c * 512
                    for dc in range(DC):
                        ins = t.matmul(dst, ws[:, dc * 128:(dc + 1) * 128],
                                       XnT[:, dc * 4096 + toff:dc * 4096 + toff + 512],
                                       start=(dc == 0), stop=(dc == DC - 1))
                        if dc == DC - 1:
                            ins.then_inc(apq, 1)
                    if rope_pending:
                        rj, rslot = rope_pending.pop(0)
                        t.wait_ge(aev, rj + 1)
                        if rslot == 0 and h > 0:
                            t.wait_ge(anr, 12 * h)
                        if (h * NQK + rslot) >= 1:
                            t.wait_ge(arp, h * NQK + rslot)
                        xr = xr0 if (rslot % 2 == 0) else xr1
                        t.matmul(PS[0:64, 512 * 4:512 * 4 + 512],
                                 perm_s, xr[0:64, :], start=True, stop=True).then_inc(art, 1)
                    if r in (0, 1):
                        rqk = c if r == 0 else 4 + c
                        rope_pending.append((gcid, rqk))
                while rope_pending:
                    rj, rslot = rope_pending.pop(0)
                    t.wait_ge(aev, rj + 1)
                    if (h * NQK + rslot) >= 1:
                        t.wait_ge(arp, h * NQK + rslot)
                    xr = xr0 if (rslot % 2 == 0) else xr1
                    t.matmul(PS[0:64, 512 * 4:512 * 4 + 512],
                             perm_s, xr[0:64, :], start=True, stop=True).then_inc(art, 1)

                t.wait_ge(arp, h * NQK + NQK)
                for c in range(12):
                    t.wait_ge(aqn, h * 12 + c + 1)
                    if h * 12 + c >= 1:
                        t.wait_ge(anr, h * 12 + c)
                    t.matmul(PS[0:1, 512 * 5:512 * 5 + 512],
                             ones_cb, sqst[:, (c % 2) * 512:(c % 2 + 1) * 512],
                             start=True, stop=True).then_inc(anm, 1)
                t.wait_ge(avb, h * 3 + 3)
                for kc in range(32):
                    t.wait_ge(aev, h * NCH + 12 + (kc // 4) + 1)
                    gk = h * 32 + kc
                    if gk >= 2:
                        t.wait_ge(avc, gk - 1)
                    tp = bank_bf(6 + (kc % 2), 128)
                    t.transpose(tp, vfb[:, kc * 128:(kc + 1) * 128], idb).then_inc(avt, 1)

            t.nop().then_inc(pb2a, 1)
            t.wait_ge(pb2a, 5)

            t.wait_ge(aev, H * NCH)
            t.wait_ge(avc, H * 32)
            t.wait_ge(arp, H * NQK)
            t.wait_ge(anr, H * 12)
            for gj in range(64):
                h, j = gj // 4, gj % 4
                sl = h % 2
                if j == 0:
                    t.wait_ge(bql2[h % 2], 16 * 3 * (h // 2 + 1))
                if gj == 0:
                    t.wait_ge(bz2, 2)
                else:
                    t.wait_ge(bv3, gj)
                t.matmul(PS[:, 512 * 7:512 * 7 + 1], ones_r, nbtab[0:1, gj:gj + 1],
                         start=True, stop=True).then_inc(bt2, 1)
                for g in range(16):
                    if gj * 16 + g >= 3:
                        t.wait_ge(bpt, gj * 16 + g - 2)
                    pb_ = (g % 3) * 2
                    for half in range(2):
                        kc = 2 * g + half
                        t.matmul(PS[:, (pb_ + half) * 512:(pb_ + half) * 512 + 512],
                                 kh(sl)[:, kc * 128:(kc + 1) * 128],
                                 qh(sl)[:, j * 512:(j + 1) * 512],
                                 start=True, stop=True).then_inc(b2m, 1)
                    if g >= 1:
                        t.wait_ge(bpt, gj * 16 + g)
                        for kc in (2 * g - 2, 2 * g - 1):
                            if kc == 0:
                                t.wait_ge(bcx, gj)
                                t.wait_ge(bv2, gj + 1)
                            t.matmul(bank(6), vth(sl)[:, kc * 128:(kc + 1) * 128],
                                     P_T[:, kc * 512:(kc + 1) * 512],
                                     start=(kc == 0), stop=False)
                            t.matmul(PS[0:1, 512 * 7:512 * 7 + 512], ones_cb,
                                     P_T[:, kc * 512:(kc + 1) * 512],
                                     start=(kc == 0), stop=False)
                t.wait_ge(bpt, gj * 16 + 16)
                for kc in (30, 31):
                    t.matmul(bank(6), vth(sl)[:, kc * 128:(kc + 1) * 128],
                             P_T[:, kc * 512:(kc + 1) * 512],
                             start=False, stop=(kc == 31))
                    ins = t.matmul(PS[0:1, 512 * 7:512 * 7 + 512], ones_cb,
                                   P_T[:, kc * 512:(kc + 1) * 512],
                                   start=False, stop=(kc == 31))
                    if kc == 31:
                        ins.then_inc(bpv, 1)
                t.wait_ge(blv, gj + 1)
                t.matmul(bank(7), ones_r, rrow, start=True, stop=True).then_inc(bt3, 1)

            t.nop().then_inc(pb2b, 1)
            t.wait_ge(pb2b, 5)

            t.wait_ge(bcx, 64)
            t.wait_ge(bv3, 64)
            t.wait_ge(bpt, 1024)
            t.wait_ge(c3w, 16 * DC)
            for qt in range(NQT):
                pp = qt % 2
                if qt >= 2:
                    t.wait_ge(c3e, qt - 1)
                for oc in range(4):
                    dst = PS[:, pp * 2048 + oc * 512: pp * 2048 + oc * 512 + 512]
                    for dcc in range(DC):
                        ins = t.matmul(dst,
                                       Ctx[:, dcc * 2048 + qt * 128: dcc * 2048 + (qt + 1) * 128],
                                       WOB[:, dcc * 2048 + oc * 512: dcc * 2048 + oc * 512 + 512],
                                       start=(dcc == 0), stop=(dcc == DC - 1))
                        if dcc == DC - 1:
                            ins.then_inc(c3m, 1)
            t.nop().then_inc(pb3, 1)
            t.wait_ge(pb3, 5)

            t.wait_ge(c3e, NQT)
            for qt in range(NQT):
                t.wait_ge(d5u, qt + 1)
                for dc in range(DC):
                    k = qt * DC + dc
                    if k >= 2:
                        t.wait_ge(d5v, k - 1)
                    tp = bank_bf(0 + (k % 2), 128)
                    usl = u2b[:, (qt % 2) * 2048 + dc * 128:(qt % 2) * 2048 + (dc + 1) * 128]
                    t.transpose(tp, usl, idb).then_inc(d5t, 1)
            t.nop().then_inc(pb35, 1)
            t.wait_ge(pb35, 5)

            t.wait_ge(d5v, NQT * DC)
            for g4 in range(4):
                if g4 > 0:
                    t.wait_ge(e4e, g4 * 4)
                for oc in range(64):
                    gi = g4 * 64 + oc
                    t.wait_ge(e4w2[gi % 2], 16 * 2 * (gi // 2 + 1))
                    if gi >= 2:
                        t.wait_ge(e4s, gi - 1)
                        t.wait_ge(e4h, gi - 1)
                    pg = bank(0 + 2 * (gi % 2))
                    pu = bank(1 + 2 * (gi % 2))
                    for hc in range(DC):
                        ins = t.matmul(pg, wgb[:, (gi % 2) * 2048 + hc * 128:(gi % 2) * 2048 + (hc + 1) * 128],
                                       X1nT[:, hc * 2048 + g4 * 512:hc * 2048 + (g4 + 1) * 512],
                                       start=(hc == 0), stop=(hc == DC - 1))
                        if hc == DC - 1:
                            ins.then_inc(e4g, 1)
                    for hc in range(DC):
                        ins = t.matmul(pu, wub[:, (gi % 2) * 2048 + hc * 128:(gi % 2) * 2048 + (hc + 1) * 128],
                                       X1nT[:, hc * 2048 + g4 * 512:hc * 2048 + (g4 + 1) * 512],
                                       start=(hc == 0), stop=(hc == DC - 1))
                        if hc == DC - 1:
                            ins.then_inc(e4u, 1)
                t.wait_ge(e4h, g4 * 64 + 64)
                for p in range(2):
                    if p == 1:
                        t.wait_ge(e4e, g4 * 4 + 2)
                    for oc in range(64):
                        di = (g4 * 2 + p) * 64 + oc
                        t.wait_ge(e4d2[di % 2], 16 * (di // 2 + 1))
                        for qs in range(2):
                            qloc = p * 2 + qs
                            for ho in range(4):
                                ins = t.matmul(PS[:, qs * 2048 + ho * 512: qs * 2048 + ho * 512 + 512],
                                               HT[:, oc * 512 + qloc * 128: oc * 512 + qloc * 128 + 128],
                                               wdb[:, (di % 4) * 2048 + ho * 512:(di % 4) * 2048 + ho * 512 + 512],
                                               start=(oc == 0), stop=(oc == 63))
                                if qs == 1 and ho == 3:
                                    ins.then_inc(e4bm, 1)

        @block.scalar
        def _(sc):
            sc.wait_ge(s_const, 16 * 6)
            sc.wait_ge(s_constg, 16 * 3)
            for i in range(NT):
                xs = xb[:, (i % 3) * 2048:(i % 3 + 1) * 2048]
                sc.wait_ge(xl[i % 3], 16 * (i // 3 + 1))
                sc.wait_ge(x1c, 2 * i + 1)
                sc.activation(col(8), col(7), AF.Ln)
                sc.activation(col(9 + i % 2), col(8), AF.Exp, scale=-0.5).then_inc(x1a, 1)
                sc.wait_ge(x1c, 2 * i + 2)
                if i >= 2:
                    sc.wait_ge(x1t, 16 * (i - 1))
                sc.activation(ub[:, (i % 2) * 2048:(i % 2 + 1) * 2048], xs, AF.Identity,
                              bias=col(11 + i % 2), scale=col(9 + i % 2)).then_inc(x1u, 1)
            sc.nop().then_inc(pb1, 1)
            sc.wait_ge(pb1, 5)

            for h in range(H):
                for ci, (r, c, n) in enumerate(CH):
                    gcid = h * NCH + ci
                    slot = gcid % 4
                    sc.wait_ge(apq, gcid + 1)
                    sc.wait_ge(avb, h * 3 + r + 1)
                    bcol = col(16 + (h % 2) * 3 + r)
                    scl = QSCALE if r == 0 else 1.0
                    if r == 0:
                        if h > 0 and c == 0:
                            sc.wait_ge(aqsq, 16 * h)
                        rqk = c
                        if h * NQK + rqk >= 2:
                            sc.wait_ge(arp, h * NQK + rqk - 1)
                        sc.activation(qbuf[:, c * 512:(c + 1) * 512], bank(slot),
                                      AF.Identity, bias=bcol, scale=scl)
                        sc.activation(xr0[0:64, :] if (c % 2 == 0) else xr1[0:64, :],
                                      bank(slot)[0:64, :], AF.Identity,
                                      bias=bcol[0:64, :], scale=scl).then_inc(aev, 1)
                    elif r == 1:
                        if h > 0 and c == 0:
                            sc.wait_ge(aqsk, 16 * h)
                        rqk = 4 + c
                        if h * NQK + rqk >= 2:
                            sc.wait_ge(arp, h * NQK + rqk - 1)
                        sc.activation(kbuf[:, c * 512:(c + 1) * 512], bank(slot),
                                      AF.Identity, bias=bcol, scale=scl)
                        sc.activation(xr0[0:64, :] if (rqk % 2 == 0) else xr1[0:64, :],
                                      bank(slot)[0:64, :], AF.Identity,
                                      bias=bcol[0:64, :], scale=scl).then_inc(aev, 1)
                    else:
                        if h > 0 and c == 0:
                            sc.wait_ge(avt, 32 * h)
                            sc.wait_ge(aqsv, 16 * 8 * h)
                        sc.activation(vfb[:, c * 512:(c + 1) * 512], bank(slot),
                                      AF.Identity, bias=bcol, scale=1.0).then_inc(aev, 1)
                for c in range(12):
                    sc.wait_ge(arp, h * NQK + c + 1)
                    if h * 12 + c >= 2:
                        sc.wait_ge(anm, h * 12 + c - 1)
                    srcap = (qbuf[:, c * 512:(c + 1) * 512] if c < 4
                             else kbuf[:, (c - 4) * 512:(c - 3) * 512])
                    sc.activation(sqst[:, (c % 2) * 512:(c % 2 + 1) * 512], srcap,
                                  AF.Square).then_inc(aqn, 1)
            sc.nop().then_inc(pb2a, 1)
            sc.wait_ge(pb2a, 5)

            sc.wait_ge(aqsq, 16 * 16)
            sc.wait_ge(aqsk, 16 * 16)
            sc.wait_ge(aqsv, 16 * 128)
            for h in range(2):
                sc.dma_start(out=qh(h % 2), in_=q_scr[h]).then_inc(bql2[h % 2], 16)
                sc.dma_start(out=kh(h % 2), in_=k_scr[h]).then_inc(bql2[h % 2], 16)
                sc.dma_start(out=vth(h % 2), in_=vt_scr[h]).then_inc(bql2[h % 2], 16)
            sc.wait_ge(bz1, 1)
            sc.activation(ztab, ztab, AF.Ln)
            sc.activation(ztab, ztab, AF.Exp, scale=0.5).then_inc(bz2, 1)
            for h in range(H):
                for j in range(4):
                    gj = h * 4 + j
                    sc.wait_ge(bv2, gj + 1)
                    if gj > 0:
                        sc.wait_ge(bpv, gj)
                    for g in range(16):
                        sc.wait_ge(b2m, gj * 32 + 2 * g + 2)
                        pb_ = (g % 3) * 2
                        sc.activation(P_T[:, g * 1024:(g + 1) * 1024],
                                      PS[:, pb_ * 512:pb_ * 512 + 1024],
                                      AF.Exp, bias=col(33)).then_inc(bpt, 1)
                if h + 2 < H:
                    sc.wait_ge(bcx, 4 * (h + 1))
                    sc.dma_start(out=qh((h + 2) % 2), in_=q_scr[h + 2]).then_inc(bql2[h % 2], 16)
                    sc.dma_start(out=kh((h + 2) % 2), in_=k_scr[h + 2]).then_inc(bql2[h % 2], 16)
                    sc.dma_start(out=vth((h + 2) % 2), in_=vt_scr[h + 2]).then_inc(bql2[h % 2], 16)
            sc.nop().then_inc(pb2b, 1)
            sc.wait_ge(pb2b, 5)

            for qt in range(NQT):
                sc.wait_ge(c3c, qt + 1)
                sc.activation(col(8), col(4), AF.Ln)
                sc.activation(col(9 + qt % 2), col(8), AF.Exp, scale=-0.5).then_inc(c3a, 1)
            sc.nop().then_inc(pb3, 1)
            sc.wait_ge(pb3, 5)

            sc.wait_ge(c3st2[0], 16 * 8)
            sc.wait_ge(c3st2[1], 16 * 8)
            for qt in range(NQT):
                xs = x1rb[:, (qt % 2) * 2048:(qt % 2 + 1) * 2048]
                sc.wait_ge(d5x2[qt % 2], 16 * (qt // 2 + 1))
                sc.wait_ge(d5c, 2 * qt + 1)
                sc.activation(col(8), col(7), AF.Ln)
                sc.activation(col(9 + qt % 2), col(8), AF.Exp, scale=-0.5).then_inc(d5a, 1)
                sc.wait_ge(d5c, 2 * qt + 2)
                if qt >= 2:
                    sc.wait_ge(d5t, 16 * (qt - 1))
                sc.activation(u2b[:, (qt % 2) * 2048:(qt % 2 + 1) * 2048], xs, AF.Identity,
                              bias=col(11 + qt % 2), scale=col(9 + qt % 2)).then_inc(d5u, 1)
            sc.nop().then_inc(pb35, 1)
            sc.wait_ge(pb35, 5)

            for g4 in range(4):
                for oc in range(64):
                    gi = g4 * 64 + oc
                    sc.wait_ge(e4g, gi + 1)
                    if gi >= 2:
                        sc.wait_ge(e4h, gi - 1)
                    sc.activation(sgb[:, (gi % 2) * 512:(gi % 2 + 1) * 512],
                                  bank(0 + 2 * (gi % 2)), AF.Silu).then_inc(e4s, 1)

        fsem = sem("fence_v")
        _fc = [0]

        @block.vector
        def _(v):
            def F(ins):
                _fc[0] += 1
                ins.then_inc(fsem, 1)
                v.wait_ge(fsem, _fc[0])
            v.wait_ge(s_const, 16 * 6)
            v.wait_ge(s_constg, 16 * 3)
            for i in range(NT):
                xs = xb[:, (i % 3) * 2048:(i % 3 + 1) * 2048]
                v.wait_ge(xl[i % 3], 16 * (i // 3 + 1))
                v.reduce_sum(col(0), xs, axis=AX)
                F(v.scalar_tensor_tensor(jnk1, xs, 1.0, xs, ALU.bypass, ALU.mult,
                                         accum_out=col(1)))
                F(v.tensor_scalar_mul(col(2), col(0), 1.0 / D))
                F(v.tensor_tensor(col(46), col(2), col(2), ALU.mult))
                F(v.tensor_scalar(col(3), col(1), 1.0 / D, col(46), ALU.mult, ALU.subtract))
                F(v.tensor_scalar_add(col(4), col(3), EPS))
                F(v.reciprocal(col(5), col(4)))
                F(v.tensor_scalar(col(6), col(3), col(5), EPS, ALU.mult, ALU.add))
                v.tensor_tensor(col(7), col(4), col(6), ALU.mult).then_inc(x1c, 1)
                v.wait_ge(x1a, i + 1)
                v.tensor_scalar(col(11 + i % 2), col(2), col(9 + i % 2), -1.0,
                                ALU.mult, ALU.mult).then_inc(x1c, 1)
                for dc in range(DC):
                    k = i * DC + dc
                    v.wait_ge(x1t, k + 1)
                    tp = bank_bf(6 + (k % 2), 128)
                    v.tensor_scalar(XnT[:, dc * 4096 + i * 128: dc * 4096 + (i + 1) * 128],
                                    tp, lnw_s[:, dc:dc + 1], None, ALU.mult).then_inc(x1v, 1)
            v.nop().then_inc(pb1, 1)
            v.wait_ge(pb1, 5)

            for h in range(H):
                for r in range(3):
                    v.wait_ge(apb, h * 3 + r + 1)
                    scl = QSCALE if r == 0 else 1.0
                    v.tensor_scalar(col(16 + (h % 2) * 3 + r), PS[:, 512 * 6:512 * 6 + 1],
                                    scl, None, ALU.mult).then_inc(avb, 1)
                for rqk in range(NQK):
                    grp = h * NQK + rqk
                    v.wait_ge(art, grp + 1)
                    xr = xr0 if (rqk % 2 == 0) else xr1
                    rot = PS[0:64, 512 * 4:512 * 4 + 512]
                    if rqk < 4:
                        pos = rqk * 512
                        tgt = qbuf[0:64, rqk * 512:(rqk + 1) * 512]
                    else:
                        pos = (rqk - 4) * 512
                        tgt = kbuf[0:64, (rqk - 4) * 512:(rqk - 4 + 1) * 512]
                    v.tensor_tensor(t1b[0:64, :], xr[0:64, :], cosb[0:64, pos:pos + 512], ALU.mult)
                    v.tensor_tensor(t2b[0:64, :], rot, sinb[0:64, pos:pos + 512], ALU.mult)
                    v.tensor_add(tgt, t1b[0:64, :], t2b[0:64, :]).then_inc(arp, 1)
                for c in range(12):
                    v.wait_ge(anm, h * 12 + c + 1)
                    row = PS[0:1, 512 * 5:512 * 5 + 512]
                    if c < 4:
                        v.tensor_reduce(ntab[0:1, 4 * h + c:4 * h + c + 1], row,
                                        AX, ALU.max).then_inc(anr, 1)
                    elif c == 4:
                        v.tensor_reduce(ntab[0:1, 64 + h:64 + h + 1], row,
                                        AX, ALU.max).then_inc(anr, 1)
                    else:
                        F(v.tensor_reduce(ktmp, row, AX, ALU.max))
                        v.tensor_tensor(ntab[0:1, 64 + h:64 + h + 1],
                                        ntab[0:1, 64 + h:64 + h + 1], ktmp,
                                        ALU.max).then_inc(anr, 1)
                for kc in range(32):
                    gk = h * 32 + kc
                    v.wait_ge(avt, gk + 1)
                    tp = bank_bf(6 + (kc % 2), 128)
                    v.tensor_copy(vfb[:, kc * 128:(kc + 1) * 128], tp).then_inc(avc, 1)
            v.nop().then_inc(pb2a, 1)
            v.wait_ge(pb2a, 5)

            for h in range(H):
                ins = v.tensor_scalar(ztab[0:1, 4 * h:4 * h + 4],
                                      ntab[0:1, 4 * h:4 * h + 4],
                                      ntab[0:1, 64 + h:64 + h + 1], None, ALU.mult)
            ins.then_inc(bz1, 1)
            v.wait_ge(bz2, 1)
            v.tensor_scalar_mul(nbtab, ztab, -1.02).then_inc(bz2, 1)
            for gj in range(64):
                h, j = gj // 4, gj % 4
                v.wait_ge(bt2, gj + 1)
                v.tensor_copy(col(33), PS[:, 512 * 7:512 * 7 + 1]).then_inc(bv2, 1)
                v.wait_ge(bpv, gj + 1)
                F(v.tensor_copy(lrow, PS[0:1, 512 * 7:512 * 7 + 512]))
                F(v.tensor_scalar_max(lrow, lrow, 1e-30))
                v.reciprocal(rrow, lrow).then_inc(blv, 1)
                v.wait_ge(bt3, gj + 1)
                v.tensor_copy(r_bc, bank(7)).then_inc(bv3, 1)
                v.tensor_tensor(Ctx[:, h * 2048 + j * 512: h * 2048 + (j + 1) * 512],
                                bank(6), r_bc, ALU.mult).then_inc(bcx, 1)
            v.nop().then_inc(pb2b, 1)
            v.wait_ge(pb2b, 5)

            for qt in range(NQT):
                xs = x0b[:, (qt % 2) * 2048:(qt % 2 + 1) * 2048]
                v.wait_ge(c3x2[qt % 2], 16 * (qt // 2 + 1))
                v.reduce_sum(col(0), xs, axis=AX)
                F(v.scalar_tensor_tensor(jnk3, xs, 1.0, xs, ALU.bypass, ALU.mult,
                                         accum_out=col(1)))
                F(v.tensor_scalar_mul(col(2), col(0), 1.0 / D))
                F(v.tensor_tensor(col(46), col(2), col(2), ALU.mult))
                F(v.tensor_scalar(col(3), col(1), 1.0 / D, col(46), ALU.mult, ALU.subtract))
                v.tensor_scalar_add(col(4), col(3), EPS).then_inc(c3c, 1)
                v.wait_ge(c3a, qt + 1)
                al = col(9 + qt % 2)
                F(v.tensor_scalar(col(11 + qt % 2), col(2), al, -1.0, ALU.mult, ALU.mult))
                F(v.tensor_scalar_add(col(13), al, 1.0))
                v.wait_ge(c3m, qt * 4 + 4)
                if qt >= 2:
                    v.wait_ge(c3st2[qt % 2], 16 * (qt // 2))
                pp = qt % 2
                xf = x1f[:, pp * 2048:(pp + 1) * 2048]
                v.scalar_tensor_tensor(xf, xs, col(13), PS[:, pp * 2048:(pp + 1) * 2048],
                                       ALU.mult, ALU.add)
                v.tensor_scalar_add(xf, xf, col(11 + qt % 2)).then_inc(c3e, 1)
            v.nop().then_inc(pb3, 1)
            v.wait_ge(pb3, 5)

            for qt in range(NQT):
                xs = x1rb[:, (qt % 2) * 2048:(qt % 2 + 1) * 2048]
                v.wait_ge(d5x2[qt % 2], 16 * (qt // 2 + 1))
                v.reduce_sum(col(0), xs, axis=AX)
                F(v.scalar_tensor_tensor(jnk35, xs, 1.0, xs, ALU.bypass, ALU.mult,
                                         accum_out=col(1)))
                F(v.tensor_scalar_mul(col(2), col(0), 1.0 / D))
                F(v.tensor_tensor(col(46), col(2), col(2), ALU.mult))
                F(v.tensor_scalar(col(3), col(1), 1.0 / D, col(46), ALU.mult, ALU.subtract))
                v.tensor_scalar_add(col(7), col(3), EPS).then_inc(d5c, 1)
                v.wait_ge(d5a, qt + 1)
                v.tensor_scalar(col(11 + qt % 2), col(2), col(9 + qt % 2), -1.0,
                                ALU.mult, ALU.mult).then_inc(d5c, 1)
                for dc in range(DC):
                    k = qt * DC + dc
                    v.wait_ge(d5t, k + 1)
                    tp = bank_bf(0 + (k % 2), 128)
                    v.tensor_copy(X1nT[:, dc * 2048 + qt * 128: dc * 2048 + (qt + 1) * 128],
                                  tp).then_inc(d5v, 1)
            v.nop().then_inc(pb35, 1)
            v.wait_ge(pb35, 5)

            for g4 in range(4):
                for oc in range(64):
                    gi = g4 * 64 + oc
                    v.wait_ge(e4u, gi + 1)
                    v.wait_ge(e4s, gi + 1)
                    v.tensor_tensor(HT[:, oc * 512:(oc + 1) * 512],
                                    bank(1 + 2 * (gi % 2)),
                                    sgb[:, (gi % 2) * 512:(gi % 2 + 1) * 512],
                                    ALU.mult).then_inc(e4h, 1)
                for p in range(2):
                    for qs in range(2):
                        qt = g4 * 4 + p * 2 + qs
                        v.wait_ge(e4bm, (g4 * 2 + p) * 64 + 64)
                        v.wait_ge(e4x2[qt % 2], 16 * (qt // 2 + 1))
                        if qt >= 2:
                            v.wait_ge(e4o2[qt % 2], 16 * (qt // 2))
                        x2slot = x2fB if (qt % 2 == 0) else x2fC
                        v.scalar_tensor_tensor(
                            x2slot,
                            PS[:, qs * 2048:(qs + 1) * 2048], 1.0,
                            x1r4[:, (qt % 2) * 2048:(qt % 2 + 1) * 2048],
                            ALU.mult, ALU.add).then_inc(e4e, 1)

    return nc


_NC_CACHE = None


def _get_nc():
    global _NC_CACHE
    if _NC_CACHE is None:
        _NC_CACHE = build_program()
    return _NC_CACHE


def _host_tables():
    import ml_dtypes
    rot = 64
    inv_freq = (1.0 / 10000.0 ** (np.arange(0, rot, 2, dtype=np.float32) / rot)).astype(np.float32)
    ang = np.arange(S, dtype=np.float32)[:, None] * inv_freq[None, :]
    ang = np.repeat(ang, 2, axis=-1)
    cos_g = np.cos(ang).T.astype(np.float32)
    sin_g = np.sin(ang).T.astype(np.float32)
    perm = np.zeros((64, 64), dtype=np.float32)
    for i in range(32):
        perm[2 * i + 1, 2 * i] = -1.0
        perm[2 * i, 2 * i + 1] = 1.0
    idb = np.eye(128, dtype=ml_dtypes.bfloat16)
    idf = np.eye(128, dtype=np.float32)
    ones_r = np.ones((1, 128), dtype=np.float32)
    ones_cb = np.ones((128, 1), dtype=ml_dtypes.bfloat16)
    return cos_g, sin_g, perm, idb, idf, ones_r, ones_cb


def kernel(L0_attn_X, L0_attn_ln_w, L0_attn_ln_b, L0_attn_W_in, L0_attn_W_out,
           L0_ffn_w_gate_up, L0_ffn_w_down):
    X = np.ascontiguousarray(np.asarray(L0_attn_X, dtype=np.float32))
    ln_w = np.asarray(L0_attn_ln_w, dtype=np.float32)
    ln_b = np.asarray(L0_attn_ln_b, dtype=np.float32)
    W_in = np.asarray(L0_attn_W_in, dtype=np.float32)
    W_out = np.ascontiguousarray(np.asarray(L0_attn_W_out, dtype=np.float32))
    w_gu = np.asarray(L0_ffn_w_gate_up, dtype=np.float32)
    w_dn = np.asarray(L0_ffn_w_down, dtype=np.float32)

    w_in_t = np.ascontiguousarray(
        W_in.reshape(DC, 128, 48, 128).transpose(2, 1, 0, 3).reshape(48, 128, DC * 128))
    wgu_t = np.ascontiguousarray(
        w_gu.T.reshape(DC, 128, 128, 128).transpose(2, 1, 0, 3).reshape(128, 128, DC * 128))
    wdn_t = np.ascontiguousarray(w_dn.T.reshape(64, 128, D))
    lnw_c = np.ascontiguousarray(ln_w.reshape(DC, 128).T)
    lnb_c = np.ascontiguousarray(ln_b.reshape(DC, 128).T)

    cos_g, sin_g, perm, idb, idf, ones_r, ones_cb = _host_tables()

    nc = _get_nc()
    in_maps = []
    for c in range(8):
        b, half = c // 2, c % 2
        off = half * SQ
        in_maps.append({
            "x": np.roll(X[b], -off, axis=0) if off else X[b],
            "w_in_t": w_in_t,
            "w_out": W_out,
            "wgu_t": wgu_t,
            "wdn_t": wdn_t,
            "lnw_c": lnw_c,
            "lnb_c": lnb_c,
            "cos_t": np.roll(cos_g, -off, axis=1) if off else cos_g,
            "sin_t": np.roll(sin_g, -off, axis=1) if off else sin_g,
            "perm": perm,
            "id_bf": idb,
            "id_f32": idf,
            "ones_r": ones_r,
            "ones_cb": ones_cb,
        })
    res = run_bass_kernel_spmd(nc, in_maps, core_ids=list(range(8)))
    out = np.empty((4, S, D), dtype=np.float32)
    for c in range(8):
        b, half = c // 2, c % 2
        out[b, half * SQ:(half + 1) * SQ, :] = np.asarray(res.results[c]["out"],
                                                          dtype=np.float32)
    return out


# version 04
# speedup vs baseline: 1.3234x; 1.1651x over previous
import numpy as np
from contextlib import ExitStack

import concourse.bass as bass
import concourse.mybir as mybir
from concourse.bass_utils import run_bass_kernel_spmd

F32 = mybir.dt.float32
BF16 = mybir.dt.bfloat16
AF = mybir.ActivationFunctionType
ALU = mybir.AluOpType
AX = mybir.AxisListType.X

D = 2048
S = 4096
SQ = 2048
H = 16
DH = 128
DC = 16
INTER = 8192
NT = S // 128
NQT = SQ // 128
QSCALE = 1.0 / float(np.sqrt(128.0))
EPS = 1e-5

_DEBUG_TAPS = []


def build_program(debug=False):
    nc = bass.Bass("TRN2", target_bir_lowering=False)
    SCR = "ExternalOutput" if debug else "Internal"

    x_in = nc.dram_tensor("x", [S, D], F32, kind="ExternalInput").ap()
    w_in_t = nc.dram_tensor("w_in_t", [48, 128, DC * 128], F32, kind="ExternalInput").ap()
    w_out_in = nc.dram_tensor("w_out", [D, D], F32, kind="ExternalInput").ap()
    wgu_t = nc.dram_tensor("wgu_t", [128, 128, DC * 128], F32, kind="ExternalInput").ap()
    wdn_t = nc.dram_tensor("wdn_t", [64, 128, D], F32, kind="ExternalInput").ap()
    lnw_in = nc.dram_tensor("lnw_c", [128, DC], F32, kind="ExternalInput").ap()
    lnb_in = nc.dram_tensor("lnb_c", [128, DC], F32, kind="ExternalInput").ap()
    cos_in = nc.dram_tensor("cos_t", [64, S], F32, kind="ExternalInput").ap()
    sin_in = nc.dram_tensor("sin_t", [64, S], F32, kind="ExternalInput").ap()
    perm_in = nc.dram_tensor("perm", [64, 64], F32, kind="ExternalInput").ap()
    idb_in = nc.dram_tensor("id_bf", [128, 128], BF16, kind="ExternalInput").ap()
    idf_in = nc.dram_tensor("id_f32", [128, 128], F32, kind="ExternalInput").ap()
    onesr_in = nc.dram_tensor("ones_r", [1, 128], F32, kind="ExternalInput").ap()
    onesc_in = nc.dram_tensor("ones_cb", [128, 1], BF16, kind="ExternalInput").ap()

    out_t = nc.dram_tensor("out", [SQ, D], F32, kind="ExternalOutput").ap()

    q_scr = nc.dram_tensor("q_scr", [H, 128, SQ], BF16, kind=SCR).ap()
    k_scr = nc.dram_tensor("k_scr", [H, 128, S], BF16, kind=SCR).ap()
    vt_scr = nc.dram_tensor("vt_scr", [H, 128, S], BF16, kind=SCR).ap()
    x1_scr = nc.dram_tensor("x1_scr", [NQT, 128, D], F32, kind=SCR).ap()
    wgu_bf = nc.dram_tensor("wgu_bf", [128, 128, DC * 128], BF16, kind="Internal").ap()
    wdn_bf = nc.dram_tensor("wdn_bf", [64, 128, D], BF16, kind="Internal").ap()

    with ExitStack() as ctx:
        E = ctx.enter_context

        A = E(nc.sbuf_tensor("arA", [128, 65536], BF16)).ap()
        B = E(nc.sbuf_tensor("arB", [128, 28672], BF16)).ap()
        C = E(nc.sbuf_tensor("arC", [128, 4096], F32)).ap()
        COLS = E(nc.sbuf_tensor("cols", [128, 96], F32)).ap()
        perm_s = E(nc.sbuf_tensor("perm_s", [64, 64], F32)).ap()
        idb = E(nc.sbuf_tensor("idb", [128, 128], BF16)).ap()
        idf = E(nc.sbuf_tensor("idf", [128, 128], F32)).ap()
        ones_r = E(nc.sbuf_tensor("ones_r_s", [1, 128], F32)).ap()
        ones_cb = E(nc.sbuf_tensor("ones_cb_s", [128, 1], BF16)).ap()
        lnw_s = E(nc.sbuf_tensor("lnw_s", [128, DC], F32)).ap()
        ntab = E(nc.sbuf_tensor("ntab", [1, 128], F32)).ap()
        lnbb = E(nc.sbuf_tensor("lnbb", [128, DC], BF16)).ap()

        PS = E(nc.psum_tensor("ps", [128, 4096], F32)).ap()

        def bank(i, n=512):
            return PS[:, 512 * i: 512 * i + n]

        def bank_bf(i, cols=128, off=0):
            return PS[:, 512 * i + off // 2: 512 * i + (off + cols) // 2].bitcast(BF16)

        XnT = A
        Ctx = A[:, 0:32768]
        P_T = A[:, 32768:49152]
        WOB = A[:, 32768:65536]
        X1nT = A[:, 0:32768]
        HT = A[:, 32768:65536]

        xb = B[:, 0:12288].bitcast(F32)
        ub = B[:, 12288:16384]
        jnk1 = B[:, 16384:20480].bitcast(F32)

        wbuf = B[:, 0:12288]
        qbuf = B[:, 12288:14336]
        kbuf = B[:, 14336:18432]
        vfb = B[:, 18432:22528]
        xr0 = B[:, 22528:23552].bitcast(F32)
        xr1 = B[:, 23552:24576].bitcast(F32)
        t1b = B[:, 24576:25600].bitcast(F32)
        t2b = B[:, 25600:26624].bitcast(F32)
        sqst = B[:, 26624:27648]

        def qh(s):
            return B[:, s * 10240: s * 10240 + 2048]

        def kh(s):
            return B[:, s * 10240 + 2048: s * 10240 + 6144]

        def vth(s):
            return B[:, s * 10240 + 6144: s * 10240 + 10240]

        x0b = B[:, 0:8192].bitcast(F32)
        x1f = B[:, 8192:16384].bitcast(F32)
        jnk3 = B[:, 16384:20480].bitcast(F32)

        x1rb = B[:, 0:8192].bitcast(F32)
        u2b = B[:, 8192:12288]
        jnk35 = B[:, 12288:16384].bitcast(F32)

        wgb = B[:, 0:4096]
        wub = B[:, 4096:8192]
        wdb = B[:, 8192:16384]
        x1r4 = B[:, 16384:24576].bitcast(F32)
        x2fB = B[:, 24576:28672].bitcast(F32)

        cosb = C[:, 0:2048].bitcast(BF16)
        sinb = C[:, 2048:4096].bitcast(BF16)
        r_bc = C[:, 0:512]
        lrow = C[0:1, 512:1024]
        rrow = C[0:1, 1024:1536]
        nmrow = C[0:1, 1536:1664]
        ncsc = C[0:1, 1664:1665]
        ktmp = COLS[0:1, 47:48]
        ztab = C[0:1, 1696:1760]
        nbtab = C[0:1, 1760:1824]
        sgb = C[:, 0:1024]
        x2fC = C[:, 2048:4096]

        def col(i, n=1):
            return COLS[:, i:i + n]

        def sem(name):
            return E(nc.semaphore(name))

        pb1, pb2a, pb2b, pb3, pb35 = sem("pb1"), sem("pb2a"), sem("pb2b"), sem("pb3"), sem("pb35")
        x1q, x1c, x1a, x1u, x1t, x1v = (sem(n) for n in
            ("x1q", "x1c", "x1a", "x1u", "x1t", "x1v"))
        xl = [sem(f"xl{i}") for i in range(3)]
        ahd, apb, avb, apq, aev, art, arp, avt, avc, acache = (sem(n) for n in
            ("ahd", "apb", "avb", "apq", "aev", "art", "arp", "avt", "avc", "acache"))
        aw2 = [sem(f"aw{i}") for i in range(2)]
        aqsq, aqsk, aqsv = sem("aqsq"), sem("aqsk"), sem("aqsv")
        b1m, b1x, b1q, bnm, bt1, bv1, bt2, bv2, b2m, bpt, bpv, blv, bt3, bv3, bcx, bhd = (
            sem(n) for n in ("b1m", "b1x", "b1q", "bnm", "bt1", "bv1", "bt2",
                             "bv2", "b2m", "bpt", "bpv", "blv", "bt3", "bv3", "bcx", "bhd"))
        bql2 = [sem(f"bql{i}") for i in range(2)]
        c3w, c3q, c3c, c3a, c3g, c3m, c3e = (sem(n) for n in
            ("c3w", "c3q", "c3c", "c3a", "c3g", "c3m", "c3e"))
        c3x2 = [sem(f"c3x{i}") for i in range(2)]
        c3st2 = [sem(f"c3st{i}") for i in range(2)]
        d5q, d5c, d5a, d5u, d5t, d5v = (sem(n) for n in
            ("d5q", "d5c", "d5a", "d5u", "d5t", "d5v"))
        d5x2 = [sem(f"d5x{i}") for i in range(2)]
        e4g, e4u, e4s, e4h, e4bm, e4e = (sem(n) for n in
            ("e4g", "e4u", "e4s", "e4h", "e4bm", "e4e"))
        aqn, anm, anr, bz1, bz2 = (sem(n) for n in ("aqn", "anm", "anr", "bz1", "bz2"))
        e4w2 = [sem(f"e4w{i}") for i in range(2)]
        e4d2 = [sem(f"e4d{i}") for i in range(2)]
        e4x2 = [sem(f"e4x{i}") for i in range(2)]
        e4o2 = [sem(f"e4o{i}") for i in range(2)]
        s_const = sem("s_const")
        s_constg = sem("s_constg")

        block = E(nc.Block())

        CH = []
        for r, n in ((0, 4), (1, 8), (2, 8)):
            for c in range(n):
                CH.append((r, c, n))
        NCH = len(CH)
        NQK = 12

        def fcol(h, r):
            return r * 16 + h

        @block.sync
        def _(s):
            s.dma_start(out=perm_s, in_=perm_in).then_inc(s_const, 16)
            s.dma_start(out=idb, in_=idb_in).then_inc(s_const, 16)
            s.dma_start(out=idf, in_=idf_in).then_inc(s_const, 16)
            s.dma_start(out=ones_r, in_=onesr_in).then_inc(s_const, 16)
            s.dma_start(out=ones_cb, in_=onesc_in).then_inc(s_const, 16)
            s.dma_start(out=lnw_s, in_=lnw_in).then_inc(s_const, 16)

            for i in range(NT):
                if i >= 3:
                    s.wait_ge(x1u, i - 2)
                s.dma_start(out=xb[:, (i % 3) * 2048:(i % 3 + 1) * 2048],
                            in_=x_in[i * 128:(i + 1) * 128, :]).then_inc(xl[i % 3], 16)

            s.nop().then_inc(pb1, 1)
            s.wait_ge(pb1, 5)
            for h in range(H):
                s.wait_ge(arp, h * NQK + 4)
                s.wait_ge(aev, h * NCH + 4)
                s.dma_start(out=q_scr[h], in_=qbuf).then_inc(aqsq, 16)
                s.wait_ge(arp, h * NQK + 12)
                s.wait_ge(aev, h * NCH + 12)
                s.dma_start(out=k_scr[h], in_=kbuf).then_inc(aqsk, 16)
                for g in range(8):
                    s.wait_ge(avc, h * 32 + 4 * (g + 1))
                    s.dma_start(out=vt_scr[h][:, g * 512:(g + 1) * 512],
                                in_=vfb[:, g * 512:(g + 1) * 512]).then_inc(aqsv, 16)

            s.nop().then_inc(pb2a, 1)
            s.wait_ge(pb2a, 5)
            s.nop().then_inc(pb2b, 1)
            s.wait_ge(pb2b, 5)

            s.wait_ge(bpv, 64)
            for qt in range(NQT):
                if qt >= 2:
                    s.wait_ge(c3e, qt - 1)
                s.dma_start(out=x0b[:, (qt % 2) * 2048:(qt % 2 + 1) * 2048],
                            in_=x_in[qt * 128:(qt + 1) * 128, :]).then_inc(c3x2[qt % 2], 16)
                s.wait_ge(c3e, qt + 1)
                s.dma_start(out=x1_scr[qt],
                            in_=x1f[:, (qt % 2) * 2048:(qt % 2 + 1) * 2048]).then_inc(c3st2[qt % 2], 16)

            s.nop().then_inc(pb3, 1)
            s.wait_ge(pb3, 5)

            s.wait_ge(c3st2[0], 16 * 8)
            s.wait_ge(c3st2[1], 16 * 8)
            for qt in range(NQT):
                if qt >= 2:
                    s.wait_ge(d5u, qt - 1)
                s.dma_start(out=x1rb[:, (qt % 2) * 2048:(qt % 2 + 1) * 2048],
                            in_=x1_scr[qt]).then_inc(d5x2[qt % 2], 16)

            s.nop().then_inc(pb35, 1)
            s.wait_ge(pb35, 5)

            s.wait_ge(d5u, NQT)
            for g in range(4):
                for oc in range(64):
                    gi = g * 64 + oc
                    if gi >= 2:
                        s.wait_ge(e4g, gi - 1)
                        s.wait_ge(e4u, gi - 1)
                    s.wait_ge(acache, 16 * 24)
                    s.dma_start(out=wgb[:, (gi % 2) * 2048:(gi % 2 + 1) * 2048],
                                in_=wgu_bf[oc]).then_inc(e4w2[gi % 2], 16)
                    s.dma_start(out=wub[:, (gi % 2) * 2048:(gi % 2 + 1) * 2048],
                                in_=wgu_bf[64 + oc]).then_inc(e4w2[gi % 2], 16)
                for p in range(2):
                    for qs in range(2):
                        qt = g * 4 + p * 2 + qs
                        if qt == 0:
                            s.wait_ge(c3st2[0], 16 * 8)
                            s.wait_ge(c3st2[1], 16 * 8)
                        if qt >= 2:
                            s.wait_ge(e4e, qt - 1)
                        s.dma_start(out=x1r4[:, (qt % 2) * 2048:(qt % 2 + 1) * 2048],
                                    in_=x1_scr[qt]).then_inc(e4x2[qt % 2], 16)
                        s.wait_ge(e4e, qt + 1)
                        s.dma_start(out=out_t[qt * 128:(qt + 1) * 128, :],
                                    in_=(x2fB if (qt % 2 == 0) else x2fC.copy())
                                    ).then_inc(e4o2[qt % 2], 16)
            s.wait_ge(e4o2[0], 16 * 8)
            s.wait_ge(e4o2[1], 16 * 8)

        @block.gpsimd
        def _(g):
            g.dma_start(out=lnbb, in_=lnb_in).then_inc(s_constg, 16)
            g.dma_start(out=cosb[0:64, :], in_=cos_in).then_inc(s_constg, 16)
            g.dma_start(out=sinb[0:64, :], in_=sin_in).then_inc(s_constg, 16)
            g.nop().then_inc(pb1, 1)
            g.wait_ge(pb1, 5)
            g.wait_ge(x1u, NT)
            g.wait_ge(x1c, 2 * NT)

            cache_jobs = []
            for i in range(16):
                cache_jobs.append((wgu_bf[i * 8:(i + 1) * 8], wgu_t[i * 8:(i + 1) * 8]))
            for i in range(8):
                cache_jobs.append((wdn_bf[i * 8:(i + 1) * 8], wdn_t[i * 8:(i + 1) * 8]))
            cj = 0
            for h in range(H):
                if h >= 2:
                    g.wait_ge(apq, (h - 1) * NCH)
                for r in range(3):
                    g.dma_start(out=wbuf[:, ((h % 2) * 3 + r) * 2048:((h % 2) * 3 + r + 1) * 2048],
                                in_=w_in_t[fcol(h, r)]).then_inc(aw2[h % 2], 16)
                for _ in range(2):
                    if cj < len(cache_jobs):
                        dst, src = cache_jobs[cj]
                        g.dma_start(out=dst, in_=src).then_inc(acache, 16)
                        cj += 1
            while cj < len(cache_jobs):
                dst, src = cache_jobs[cj]
                g.dma_start(out=dst, in_=src).then_inc(acache, 16)
                cj += 1

            g.nop().then_inc(pb2a, 1)
            g.wait_ge(pb2a, 5)
            g.nop().then_inc(pb2b, 1)
            g.wait_ge(pb2b, 5)

            g.wait_ge(bpv, 64)
            for dc in range(DC):
                g.dma_start(out=WOB[:, dc * 2048:(dc + 1) * 2048],
                            in_=w_out_in[dc * 128:(dc + 1) * 128, :]).then_inc(c3w, 16)
            g.nop().then_inc(pb3, 1)
            g.wait_ge(pb3, 5)
            g.nop().then_inc(pb35, 1)
            g.wait_ge(pb35, 5)

            g.wait_ge(d5t, NQT * DC)
            for gi in range(4):
                for p in range(2):
                    for oc in range(64):
                        di = (gi * 2 + p) * 64 + oc
                        if di >= 4:
                            g.wait_ge(e4bm, di - 3)
                        g.wait_ge(acache, 16 * 24)
                        g.dma_start(out=wdb[:, (di % 4) * 2048:(di % 4 + 1) * 2048],
                                    in_=wdn_bf[oc]).then_inc(e4d2[di % 2], 16)

        @block.tensor
        def _(t):
            t.wait_ge(s_const, 16 * 6)
            t.wait_ge(s_constg, 16 * 3)
            for i in range(NT):
                t.wait_ge(x1u, i + 1)
                for dc in range(DC):
                    k = i * DC + dc
                    if k >= 2:
                        t.wait_ge(x1v, k - 1)
                    tp = bank_bf(6 + (k % 2), 128)
                    usl = ub[:, (i % 2) * 2048 + dc * 128:(i % 2) * 2048 + (dc + 1) * 128]
                    t.transpose(tp, usl, idb).then_inc(x1t, 1)
            t.nop().then_inc(pb1, 1)
            t.wait_ge(pb1, 5)
            t.wait_ge(x1v, 512)

            for h in range(H):
                t.wait_ge(aw2[h % 2], 16 * 3 * (h // 2 + 1))
                for r in range(3):
                    pb_ = PS[:, 512 * 6:512 * 6 + 1]
                    if h > 0 or r > 0:
                        t.wait_ge(avb, h * 3 + r)
                    if h > 0 and r == 0:
                        t.wait_ge(avc, 32 * h)
                    ws = wbuf[:, ((h % 2) * 3 + r) * 2048:((h % 2) * 3 + r + 1) * 2048]
                    for dc in range(DC):
                        ins = t.matmul(pb_, ws[:, dc * 128:(dc + 1) * 128], lnbb[:, dc:dc + 1],
                                       start=(dc == 0), stop=(dc == DC - 1))
                        if dc == DC - 1:
                            ins.then_inc(apb, 1)

                rope_pending = []
                for ci, (r, c, n) in enumerate(CH):
                    gcid = h * NCH + ci
                    slot = gcid % 4
                    if gcid >= 4:
                        t.wait_ge(aev, gcid - 3)
                    ws = wbuf[:, ((h % 2) * 3 + r) * 2048:((h % 2) * 3 + r + 1) * 2048]
                    dst = bank(slot)
                    toff = c * 512 if r == 0 else c * 512
                    for dc in range(DC):
                        ins = t.matmul(dst, ws[:, dc * 128:(dc + 1) * 128],
                                       XnT[:, dc * 4096 + toff:dc * 4096 + toff + 512],
                                       start=(dc == 0), stop=(dc == DC - 1))
                        if dc == DC - 1:
                            ins.then_inc(apq, 1)
                    if rope_pending:
                        rj, rslot = rope_pending.pop(0)
                        t.wait_ge(aev, rj + 1)
                        if rslot == 0 and h > 0:
                            t.wait_ge(anr, 12 * h)
                        if (h * NQK + rslot) >= 2:
                            t.wait_ge(arp, h * NQK + rslot - 1)
                        xr = xr0 if (rslot % 2 == 0) else xr1
                        t.matmul(PS[0:64, 512 * (4 + rslot % 2):512 * (4 + rslot % 2) + 512],
                                 perm_s, xr[0:64, :], start=True, stop=True).then_inc(art, 1)
                    if r in (0, 1):
                        rqk = c if r == 0 else 4 + c
                        rope_pending.append((gcid, rqk))
                while rope_pending:
                    rj, rslot = rope_pending.pop(0)
                    t.wait_ge(aev, rj + 1)
                    if (h * NQK + rslot) >= 2:
                        t.wait_ge(arp, h * NQK + rslot - 1)
                    xr = xr0 if (rslot % 2 == 0) else xr1
                    t.matmul(PS[0:64, 512 * (4 + rslot % 2):512 * (4 + rslot % 2) + 512],
                             perm_s, xr[0:64, :], start=True, stop=True).then_inc(art, 1)

                t.wait_ge(arp, h * NQK + NQK)
                for c in range(12):
                    t.wait_ge(aqn, h * 12 + c + 1)
                    if h * 12 + c >= 2:
                        t.wait_ge(anr, h * 12 + c - 1)
                    t.matmul(PS[0:1, 512 * (4 + c % 2):512 * (4 + c % 2) + 512],
                             ones_cb, sqst[:, (c % 2) * 512:(c % 2 + 1) * 512],
                             start=True, stop=True).then_inc(anm, 1)
                t.wait_ge(avb, h * 3 + 3)
                for kc in range(32):
                    t.wait_ge(aev, h * NCH + 12 + (kc // 4) + 1)
                    gk = h * 32 + kc
                    if gk >= 2:
                        t.wait_ge(avc, gk - 1)
                    tp = bank_bf(6 + (kc % 2), 128)
                    t.transpose(tp, vfb[:, kc * 128:(kc + 1) * 128], idb).then_inc(avt, 1)

            t.nop().then_inc(pb2a, 1)
            t.wait_ge(pb2a, 5)

            t.wait_ge(aev, H * NCH)
            t.wait_ge(avc, H * 32)
            t.wait_ge(arp, H * NQK)
            t.wait_ge(anr, H * 12)
            for gj in range(64):
                h, j = gj // 4, gj % 4
                sl = h % 2
                if j == 0:
                    t.wait_ge(bql2[h % 2], 16 * 3 * (h // 2 + 1))
                if gj == 0:
                    t.wait_ge(bz2, 2)
                else:
                    t.wait_ge(bv3, gj)
                t.matmul(PS[:, 512 * 7:512 * 7 + 1], ones_r, nbtab[0:1, gj:gj + 1],
                         start=True, stop=True).then_inc(bt2, 1)
                for g in range(16):
                    if gj * 16 + g >= 3:
                        t.wait_ge(bpt, gj * 16 + g - 2)
                    pb_ = (g % 3) * 2
                    for half in range(2):
                        kc = 2 * g + half
                        t.matmul(PS[:, (pb_ + half) * 512:(pb_ + half) * 512 + 512],
                                 kh(sl)[:, kc * 128:(kc + 1) * 128],
                                 qh(sl)[:, j * 512:(j + 1) * 512],
                                 start=True, stop=True).then_inc(b2m, 1)
                    if g >= 1:
                        t.wait_ge(bpt, gj * 16 + g)
                        for kc in (2 * g - 2, 2 * g - 1):
                            if kc == 0:
                                t.wait_ge(bcx, gj)
                                t.wait_ge(bv2, gj + 1)
                            t.matmul(bank(6), vth(sl)[:, kc * 128:(kc + 1) * 128],
                                     P_T[:, kc * 512:(kc + 1) * 512],
                                     start=(kc == 0), stop=False)
                            t.matmul(PS[0:1, 512 * 7:512 * 7 + 512], ones_cb,
                                     P_T[:, kc * 512:(kc + 1) * 512],
                                     start=(kc == 0), stop=False)
                t.wait_ge(bpt, gj * 16 + 16)
                for kc in (30, 31):
                    t.matmul(bank(6), vth(sl)[:, kc * 128:(kc + 1) * 128],
                             P_T[:, kc * 512:(kc + 1) * 512],
                             start=False, stop=(kc == 31))
                    ins = t.matmul(PS[0:1, 512 * 7:512 * 7 + 512], ones_cb,
                                   P_T[:, kc * 512:(kc + 1) * 512],
                                   start=False, stop=(kc == 31))
                    if kc == 31:
                        ins.then_inc(bpv, 1)
                t.wait_ge(blv, gj + 1)
                t.matmul(bank(7), ones_r, rrow, start=True, stop=True).then_inc(bt3, 1)

            t.nop().then_inc(pb2b, 1)
            t.wait_ge(pb2b, 5)

            t.wait_ge(bcx, 64)
            t.wait_ge(bv3, 64)
            t.wait_ge(bpt, 1024)
            t.wait_ge(c3w, 16 * DC)
            for qt in range(NQT):
                pp = qt % 2
                if qt >= 2:
                    t.wait_ge(c3e, qt - 1)
                for oc in range(4):
                    dst = PS[:, pp * 2048 + oc * 512: pp * 2048 + oc * 512 + 512]
                    for dcc in range(DC):
                        ins = t.matmul(dst,
                                       Ctx[:, dcc * 2048 + qt * 128: dcc * 2048 + (qt + 1) * 128],
                                       WOB[:, dcc * 2048 + oc * 512: dcc * 2048 + oc * 512 + 512],
                                       start=(dcc == 0), stop=(dcc == DC - 1))
                        if dcc == DC - 1:
                            ins.then_inc(c3m, 1)
            t.nop().then_inc(pb3, 1)
            t.wait_ge(pb3, 5)

            t.wait_ge(c3e, NQT)
            for qt in range(NQT):
                t.wait_ge(d5u, qt + 1)
                for dc in range(DC):
                    k = qt * DC + dc
                    if k >= 2:
                        t.wait_ge(d5v, k - 1)
                    tp = bank_bf(0 + (k % 2), 128)
                    usl = u2b[:, (qt % 2) * 2048 + dc * 128:(qt % 2) * 2048 + (dc + 1) * 128]
                    t.transpose(tp, usl, idb).then_inc(d5t, 1)
            t.nop().then_inc(pb35, 1)
            t.wait_ge(pb35, 5)

            t.wait_ge(d5v, NQT * DC)
            for g4 in range(4):
                if g4 > 0:
                    t.wait_ge(e4e, g4 * 4)
                for oc in range(64):
                    gi = g4 * 64 + oc
                    t.wait_ge(e4w2[gi % 2], 16 * 2 * (gi // 2 + 1))
                    if gi >= 2:
                        t.wait_ge(e4s, gi - 1)
                        t.wait_ge(e4h, gi - 1)
                    pg = bank(0 + 2 * (gi % 2))
                    pu = bank(1 + 2 * (gi % 2))
                    for hc in range(DC):
                        ins = t.matmul(pg, wgb[:, (gi % 2) * 2048 + hc * 128:(gi % 2) * 2048 + (hc + 1) * 128],
                                       X1nT[:, hc * 2048 + g4 * 512:hc * 2048 + (g4 + 1) * 512],
                                       start=(hc == 0), stop=(hc == DC - 1))
                        if hc == DC - 1:
                            ins.then_inc(e4g, 1)
                    for hc in range(DC):
                        ins = t.matmul(pu, wub[:, (gi % 2) * 2048 + hc * 128:(gi % 2) * 2048 + (hc + 1) * 128],
                                       X1nT[:, hc * 2048 + g4 * 512:hc * 2048 + (g4 + 1) * 512],
                                       start=(hc == 0), stop=(hc == DC - 1))
                        if hc == DC - 1:
                            ins.then_inc(e4u, 1)
                t.wait_ge(e4h, g4 * 64 + 64)
                for p in range(2):
                    if p == 1:
                        t.wait_ge(e4e, g4 * 4 + 2)
                    for oc in range(64):
                        di = (g4 * 2 + p) * 64 + oc
                        t.wait_ge(e4d2[di % 2], 16 * (di // 2 + 1))
                        for qs in range(2):
                            qloc = p * 2 + qs
                            for ho in range(4):
                                ins = t.matmul(PS[:, qs * 2048 + ho * 512: qs * 2048 + ho * 512 + 512],
                                               HT[:, oc * 512 + qloc * 128: oc * 512 + qloc * 128 + 128],
                                               wdb[:, (di % 4) * 2048 + ho * 512:(di % 4) * 2048 + ho * 512 + 512],
                                               start=(oc == 0), stop=(oc == 63))
                                if qs == 1 and ho == 3:
                                    ins.then_inc(e4bm, 1)

        @block.scalar
        def _(sc):
            sc.wait_ge(s_const, 16 * 6)
            sc.wait_ge(s_constg, 16 * 3)
            for i in range(NT):
                xs = xb[:, (i % 3) * 2048:(i % 3 + 1) * 2048]
                sc.wait_ge(xl[i % 3], 16 * (i // 3 + 1))
                sc.wait_ge(x1c, 2 * i + 1)
                sc.activation(col(8), col(7), AF.Ln)
                if i < NQT:
                    sc.activation(col(9 + i % 2), col(8), AF.Exp, scale=-0.5)
                    sc.activation(col(14), col(4), AF.Ln)
                    sc.activation(col(15), col(14), AF.Exp, scale=-0.5).then_inc(x1a, 1)
                else:
                    sc.activation(col(9 + i % 2), col(8), AF.Exp, scale=-0.5
                                  ).then_inc(x1a, 1)
                sc.wait_ge(x1c, 2 * i + 2)
                if i >= 2:
                    sc.wait_ge(x1t, 16 * (i - 1))
                sc.activation(ub[:, (i % 2) * 2048:(i % 2 + 1) * 2048], xs, AF.Identity,
                              bias=col(11 + i % 2), scale=col(9 + i % 2)).then_inc(x1u, 1)
            sc.nop().then_inc(pb1, 1)
            sc.wait_ge(pb1, 5)

            for h in range(H):
                for ci, (r, c, n) in enumerate(CH):
                    gcid = h * NCH + ci
                    slot = gcid % 4
                    sc.wait_ge(apq, gcid + 1)
                    sc.wait_ge(avb, h * 3 + r + 1)
                    bcol = col(16 + (h % 2) * 3 + r)
                    scl = QSCALE if r == 0 else 1.0
                    if r == 0:
                        if h > 0 and c == 0:
                            sc.wait_ge(aqsq, 16 * h)
                        rqk = c
                        if h * NQK + rqk >= 2:
                            sc.wait_ge(arp, h * NQK + rqk - 1)
                        sc.activation(qbuf[:, c * 512:(c + 1) * 512], bank(slot),
                                      AF.Identity, bias=bcol, scale=scl)
                        sc.activation(xr0[0:64, :] if (c % 2 == 0) else xr1[0:64, :],
                                      bank(slot)[0:64, :], AF.Identity,
                                      bias=bcol[0:64, :], scale=scl).then_inc(aev, 1)
                    elif r == 1:
                        if h > 0 and c == 0:
                            sc.wait_ge(aqsk, 16 * h)
                        rqk = 4 + c
                        if h * NQK + rqk >= 2:
                            sc.wait_ge(arp, h * NQK + rqk - 1)
                        sc.activation(kbuf[:, c * 512:(c + 1) * 512], bank(slot),
                                      AF.Identity, bias=bcol, scale=scl)
                        sc.activation(xr0[0:64, :] if (rqk % 2 == 0) else xr1[0:64, :],
                                      bank(slot)[0:64, :], AF.Identity,
                                      bias=bcol[0:64, :], scale=scl).then_inc(aev, 1)
                    else:
                        if h > 0 and c == 0:
                            sc.wait_ge(avt, 32 * h)
                            sc.wait_ge(aqsv, 16 * 8 * h)
                        sc.activation(vfb[:, c * 512:(c + 1) * 512], bank(slot),
                                      AF.Identity, bias=bcol, scale=1.0).then_inc(aev, 1)
                for c in range(12):
                    sc.wait_ge(arp, h * NQK + c + 1)
                    if h * 12 + c >= 2:
                        sc.wait_ge(anm, h * 12 + c - 1)
                    srcap = (qbuf[:, c * 512:(c + 1) * 512] if c < 4
                             else kbuf[:, (c - 4) * 512:(c - 3) * 512])
                    sc.activation(sqst[:, (c % 2) * 512:(c % 2 + 1) * 512], srcap,
                                  AF.Square).then_inc(aqn, 1)
            sc.nop().then_inc(pb2a, 1)
            sc.wait_ge(pb2a, 5)

            sc.wait_ge(aqsq, 16 * 16)
            sc.wait_ge(aqsk, 16 * 16)
            sc.wait_ge(aqsv, 16 * 128)
            for h in range(2):
                sc.dma_start(out=qh(h % 2), in_=q_scr[h]).then_inc(bql2[h % 2], 16)
                sc.dma_start(out=kh(h % 2), in_=k_scr[h]).then_inc(bql2[h % 2], 16)
                sc.dma_start(out=vth(h % 2), in_=vt_scr[h]).then_inc(bql2[h % 2], 16)
            sc.wait_ge(bz1, 1)
            sc.activation(ztab, ztab, AF.Ln)
            sc.activation(ztab, ztab, AF.Exp, scale=0.5).then_inc(bz2, 1)
            for h in range(H):
                for j in range(4):
                    gj = h * 4 + j
                    sc.wait_ge(bv2, gj + 1)
                    if gj > 0:
                        sc.wait_ge(bpv, gj)
                    for g in range(16):
                        sc.wait_ge(b2m, gj * 32 + 2 * g + 2)
                        pb_ = (g % 3) * 2
                        sc.activation(P_T[:, g * 1024:(g + 1) * 1024],
                                      PS[:, pb_ * 512:pb_ * 512 + 1024],
                                      AF.Exp, bias=col(33)).then_inc(bpt, 1)
                if h + 2 < H:
                    sc.wait_ge(bcx, 4 * (h + 1))
                    sc.dma_start(out=qh((h + 2) % 2), in_=q_scr[h + 2]).then_inc(bql2[h % 2], 16)
                    sc.dma_start(out=kh((h + 2) % 2), in_=k_scr[h + 2]).then_inc(bql2[h % 2], 16)
                    sc.dma_start(out=vth((h + 2) % 2), in_=vt_scr[h + 2]).then_inc(bql2[h % 2], 16)
            sc.nop().then_inc(pb2b, 1)
            sc.wait_ge(pb2b, 5)

            sc.nop().then_inc(pb3, 1)
            sc.wait_ge(pb3, 5)

            sc.wait_ge(c3st2[0], 16 * 8)
            sc.wait_ge(c3st2[1], 16 * 8)
            for qt in range(NQT):
                xs = x1rb[:, (qt % 2) * 2048:(qt % 2 + 1) * 2048]
                sc.wait_ge(d5x2[qt % 2], 16 * (qt // 2 + 1))
                sc.wait_ge(d5c, 2 * qt + 1)
                sc.activation(col(8), col(7), AF.Ln)
                sc.activation(col(9 + qt % 2), col(8), AF.Exp, scale=-0.5).then_inc(d5a, 1)
                sc.wait_ge(d5c, 2 * qt + 2)
                if qt >= 2:
                    sc.wait_ge(d5t, 16 * (qt - 1))
                sc.activation(u2b[:, (qt % 2) * 2048:(qt % 2 + 1) * 2048], xs, AF.Identity,
                              bias=col(11 + qt % 2), scale=col(9 + qt % 2)).then_inc(d5u, 1)
            sc.nop().then_inc(pb35, 1)
            sc.wait_ge(pb35, 5)

            for g4 in range(4):
                for oc in range(64):
                    gi = g4 * 64 + oc
                    sc.wait_ge(e4g, gi + 1)
                    if gi >= 2:
                        sc.wait_ge(e4h, gi - 1)
                    sc.activation(sgb[:, (gi % 2) * 512:(gi % 2 + 1) * 512],
                                  bank(0 + 2 * (gi % 2)), AF.Silu).then_inc(e4s, 1)

        fsem = sem("fence_v")
        _fc = [0]

        @block.vector
        def _(v):
            def F(ins):
                _fc[0] += 1
                ins.then_inc(fsem, 1)
                v.wait_ge(fsem, _fc[0])
            v.wait_ge(s_const, 16 * 6)
            v.wait_ge(s_constg, 16 * 3)
            for i in range(NT):
                xs = xb[:, (i % 3) * 2048:(i % 3 + 1) * 2048]
                v.wait_ge(xl[i % 3], 16 * (i // 3 + 1))
                v.reduce_sum(col(0), xs, axis=AX)
                F(v.scalar_tensor_tensor(jnk1, xs, 1.0, xs, ALU.bypass, ALU.mult,
                                         accum_out=col(1)))
                F(v.tensor_scalar_mul(col(2), col(0), 1.0 / D))
                F(v.tensor_tensor(col(46), col(2), col(2), ALU.mult))
                F(v.tensor_scalar(col(3), col(1), 1.0 / D, col(46), ALU.mult, ALU.subtract))
                F(v.tensor_scalar_add(col(4), col(3), EPS))
                F(v.reciprocal(col(5), col(4)))
                F(v.tensor_scalar(col(6), col(3), col(5), EPS, ALU.mult, ALU.add))
                v.tensor_tensor(col(7), col(4), col(6), ALU.mult).then_inc(x1c, 1)
                v.wait_ge(x1a, i + 1)
                if i < NQT:
                    v.tensor_scalar_add(col(48 + i), col(15), 1.0)
                    v.tensor_scalar(col(64 + i), col(2), col(15), -1.0,
                                    ALU.mult, ALU.mult)
                v.tensor_scalar(col(11 + i % 2), col(2), col(9 + i % 2), -1.0,
                                ALU.mult, ALU.mult).then_inc(x1c, 1)
                for dc in range(DC):
                    k = i * DC + dc
                    v.wait_ge(x1t, k + 1)
                    tp = bank_bf(6 + (k % 2), 128)
                    v.tensor_scalar(XnT[:, dc * 4096 + i * 128: dc * 4096 + (i + 1) * 128],
                                    tp, lnw_s[:, dc:dc + 1], None, ALU.mult).then_inc(x1v, 1)
            v.nop().then_inc(pb1, 1)
            v.wait_ge(pb1, 5)

            for h in range(H):
                for r in range(3):
                    v.wait_ge(apb, h * 3 + r + 1)
                    scl = QSCALE if r == 0 else 1.0
                    v.tensor_scalar(col(16 + (h % 2) * 3 + r), PS[:, 512 * 6:512 * 6 + 1],
                                    scl, None, ALU.mult).then_inc(avb, 1)
                for rqk in range(NQK):
                    grp = h * NQK + rqk
                    v.wait_ge(art, grp + 1)
                    xr = xr0 if (rqk % 2 == 0) else xr1
                    rot = PS[0:64, 512 * (4 + rqk % 2):512 * (4 + rqk % 2) + 512]
                    if rqk < 4:
                        pos = rqk * 512
                        tgt = qbuf[0:64, rqk * 512:(rqk + 1) * 512]
                    else:
                        pos = (rqk - 4) * 512
                        tgt = kbuf[0:64, (rqk - 4) * 512:(rqk - 4 + 1) * 512]
                    v.tensor_tensor(t1b[0:64, :], xr[0:64, :], cosb[0:64, pos:pos + 512], ALU.mult)
                    v.tensor_tensor(t2b[0:64, :], rot, sinb[0:64, pos:pos + 512], ALU.mult)
                    v.tensor_add(tgt, t1b[0:64, :], t2b[0:64, :]).then_inc(arp, 1)
                for c in range(12):
                    v.wait_ge(anm, h * 12 + c + 1)
                    row = PS[0:1, 512 * (4 + c % 2):512 * (4 + c % 2) + 512]
                    if c < 4:
                        v.tensor_reduce(ntab[0:1, 4 * h + c:4 * h + c + 1], row,
                                        AX, ALU.max).then_inc(anr, 1)
                    elif c == 4:
                        v.tensor_reduce(ntab[0:1, 64 + h:64 + h + 1], row,
                                        AX, ALU.max).then_inc(anr, 1)
                    else:
                        F(v.tensor_reduce(ktmp, row, AX, ALU.max))
                        v.tensor_tensor(ntab[0:1, 64 + h:64 + h + 1],
                                        ntab[0:1, 64 + h:64 + h + 1], ktmp,
                                        ALU.max).then_inc(anr, 1)
                for kc in range(32):
                    gk = h * 32 + kc
                    v.wait_ge(avt, gk + 1)
                    tp = bank_bf(6 + (kc % 2), 128)
                    v.tensor_copy(vfb[:, kc * 128:(kc + 1) * 128], tp).then_inc(avc, 1)
            v.nop().then_inc(pb2a, 1)
            v.wait_ge(pb2a, 5)

            for h in range(H):
                ins = v.tensor_scalar(ztab[0:1, 4 * h:4 * h + 4],
                                      ntab[0:1, 4 * h:4 * h + 4],
                                      ntab[0:1, 64 + h:64 + h + 1], None, ALU.mult)
            ins.then_inc(bz1, 1)
            v.wait_ge(bz2, 1)
            v.tensor_scalar_mul(nbtab, ztab, -1.02).then_inc(bz2, 1)
            for gj in range(64):
                h, j = gj // 4, gj % 4
                v.wait_ge(bt2, gj + 1)
                v.tensor_copy(col(33), PS[:, 512 * 7:512 * 7 + 1]).then_inc(bv2, 1)
                v.wait_ge(bpv, gj + 1)
                F(v.tensor_copy(lrow, PS[0:1, 512 * 7:512 * 7 + 512]))
                F(v.tensor_scalar_max(lrow, lrow, 1e-30))
                v.reciprocal(rrow, lrow).then_inc(blv, 1)
                v.wait_ge(bt3, gj + 1)
                v.tensor_copy(r_bc, bank(7)).then_inc(bv3, 1)
                v.tensor_tensor(Ctx[:, h * 2048 + j * 512: h * 2048 + (j + 1) * 512],
                                bank(6), r_bc, ALU.mult).then_inc(bcx, 1)
            v.nop().then_inc(pb2b, 1)
            v.wait_ge(pb2b, 5)

            for qt in range(NQT):
                xs = x0b[:, (qt % 2) * 2048:(qt % 2 + 1) * 2048]
                v.wait_ge(c3x2[qt % 2], 16 * (qt // 2 + 1))
                v.wait_ge(c3m, qt * 4 + 4)
                if qt >= 2:
                    v.wait_ge(c3st2[qt % 2], 16 * (qt // 2))
                pp = qt % 2
                xf = x1f[:, pp * 2048:(pp + 1) * 2048]
                v.scalar_tensor_tensor(xf, xs, col(48 + qt), PS[:, pp * 2048:(pp + 1) * 2048],
                                       ALU.mult, ALU.add)
                v.tensor_scalar_add(xf, xf, col(64 + qt)).then_inc(c3e, 1)
            v.nop().then_inc(pb3, 1)
            v.wait_ge(pb3, 5)

            for qt in range(NQT):
                xs = x1rb[:, (qt % 2) * 2048:(qt % 2 + 1) * 2048]
                v.wait_ge(d5x2[qt % 2], 16 * (qt // 2 + 1))
                v.reduce_sum(col(0), xs, axis=AX)
                F(v.scalar_tensor_tensor(jnk35, xs, 1.0, xs, ALU.bypass, ALU.mult,
                                         accum_out=col(1)))
                F(v.tensor_scalar_mul(col(2), col(0), 1.0 / D))
                F(v.tensor_tensor(col(46), col(2), col(2), ALU.mult))
                F(v.tensor_scalar(col(3), col(1), 1.0 / D, col(46), ALU.mult, ALU.subtract))
                v.tensor_scalar_add(col(7), col(3), EPS).then_inc(d5c, 1)
                v.wait_ge(d5a, qt + 1)
                v.tensor_scalar(col(11 + qt % 2), col(2), col(9 + qt % 2), -1.0,
                                ALU.mult, ALU.mult).then_inc(d5c, 1)
                for dc in range(DC):
                    k = qt * DC + dc
                    v.wait_ge(d5t, k + 1)
                    tp = bank_bf(0 + (k % 2), 128)
                    v.tensor_copy(X1nT[:, dc * 2048 + qt * 128: dc * 2048 + (qt + 1) * 128],
                                  tp).then_inc(d5v, 1)
            v.nop().then_inc(pb35, 1)
            v.wait_ge(pb35, 5)

            for g4 in range(4):
                for oc in range(64):
                    gi = g4 * 64 + oc
                    v.wait_ge(e4u, gi + 1)
                    v.wait_ge(e4s, gi + 1)
                    v.tensor_tensor(HT[:, oc * 512:(oc + 1) * 512],
                                    bank(1 + 2 * (gi % 2)),
                                    sgb[:, (gi % 2) * 512:(gi % 2 + 1) * 512],
                                    ALU.mult).then_inc(e4h, 1)
                for p in range(2):
                    for qs in range(2):
                        qt = g4 * 4 + p * 2 + qs
                        v.wait_ge(e4bm, (g4 * 2 + p) * 64 + 64)
                        v.wait_ge(e4x2[qt % 2], 16 * (qt // 2 + 1))
                        if qt >= 2:
                            v.wait_ge(e4o2[qt % 2], 16 * (qt // 2))
                        x2slot = x2fB if (qt % 2 == 0) else x2fC
                        v.scalar_tensor_tensor(
                            x2slot,
                            PS[:, qs * 2048:(qs + 1) * 2048], 1.0,
                            x1r4[:, (qt % 2) * 2048:(qt % 2 + 1) * 2048],
                            ALU.mult, ALU.add).then_inc(e4e, 1)

    return nc


_NC_CACHE = None


def _get_nc():
    global _NC_CACHE
    if _NC_CACHE is None:
        _NC_CACHE = build_program()
    return _NC_CACHE


def _host_tables():
    import ml_dtypes
    rot = 64
    inv_freq = (1.0 / 10000.0 ** (np.arange(0, rot, 2, dtype=np.float32) / rot)).astype(np.float32)
    ang = np.arange(S, dtype=np.float32)[:, None] * inv_freq[None, :]
    ang = np.repeat(ang, 2, axis=-1)
    cos_g = np.cos(ang).T.astype(np.float32)
    sin_g = np.sin(ang).T.astype(np.float32)
    perm = np.zeros((64, 64), dtype=np.float32)
    for i in range(32):
        perm[2 * i + 1, 2 * i] = -1.0
        perm[2 * i, 2 * i + 1] = 1.0
    idb = np.eye(128, dtype=ml_dtypes.bfloat16)
    idf = np.eye(128, dtype=np.float32)
    ones_r = np.ones((1, 128), dtype=np.float32)
    ones_cb = np.ones((128, 1), dtype=ml_dtypes.bfloat16)
    return cos_g, sin_g, perm, idb, idf, ones_r, ones_cb


def kernel(L0_attn_X, L0_attn_ln_w, L0_attn_ln_b, L0_attn_W_in, L0_attn_W_out,
           L0_ffn_w_gate_up, L0_ffn_w_down):
    X = np.ascontiguousarray(np.asarray(L0_attn_X, dtype=np.float32))
    ln_w = np.asarray(L0_attn_ln_w, dtype=np.float32)
    ln_b = np.asarray(L0_attn_ln_b, dtype=np.float32)
    W_in = np.asarray(L0_attn_W_in, dtype=np.float32)
    W_out = np.ascontiguousarray(np.asarray(L0_attn_W_out, dtype=np.float32))
    w_gu = np.asarray(L0_ffn_w_gate_up, dtype=np.float32)
    w_dn = np.asarray(L0_ffn_w_down, dtype=np.float32)

    w_in_t = np.ascontiguousarray(
        W_in.reshape(DC, 128, 48, 128).transpose(2, 1, 0, 3).reshape(48, 128, DC * 128))
    wgu_t = np.ascontiguousarray(
        w_gu.T.reshape(DC, 128, 128, 128).transpose(2, 1, 0, 3).reshape(128, 128, DC * 128))
    wdn_t = np.ascontiguousarray(w_dn.T.reshape(64, 128, D))
    lnw_c = np.ascontiguousarray(ln_w.reshape(DC, 128).T)
    lnb_c = np.ascontiguousarray(ln_b.reshape(DC, 128).T)

    cos_g, sin_g, perm, idb, idf, ones_r, ones_cb = _host_tables()

    nc = _get_nc()
    in_maps = []
    for c in range(8):
        b, half = c // 2, c % 2
        off = half * SQ
        in_maps.append({
            "x": np.roll(X[b], -off, axis=0) if off else X[b],
            "w_in_t": w_in_t,
            "w_out": W_out,
            "wgu_t": wgu_t,
            "wdn_t": wdn_t,
            "lnw_c": lnw_c,
            "lnb_c": lnb_c,
            "cos_t": np.roll(cos_g, -off, axis=1) if off else cos_g,
            "sin_t": np.roll(sin_g, -off, axis=1) if off else sin_g,
            "perm": perm,
            "id_bf": idb,
            "id_f32": idf,
            "ones_r": ones_r,
            "ones_cb": ones_cb,
        })
    res = run_bass_kernel_spmd(nc, in_maps, core_ids=list(range(8)))
    out = np.empty((4, S, D), dtype=np.float32)
    for c in range(8):
        b, half = c // 2, c % 2
        out[b, half * SQ:(half + 1) * SQ, :] = np.asarray(res.results[c]["out"],
                                                          dtype=np.float32)
    return out


# version 05
# speedup vs baseline: 1.4361x; 1.0852x over previous
import numpy as np
from contextlib import ExitStack

import concourse.bass as bass
import concourse.mybir as mybir
from concourse.bass_utils import run_bass_kernel_spmd

F32 = mybir.dt.float32
BF16 = mybir.dt.bfloat16
AF = mybir.ActivationFunctionType
ALU = mybir.AluOpType
AX = mybir.AxisListType.X

D = 2048
S = 4096
SQ = 2048
H = 16
DH = 128
DC = 16
INTER = 8192
NT = S // 128
NQT = SQ // 128
QSCALE = 1.0 / float(np.sqrt(128.0))
EPS = 1e-5

_DEBUG_TAPS = []


def build_program(debug=False):
    nc = bass.Bass("TRN2", target_bir_lowering=False)
    SCR = "ExternalOutput" if debug else "Internal"

    x_in = nc.dram_tensor("x", [S, D], F32, kind="ExternalInput").ap()
    w_in_t = nc.dram_tensor("w_in_t", [48, 128, DC * 128], F32, kind="ExternalInput").ap()
    w_out_in = nc.dram_tensor("w_out", [D, D], F32, kind="ExternalInput").ap()
    wgu_t = nc.dram_tensor("wgu_t", [128, 128, DC * 128], F32, kind="ExternalInput").ap()
    wdn_t = nc.dram_tensor("wdn_t", [64, 128, D], F32, kind="ExternalInput").ap()
    lnw_in = nc.dram_tensor("lnw_c", [128, DC], F32, kind="ExternalInput").ap()
    lnb_in = nc.dram_tensor("lnb_c", [128, DC], F32, kind="ExternalInput").ap()
    cos_in = nc.dram_tensor("cos_t", [64, S], F32, kind="ExternalInput").ap()
    sin_in = nc.dram_tensor("sin_t", [64, S], F32, kind="ExternalInput").ap()
    perm_in = nc.dram_tensor("perm", [64, 64], F32, kind="ExternalInput").ap()
    idb_in = nc.dram_tensor("id_bf", [128, 128], BF16, kind="ExternalInput").ap()
    idf_in = nc.dram_tensor("id_f32", [128, 128], F32, kind="ExternalInput").ap()
    onesr_in = nc.dram_tensor("ones_r", [1, 128], F32, kind="ExternalInput").ap()
    onesc_in = nc.dram_tensor("ones_cb", [128, 1], BF16, kind="ExternalInput").ap()

    out_t = nc.dram_tensor("out", [SQ, D], F32, kind="ExternalOutput").ap()

    q_scr = nc.dram_tensor("q_scr", [H, 128, SQ], BF16, kind=SCR).ap()
    k_scr = nc.dram_tensor("k_scr", [H, 128, S], BF16, kind=SCR).ap()
    vt_scr = nc.dram_tensor("vt_scr", [H, 128, S], BF16, kind=SCR).ap()
    x1_scr = nc.dram_tensor("x1_scr", [NQT, 128, D], F32, kind=SCR).ap()
    wgu_bf = nc.dram_tensor("wgu_bf", [128, 128, DC * 128], BF16, kind="Internal").ap()
    wdn_bf = nc.dram_tensor("wdn_bf", [64, 128, D], BF16, kind="Internal").ap()

    with ExitStack() as ctx:
        E = ctx.enter_context

        A = E(nc.sbuf_tensor("arA", [128, 65536], BF16)).ap()
        B = E(nc.sbuf_tensor("arB", [128, 28672], BF16)).ap()
        C = E(nc.sbuf_tensor("arC", [128, 4096], F32)).ap()
        COLS = E(nc.sbuf_tensor("cols", [128, 96], F32)).ap()
        perm_s = E(nc.sbuf_tensor("perm_s", [64, 64], F32)).ap()
        idb = E(nc.sbuf_tensor("idb", [128, 128], BF16)).ap()
        idf = E(nc.sbuf_tensor("idf", [128, 128], F32)).ap()
        ones_r = E(nc.sbuf_tensor("ones_r_s", [1, 128], F32)).ap()
        ones_cb = E(nc.sbuf_tensor("ones_cb_s", [128, 1], BF16)).ap()
        lnw_s = E(nc.sbuf_tensor("lnw_s", [128, DC], F32)).ap()
        ntab = E(nc.sbuf_tensor("ntab", [1, 128], F32)).ap()
        negbt = E(nc.sbuf_tensor("negbt", [128, 64], F32)).ap()
        lnbb = E(nc.sbuf_tensor("lnbb", [128, DC], BF16)).ap()

        PS = E(nc.psum_tensor("ps", [128, 4096], F32)).ap()

        def bank(i, n=512):
            return PS[:, 512 * i: 512 * i + n]

        def bank_bf(i, cols=128, off=0):
            return PS[:, 512 * i + off // 2: 512 * i + (off + cols) // 2].bitcast(BF16)

        XnT = A
        Ctx = A[:, 0:32768]
        P_T = A[:, 32768:49152]
        WOB = A[:, 32768:65536]
        X1nT = A[:, 0:32768]
        HT = A[:, 32768:65536]

        xb = B[:, 0:12288].bitcast(F32)
        ub = B[:, 12288:16384]
        jnk1 = B[:, 16384:20480].bitcast(F32)

        wbuf = B[:, 0:12288]
        qbuf = B[:, 12288:14336]
        kbuf = B[:, 14336:18432]
        vfb = B[:, 18432:22528]
        xr0 = B[:, 22528:23552].bitcast(F32)
        xr1 = B[:, 23552:24576].bitcast(F32)
        t1b = B[:, 24576:25600].bitcast(F32)
        t2b = B[:, 25600:26624].bitcast(F32)
        sqst = B[:, 26624:27648]

        def qh(s):
            return B[:, s * 10240: s * 10240 + 2048]

        def kh(s):
            return B[:, s * 10240 + 2048: s * 10240 + 6144]

        def vth(s):
            return B[:, s * 10240 + 6144: s * 10240 + 10240]

        x0b = B[:, 0:8192].bitcast(F32)
        x1f = B[:, 8192:16384].bitcast(F32)
        jnk3 = B[:, 16384:20480].bitcast(F32)

        x1rb = B[:, 0:8192].bitcast(F32)
        u2b = B[:, 8192:12288]
        jnk35 = B[:, 12288:16384].bitcast(F32)

        wgb = B[:, 0:4096]
        wub = B[:, 4096:8192]
        wdb = B[:, 8192:16384]
        x1r4 = B[:, 16384:24576].bitcast(F32)
        x2fB = B[:, 24576:28672].bitcast(F32)

        cosb = C[:, 0:2048].bitcast(BF16)
        sinb = C[:, 2048:4096].bitcast(BF16)
        r_bc = C[:, 0:512]
        lrow = C[0:1, 512:1024]
        rrow = C[0:1, 1024:1536]
        nmrow = C[0:1, 1536:1664]
        ncsc = C[0:1, 1664:1665]
        ktmp = COLS[0:1, 47:48]
        ztab = C[0:1, 1696:1760]
        nbtab = C[0:1, 1760:1824]
        sgb = C[:, 0:1024]
        x2fC = C[:, 2048:4096]

        def col(i, n=1):
            return COLS[:, i:i + n]

        def sem(name):
            return E(nc.semaphore(name))

        pb1, pb2a, pb2b, pb3, pb35 = sem("pb1"), sem("pb2a"), sem("pb2b"), sem("pb3"), sem("pb35")
        x1q, x1c, x1a, x1u, x1t, x1v = (sem(n) for n in
            ("x1q", "x1c", "x1a", "x1u", "x1t", "x1v"))
        xl = [sem(f"xl{i}") for i in range(3)]
        ahd, apb, avb, apq, aev, art, arp, avt, avc, acache = (sem(n) for n in
            ("ahd", "apb", "avb", "apq", "aev", "art", "arp", "avt", "avc", "acache"))
        aw2 = [sem(f"aw{i}") for i in range(2)]
        aqsq, aqsk, aqsv = sem("aqsq"), sem("aqsk"), sem("aqsv")
        b1m, b1x, b1q, bnm, bt1, bv1, bt2, bv2, b2m, bpt, bpv, blv, bt3, bv3, bcx, bhd = (
            sem(n) for n in ("b1m", "b1x", "b1q", "bnm", "bt1", "bv1", "bt2",
                             "bv2", "b2m", "bpt", "bpv", "blv", "bt3", "bv3", "bcx", "bhd"))
        bql2 = [sem(f"bql{i}") for i in range(2)]
        c3w, c3q, c3c, c3a, c3g, c3m, c3e = (sem(n) for n in
            ("c3w", "c3q", "c3c", "c3a", "c3g", "c3m", "c3e"))
        c3x2 = [sem(f"c3x{i}") for i in range(2)]
        c3st2 = [sem(f"c3st{i}") for i in range(2)]
        d5q, d5c, d5a, d5u, d5t, d5v = (sem(n) for n in
            ("d5q", "d5c", "d5a", "d5u", "d5t", "d5v"))
        d5x2 = [sem(f"d5x{i}") for i in range(2)]
        e4g, e4u, e4s, e4h, e4bm, e4e = (sem(n) for n in
            ("e4g", "e4u", "e4s", "e4h", "e4bm", "e4e"))
        aqn, anm, anr, bz1, bz2 = (sem(n) for n in ("aqn", "anm", "anr", "bz1", "bz2"))
        e4w2 = [sem(f"e4w{i}") for i in range(2)]
        e4d2 = [sem(f"e4d{i}") for i in range(2)]
        e4x2 = [sem(f"e4x{i}") for i in range(2)]
        e4o2 = [sem(f"e4o{i}") for i in range(2)]
        s_const = sem("s_const")
        s_constg = sem("s_constg")

        block = E(nc.Block())

        CH = []
        for r, n in ((0, 4), (1, 8), (2, 8)):
            for c in range(n):
                CH.append((r, c, n))
        NCH = len(CH)
        NQK = 12

        def fcol(h, r):
            return r * 16 + h

        @block.sync
        def _(s):
            s.dma_start(out=perm_s, in_=perm_in).then_inc(s_const, 16)
            s.dma_start(out=idb, in_=idb_in).then_inc(s_const, 16)
            s.dma_start(out=idf, in_=idf_in).then_inc(s_const, 16)
            s.dma_start(out=ones_r, in_=onesr_in).then_inc(s_const, 16)
            s.dma_start(out=ones_cb, in_=onesc_in).then_inc(s_const, 16)
            s.dma_start(out=lnw_s, in_=lnw_in).then_inc(s_const, 16)

            for i in range(NT):
                if i >= 3:
                    s.wait_ge(x1u, i - 2)
                s.dma_start(out=xb[:, (i % 3) * 2048:(i % 3 + 1) * 2048],
                            in_=x_in[i * 128:(i + 1) * 128, :]).then_inc(xl[i % 3], 16)

            s.nop().then_inc(pb1, 1)
            s.wait_ge(pb1, 5)
            for h in range(H):
                s.wait_ge(arp, h * NQK + 4)
                s.wait_ge(aev, h * NCH + 4)
                s.dma_start(out=q_scr[h], in_=qbuf).then_inc(aqsq, 16)
                s.wait_ge(arp, h * NQK + 12)
                s.wait_ge(aev, h * NCH + 12)
                s.dma_start(out=k_scr[h], in_=kbuf).then_inc(aqsk, 16)
                for g in range(8):
                    s.wait_ge(avc, h * 32 + 4 * (g + 1))
                    s.dma_start(out=vt_scr[h][:, g * 512:(g + 1) * 512],
                                in_=vfb[:, g * 512:(g + 1) * 512]).then_inc(aqsv, 16)

            s.nop().then_inc(pb2a, 1)
            s.wait_ge(pb2a, 5)
            s.nop().then_inc(pb2b, 1)
            s.wait_ge(pb2b, 5)

            s.wait_ge(bpv, 64)
            for qt in range(NQT):
                if qt >= 2:
                    s.wait_ge(c3e, qt - 1)
                s.dma_start(out=x0b[:, (qt % 2) * 2048:(qt % 2 + 1) * 2048],
                            in_=x_in[qt * 128:(qt + 1) * 128, :]).then_inc(c3x2[qt % 2], 16)
                s.wait_ge(c3e, qt + 1)
                s.dma_start(out=x1_scr[qt],
                            in_=x1f[:, (qt % 2) * 2048:(qt % 2 + 1) * 2048]).then_inc(c3st2[qt % 2], 16)

            s.nop().then_inc(pb3, 1)
            s.wait_ge(pb3, 5)

            s.wait_ge(c3st2[0], 16 * 8)
            s.wait_ge(c3st2[1], 16 * 8)
            for qt in range(NQT):
                if qt >= 2:
                    s.wait_ge(d5u, qt - 1)
                s.dma_start(out=x1rb[:, (qt % 2) * 2048:(qt % 2 + 1) * 2048],
                            in_=x1_scr[qt]).then_inc(d5x2[qt % 2], 16)

            s.nop().then_inc(pb35, 1)
            s.wait_ge(pb35, 5)

            s.wait_ge(d5u, NQT)
            for g in range(4):
                for oc in range(64):
                    gi = g * 64 + oc
                    if gi >= 2:
                        s.wait_ge(e4g, gi - 1)
                        s.wait_ge(e4u, gi - 1)
                    s.wait_ge(acache, 16 * 24)
                    s.dma_start(out=wgb[:, (gi % 2) * 2048:(gi % 2 + 1) * 2048],
                                in_=wgu_bf[oc]).then_inc(e4w2[gi % 2], 16)
                    s.dma_start(out=wub[:, (gi % 2) * 2048:(gi % 2 + 1) * 2048],
                                in_=wgu_bf[64 + oc]).then_inc(e4w2[gi % 2], 16)
                for p in range(2):
                    for qs in range(2):
                        qt = g * 4 + p * 2 + qs
                        if qt == 0:
                            s.wait_ge(c3st2[0], 16 * 8)
                            s.wait_ge(c3st2[1], 16 * 8)
                        if qt >= 2:
                            s.wait_ge(e4e, qt - 1)
                        s.dma_start(out=x1r4[:, (qt % 2) * 2048:(qt % 2 + 1) * 2048],
                                    in_=x1_scr[qt]).then_inc(e4x2[qt % 2], 16)
                        s.wait_ge(e4e, qt + 1)
                        s.dma_start(out=out_t[qt * 128:(qt + 1) * 128, :],
                                    in_=(x2fB if (qt % 2 == 0) else x2fC.copy())
                                    ).then_inc(e4o2[qt % 2], 16)
            s.wait_ge(e4o2[0], 16 * 8)
            s.wait_ge(e4o2[1], 16 * 8)

        @block.gpsimd
        def _(g):
            g.dma_start(out=lnbb, in_=lnb_in).then_inc(s_constg, 16)
            g.dma_start(out=cosb[0:64, :], in_=cos_in).then_inc(s_constg, 16)
            g.dma_start(out=sinb[0:64, :], in_=sin_in).then_inc(s_constg, 16)
            g.nop().then_inc(pb1, 1)
            g.wait_ge(pb1, 5)
            g.wait_ge(x1u, NT)
            g.wait_ge(x1c, 2 * NT)

            cache_jobs = []
            for i in range(16):
                cache_jobs.append((wgu_bf[i * 8:(i + 1) * 8], wgu_t[i * 8:(i + 1) * 8]))
            for i in range(8):
                cache_jobs.append((wdn_bf[i * 8:(i + 1) * 8], wdn_t[i * 8:(i + 1) * 8]))
            cj = 0
            for h in range(H):
                if h >= 2:
                    g.wait_ge(apq, (h - 1) * NCH)
                for r in range(3):
                    g.dma_start(out=wbuf[:, ((h % 2) * 3 + r) * 2048:((h % 2) * 3 + r + 1) * 2048],
                                in_=w_in_t[fcol(h, r)]).then_inc(aw2[h % 2], 16)
                for _ in range(2):
                    if cj < len(cache_jobs):
                        dst, src = cache_jobs[cj]
                        g.dma_start(out=dst, in_=src).then_inc(acache, 16)
                        cj += 1
            while cj < len(cache_jobs):
                dst, src = cache_jobs[cj]
                g.dma_start(out=dst, in_=src).then_inc(acache, 16)
                cj += 1

            g.nop().then_inc(pb2a, 1)
            g.wait_ge(pb2a, 5)
            g.nop().then_inc(pb2b, 1)
            g.wait_ge(pb2b, 5)

            g.wait_ge(bpv, 64)
            for dc in range(DC):
                g.dma_start(out=WOB[:, dc * 2048:(dc + 1) * 2048],
                            in_=w_out_in[dc * 128:(dc + 1) * 128, :]).then_inc(c3w, 16)
            g.nop().then_inc(pb3, 1)
            g.wait_ge(pb3, 5)
            g.nop().then_inc(pb35, 1)
            g.wait_ge(pb35, 5)

            g.wait_ge(d5t, NQT * DC)
            for gi in range(4):
                for p in range(2):
                    for oc in range(64):
                        di = (gi * 2 + p) * 64 + oc
                        if di >= 4:
                            g.wait_ge(e4bm, di - 3)
                        g.wait_ge(acache, 16 * 24)
                        g.dma_start(out=wdb[:, (di % 4) * 2048:(di % 4 + 1) * 2048],
                                    in_=wdn_bf[oc]).then_inc(e4d2[di % 2], 16)

        @block.tensor
        def _(t):
            t.wait_ge(s_const, 16 * 6)
            t.wait_ge(s_constg, 16 * 3)
            for i in range(NT):
                t.wait_ge(x1u, i + 1)
                for dc in range(DC):
                    k = i * DC + dc
                    if k >= 2:
                        t.wait_ge(x1v, k - 1)
                    tp = bank_bf(6 + (k % 2), 128)
                    usl = ub[:, (i % 2) * 2048 + dc * 128:(i % 2) * 2048 + (dc + 1) * 128]
                    t.transpose(tp, usl, idb).then_inc(x1t, 1)
            t.nop().then_inc(pb1, 1)
            t.wait_ge(pb1, 5)
            t.wait_ge(x1v, 512)

            for h in range(H):
                t.wait_ge(aw2[h % 2], 16 * 3 * (h // 2 + 1))
                for r in range(3):
                    pb_ = PS[:, 512 * 6:512 * 6 + 1]
                    if h > 0 or r > 0:
                        t.wait_ge(avb, h * 3 + r)
                    if h > 0 and r == 0:
                        t.wait_ge(avc, 32 * h)
                    ws = wbuf[:, ((h % 2) * 3 + r) * 2048:((h % 2) * 3 + r + 1) * 2048]
                    for dc in range(DC):
                        ins = t.matmul(pb_, ws[:, dc * 128:(dc + 1) * 128], lnbb[:, dc:dc + 1],
                                       start=(dc == 0), stop=(dc == DC - 1))
                        if dc == DC - 1:
                            ins.then_inc(apb, 1)

                rope_pending = []
                for ci, (r, c, n) in enumerate(CH):
                    gcid = h * NCH + ci
                    slot = gcid % 4
                    if gcid >= 4:
                        t.wait_ge(aev, gcid - 3)
                    ws = wbuf[:, ((h % 2) * 3 + r) * 2048:((h % 2) * 3 + r + 1) * 2048]
                    dst = bank(slot)
                    toff = c * 512 if r == 0 else c * 512
                    for dc in range(DC):
                        ins = t.matmul(dst, ws[:, dc * 128:(dc + 1) * 128],
                                       XnT[:, dc * 4096 + toff:dc * 4096 + toff + 512],
                                       start=(dc == 0), stop=(dc == DC - 1))
                        if dc == DC - 1:
                            ins.then_inc(apq, 1)
                    if rope_pending:
                        rj, rslot = rope_pending.pop(0)
                        t.wait_ge(aev, rj + 1)
                        if rslot == 0 and h > 0:
                            t.wait_ge(anr, 12 * h)
                        if (h * NQK + rslot) >= 2:
                            t.wait_ge(arp, h * NQK + rslot - 1)
                        xr = xr0 if (rslot % 2 == 0) else xr1
                        t.matmul(PS[0:64, 512 * (4 + rslot % 2):512 * (4 + rslot % 2) + 512],
                                 perm_s, xr[0:64, :], start=True, stop=True).then_inc(art, 1)
                    if r in (0, 1):
                        rqk = c if r == 0 else 4 + c
                        rope_pending.append((gcid, rqk))
                while rope_pending:
                    rj, rslot = rope_pending.pop(0)
                    t.wait_ge(aev, rj + 1)
                    if (h * NQK + rslot) >= 2:
                        t.wait_ge(arp, h * NQK + rslot - 1)
                    xr = xr0 if (rslot % 2 == 0) else xr1
                    t.matmul(PS[0:64, 512 * (4 + rslot % 2):512 * (4 + rslot % 2) + 512],
                             perm_s, xr[0:64, :], start=True, stop=True).then_inc(art, 1)

                t.wait_ge(arp, h * NQK + NQK)
                for c in range(12):
                    t.wait_ge(aqn, h * 12 + c + 1)
                    if h * 12 + c >= 2:
                        t.wait_ge(anr, h * 12 + c - 1)
                    t.matmul(PS[0:1, 512 * (4 + c % 2):512 * (4 + c % 2) + 512],
                             ones_cb, sqst[:, (c % 2) * 512:(c % 2 + 1) * 512],
                             start=True, stop=True).then_inc(anm, 1)
                t.wait_ge(avb, h * 3 + 3)
                for kc in range(32):
                    t.wait_ge(aev, h * NCH + 12 + (kc // 4) + 1)
                    gk = h * 32 + kc
                    if gk >= 2:
                        t.wait_ge(avc, gk - 1)
                    tp = bank_bf(6 + (kc % 2), 128)
                    t.transpose(tp, vfb[:, kc * 128:(kc + 1) * 128], idb).then_inc(avt, 1)

            t.nop().then_inc(pb2a, 1)
            t.wait_ge(pb2a, 5)

            t.wait_ge(aev, H * NCH)
            t.wait_ge(avc, H * 32)
            t.wait_ge(arp, H * NQK)
            t.wait_ge(anr, H * 12)
            t.wait_ge(bz2, 2)
            t.matmul(PS[:, 512 * 7:512 * 7 + 64], ones_r, nbtab,
                     start=True, stop=True).then_inc(bt2, 1)
            for gj in range(64):
                h, j = gj // 4, gj % 4
                sl = h % 2
                if j == 0:
                    t.wait_ge(bql2[h % 2], 16 * 3 * (h // 2 + 1))
                for g in range(16):
                    if gj * 16 + g >= 3:
                        t.wait_ge(bpt, gj * 16 + g - 2)
                    pb_ = (g % 3) * 2
                    for half in range(2):
                        kc = 2 * g + half
                        t.matmul(PS[:, (pb_ + half) * 512:(pb_ + half) * 512 + 512],
                                 kh(sl)[:, kc * 128:(kc + 1) * 128],
                                 qh(sl)[:, j * 512:(j + 1) * 512],
                                 start=True, stop=True).then_inc(b2m, 1)
                    if g >= 1:
                        t.wait_ge(bpt, gj * 16 + g)
                        for kc in (2 * g - 2, 2 * g - 1):
                            if kc == 0:
                                t.wait_ge(bcx, gj)
                                t.wait_ge(bv2, 1)
                                if gj > 0:
                                    t.wait_ge(bv3, gj)
                            t.matmul(bank(6), vth(sl)[:, kc * 128:(kc + 1) * 128],
                                     P_T[:, kc * 512:(kc + 1) * 512],
                                     start=(kc == 0), stop=False)
                            t.matmul(PS[0:1, 512 * 7:512 * 7 + 512], ones_cb,
                                     P_T[:, kc * 512:(kc + 1) * 512],
                                     start=(kc == 0), stop=False)
                t.wait_ge(bpt, gj * 16 + 16)
                for kc in (30, 31):
                    t.matmul(bank(6), vth(sl)[:, kc * 128:(kc + 1) * 128],
                             P_T[:, kc * 512:(kc + 1) * 512],
                             start=False, stop=(kc == 31))
                    ins = t.matmul(PS[0:1, 512 * 7:512 * 7 + 512], ones_cb,
                                   P_T[:, kc * 512:(kc + 1) * 512],
                                   start=False, stop=(kc == 31))
                    if kc == 31:
                        ins.then_inc(bpv, 1)
                t.wait_ge(blv, gj + 1)
                t.matmul(bank(7), ones_r, rrow, start=True, stop=True).then_inc(bt3, 1)

            t.nop().then_inc(pb2b, 1)
            t.wait_ge(pb2b, 5)

            t.wait_ge(bcx, 64)
            t.wait_ge(bv3, 64)
            t.wait_ge(bpt, 1024)
            t.wait_ge(c3w, 16 * DC)
            for qt in range(NQT):
                pp = qt % 2
                if qt >= 2:
                    t.wait_ge(c3e, qt - 1)
                for oc in range(4):
                    dst = PS[:, pp * 2048 + oc * 512: pp * 2048 + oc * 512 + 512]
                    for dcc in range(DC):
                        ins = t.matmul(dst,
                                       Ctx[:, dcc * 2048 + qt * 128: dcc * 2048 + (qt + 1) * 128],
                                       WOB[:, dcc * 2048 + oc * 512: dcc * 2048 + oc * 512 + 512],
                                       start=(dcc == 0), stop=(dcc == DC - 1))
                        if dcc == DC - 1:
                            ins.then_inc(c3m, 1)
            t.nop().then_inc(pb3, 1)
            t.wait_ge(pb3, 5)

            t.wait_ge(c3e, NQT)
            for qt in range(NQT):
                t.wait_ge(d5u, qt + 1)
                for dc in range(DC):
                    k = qt * DC + dc
                    if k >= 2:
                        t.wait_ge(d5v, k - 1)
                    tp = bank_bf(0 + (k % 2), 128)
                    usl = u2b[:, (qt % 2) * 2048 + dc * 128:(qt % 2) * 2048 + (dc + 1) * 128]
                    t.transpose(tp, usl, idb).then_inc(d5t, 1)
            t.nop().then_inc(pb35, 1)
            t.wait_ge(pb35, 5)

            t.wait_ge(d5v, NQT * DC)
            for g4 in range(4):
                if g4 > 0:
                    t.wait_ge(e4e, g4 * 4)
                for oc in range(64):
                    gi = g4 * 64 + oc
                    t.wait_ge(e4w2[gi % 2], 16 * 2 * (gi // 2 + 1))
                    if gi >= 2:
                        t.wait_ge(e4s, gi - 1)
                        t.wait_ge(e4h, gi - 1)
                    pg = bank(0 + 2 * (gi % 2))
                    pu = bank(1 + 2 * (gi % 2))
                    for hc in range(DC):
                        ins = t.matmul(pg, wgb[:, (gi % 2) * 2048 + hc * 128:(gi % 2) * 2048 + (hc + 1) * 128],
                                       X1nT[:, hc * 2048 + g4 * 512:hc * 2048 + (g4 + 1) * 512],
                                       start=(hc == 0), stop=(hc == DC - 1))
                        if hc == DC - 1:
                            ins.then_inc(e4g, 1)
                    for hc in range(DC):
                        ins = t.matmul(pu, wub[:, (gi % 2) * 2048 + hc * 128:(gi % 2) * 2048 + (hc + 1) * 128],
                                       X1nT[:, hc * 2048 + g4 * 512:hc * 2048 + (g4 + 1) * 512],
                                       start=(hc == 0), stop=(hc == DC - 1))
                        if hc == DC - 1:
                            ins.then_inc(e4u, 1)
                t.wait_ge(e4h, g4 * 64 + 64)
                for p in range(2):
                    if p == 1:
                        t.wait_ge(e4e, g4 * 4 + 2)
                    for oc in range(64):
                        di = (g4 * 2 + p) * 64 + oc
                        t.wait_ge(e4d2[di % 2], 16 * (di // 2 + 1))
                        for qs in range(2):
                            qloc = p * 2 + qs
                            for ho in range(4):
                                ins = t.matmul(PS[:, qs * 2048 + ho * 512: qs * 2048 + ho * 512 + 512],
                                               HT[:, oc * 512 + qloc * 128: oc * 512 + qloc * 128 + 128],
                                               wdb[:, (di % 4) * 2048 + ho * 512:(di % 4) * 2048 + ho * 512 + 512],
                                               start=(oc == 0), stop=(oc == 63))
                                if qs == 1 and ho == 3:
                                    ins.then_inc(e4bm, 1)

        @block.scalar
        def _(sc):
            sc.wait_ge(s_const, 16 * 6)
            sc.wait_ge(s_constg, 16 * 3)
            for i in range(NT):
                xs = xb[:, (i % 3) * 2048:(i % 3 + 1) * 2048]
                sc.wait_ge(xl[i % 3], 16 * (i // 3 + 1))
                sc.wait_ge(x1c, 2 * i + 1)
                sc.activation(col(8), col(7), AF.Ln)
                if i < NQT:
                    sc.activation(col(9 + i % 2), col(8), AF.Exp, scale=-0.5)
                    sc.activation(col(14), col(4), AF.Ln)
                    sc.activation(col(15), col(14), AF.Exp, scale=-0.5).then_inc(x1a, 1)
                else:
                    sc.activation(col(9 + i % 2), col(8), AF.Exp, scale=-0.5
                                  ).then_inc(x1a, 1)
                sc.wait_ge(x1c, 2 * i + 2)
                if i >= 2:
                    sc.wait_ge(x1t, 16 * (i - 1))
                sc.activation(ub[:, (i % 2) * 2048:(i % 2 + 1) * 2048], xs, AF.Identity,
                              bias=col(11 + i % 2), scale=col(9 + i % 2)).then_inc(x1u, 1)
                if i >= 1:
                    ii = i - 1
                    for dc in range(DC):
                        k = ii * DC + dc
                        sc.wait_ge(x1t, k + 1)
                        tp = bank_bf(6 + (k % 2), 128)
                        sc.activation(XnT[:, dc * 4096 + ii * 128: dc * 4096 + (ii + 1) * 128],
                                      tp, AF.Identity,
                                      scale=lnw_s[:, dc:dc + 1]).then_inc(x1v, 1)
                if i == NT - 1:
                    for dc in range(DC):
                        k = i * DC + dc
                        sc.wait_ge(x1t, k + 1)
                        tp = bank_bf(6 + (k % 2), 128)
                        sc.activation(XnT[:, dc * 4096 + i * 128: dc * 4096 + (i + 1) * 128],
                                      tp, AF.Identity,
                                      scale=lnw_s[:, dc:dc + 1]).then_inc(x1v, 1)
            sc.nop().then_inc(pb1, 1)
            sc.wait_ge(pb1, 5)

            for h in range(H):
                for ci, (r, c, n) in enumerate(CH):
                    gcid = h * NCH + ci
                    slot = gcid % 4
                    sc.wait_ge(apq, gcid + 1)
                    sc.wait_ge(avb, h * 3 + r + 1)
                    bcol = col(16 + (h % 2) * 3 + r)
                    scl = QSCALE if r == 0 else 1.0
                    if r == 0:
                        if h > 0 and c == 0:
                            sc.wait_ge(aqsq, 16 * h)
                        rqk = c
                        if h * NQK + rqk >= 2:
                            sc.wait_ge(arp, h * NQK + rqk - 1)
                        sc.activation(qbuf[:, c * 512:(c + 1) * 512], bank(slot),
                                      AF.Identity, bias=bcol, scale=scl)
                        sc.activation(xr0[0:64, :] if (c % 2 == 0) else xr1[0:64, :],
                                      bank(slot)[0:64, :], AF.Identity,
                                      bias=bcol[0:64, :], scale=scl).then_inc(aev, 1)
                    elif r == 1:
                        if h > 0 and c == 0:
                            sc.wait_ge(aqsk, 16 * h)
                        rqk = 4 + c
                        if h * NQK + rqk >= 2:
                            sc.wait_ge(arp, h * NQK + rqk - 1)
                        sc.activation(kbuf[:, c * 512:(c + 1) * 512], bank(slot),
                                      AF.Identity, bias=bcol, scale=scl)
                        sc.activation(xr0[0:64, :] if (rqk % 2 == 0) else xr1[0:64, :],
                                      bank(slot)[0:64, :], AF.Identity,
                                      bias=bcol[0:64, :], scale=scl).then_inc(aev, 1)
                    else:
                        if h > 0 and c == 0:
                            sc.wait_ge(avt, 32 * h)
                            sc.wait_ge(aqsv, 16 * 8 * h)
                        sc.activation(vfb[:, c * 512:(c + 1) * 512], bank(slot),
                                      AF.Identity, bias=bcol, scale=1.0).then_inc(aev, 1)
                for c in range(12):
                    sc.wait_ge(arp, h * NQK + c + 1)
                    if h * 12 + c >= 2:
                        sc.wait_ge(anm, h * 12 + c - 1)
                    srcap = (qbuf[:, c * 512:(c + 1) * 512] if c < 4
                             else kbuf[:, (c - 4) * 512:(c - 3) * 512])
                    sc.activation(sqst[:, (c % 2) * 512:(c % 2 + 1) * 512], srcap,
                                  AF.Square).then_inc(aqn, 1)
            sc.nop().then_inc(pb2a, 1)
            sc.wait_ge(pb2a, 5)

            sc.wait_ge(aqsq, 16 * 16)
            sc.wait_ge(aqsk, 16 * 16)
            sc.wait_ge(aqsv, 16 * 128)
            for h in range(2):
                sc.dma_start(out=qh(h % 2), in_=q_scr[h]).then_inc(bql2[h % 2], 16)
                sc.dma_start(out=kh(h % 2), in_=k_scr[h]).then_inc(bql2[h % 2], 16)
                sc.dma_start(out=vth(h % 2), in_=vt_scr[h]).then_inc(bql2[h % 2], 16)
            sc.wait_ge(bz1, 1)
            sc.activation(ztab, ztab, AF.Ln)
            sc.activation(ztab, ztab, AF.Exp, scale=0.5).then_inc(bz2, 1)
            for h in range(H):
                for j in range(4):
                    gj = h * 4 + j
                    if gj == 0:
                        sc.wait_ge(bv2, 1)
                    if gj > 0:
                        sc.wait_ge(bpv, gj)
                    for g in range(16):
                        sc.wait_ge(b2m, gj * 32 + 2 * g + 2)
                        pb_ = (g % 3) * 2
                        sc.activation(P_T[:, g * 1024:(g + 1) * 1024],
                                      PS[:, pb_ * 512:pb_ * 512 + 1024],
                                      AF.Exp, bias=negbt[:, gj:gj + 1]).then_inc(bpt, 1)
                if h + 2 < H:
                    sc.wait_ge(bcx, 4 * (h + 1))
                    sc.dma_start(out=qh((h + 2) % 2), in_=q_scr[h + 2]).then_inc(bql2[h % 2], 16)
                    sc.dma_start(out=kh((h + 2) % 2), in_=k_scr[h + 2]).then_inc(bql2[h % 2], 16)
                    sc.dma_start(out=vth((h + 2) % 2), in_=vt_scr[h + 2]).then_inc(bql2[h % 2], 16)
            sc.nop().then_inc(pb2b, 1)
            sc.wait_ge(pb2b, 5)

            sc.nop().then_inc(pb3, 1)
            sc.wait_ge(pb3, 5)

            sc.wait_ge(c3st2[0], 16 * 8)
            sc.wait_ge(c3st2[1], 16 * 8)
            for qt in range(NQT):
                xs = x1rb[:, (qt % 2) * 2048:(qt % 2 + 1) * 2048]
                sc.wait_ge(d5x2[qt % 2], 16 * (qt // 2 + 1))
                sc.wait_ge(d5c, 2 * qt + 1)
                sc.activation(col(8), col(7), AF.Ln)
                sc.activation(col(9 + qt % 2), col(8), AF.Exp, scale=-0.5).then_inc(d5a, 1)
                sc.wait_ge(d5c, 2 * qt + 2)
                if qt >= 2:
                    sc.wait_ge(d5t, 16 * (qt - 1))
                sc.activation(u2b[:, (qt % 2) * 2048:(qt % 2 + 1) * 2048], xs, AF.Identity,
                              bias=col(11 + qt % 2), scale=col(9 + qt % 2)).then_inc(d5u, 1)
                if qt >= 1:
                    ii = qt - 1
                    for dc in range(DC):
                        k = ii * DC + dc
                        sc.wait_ge(d5t, k + 1)
                        tp = bank_bf(0 + (k % 2), 128)
                        sc.activation(X1nT[:, dc * 2048 + ii * 128: dc * 2048 + (ii + 1) * 128],
                                      tp, AF.Identity).then_inc(d5v, 1)
                if qt == NQT - 1:
                    for dc in range(DC):
                        k = qt * DC + dc
                        sc.wait_ge(d5t, k + 1)
                        tp = bank_bf(0 + (k % 2), 128)
                        sc.activation(X1nT[:, dc * 2048 + qt * 128: dc * 2048 + (qt + 1) * 128],
                                      tp, AF.Identity).then_inc(d5v, 1)
            sc.nop().then_inc(pb35, 1)
            sc.wait_ge(pb35, 5)

            for g4 in range(4):
                for oc in range(64):
                    gi = g4 * 64 + oc
                    sc.wait_ge(e4g, gi + 1)
                    if gi >= 2:
                        sc.wait_ge(e4h, gi - 1)
                    sc.activation(sgb[:, (gi % 2) * 512:(gi % 2 + 1) * 512],
                                  bank(0 + 2 * (gi % 2)), AF.Silu).then_inc(e4s, 1)

        fsem = sem("fence_v")
        _fc = [0]

        @block.vector
        def _(v):
            def F(ins):
                _fc[0] += 1
                ins.then_inc(fsem, 1)
                v.wait_ge(fsem, _fc[0])
            v.wait_ge(s_const, 16 * 6)
            v.wait_ge(s_constg, 16 * 3)
            for i in range(NT):
                xs = xb[:, (i % 3) * 2048:(i % 3 + 1) * 2048]
                v.wait_ge(xl[i % 3], 16 * (i // 3 + 1))
                v.reduce_sum(col(0), xs, axis=AX)
                F(v.scalar_tensor_tensor(jnk1, xs, 1.0, xs, ALU.bypass, ALU.mult,
                                         accum_out=col(1)))
                F(v.tensor_scalar_mul(col(2), col(0), 1.0 / D))
                F(v.tensor_tensor(col(46), col(2), col(2), ALU.mult))
                F(v.tensor_scalar(col(3), col(1), 1.0 / D, col(46), ALU.mult, ALU.subtract))
                F(v.tensor_scalar_add(col(4), col(3), EPS))
                F(v.reciprocal(col(5), col(4)))
                F(v.tensor_scalar(col(6), col(3), col(5), EPS, ALU.mult, ALU.add))
                v.tensor_tensor(col(7), col(4), col(6), ALU.mult).then_inc(x1c, 1)
                v.wait_ge(x1a, i + 1)
                if i < NQT:
                    v.tensor_scalar_add(col(48 + i), col(15), 1.0)
                    v.tensor_scalar(col(64 + i), col(2), col(15), -1.0,
                                    ALU.mult, ALU.mult)
                v.tensor_scalar(col(11 + i % 2), col(2), col(9 + i % 2), -1.0,
                                ALU.mult, ALU.mult).then_inc(x1c, 1)

            v.nop().then_inc(pb1, 1)
            v.wait_ge(pb1, 5)

            for h in range(H):
                for r in range(3):
                    v.wait_ge(apb, h * 3 + r + 1)
                    scl = QSCALE if r == 0 else 1.0
                    v.tensor_scalar(col(16 + (h % 2) * 3 + r), PS[:, 512 * 6:512 * 6 + 1],
                                    scl, None, ALU.mult).then_inc(avb, 1)
                for rqk in range(NQK):
                    grp = h * NQK + rqk
                    v.wait_ge(art, grp + 1)
                    xr = xr0 if (rqk % 2 == 0) else xr1
                    rot = PS[0:64, 512 * (4 + rqk % 2):512 * (4 + rqk % 2) + 512]
                    if rqk < 4:
                        pos = rqk * 512
                        tgt = qbuf[0:64, rqk * 512:(rqk + 1) * 512]
                    else:
                        pos = (rqk - 4) * 512
                        tgt = kbuf[0:64, (rqk - 4) * 512:(rqk - 4 + 1) * 512]
                    v.tensor_tensor(t1b[0:64, :], xr[0:64, :], cosb[0:64, pos:pos + 512], ALU.mult)
                    v.tensor_tensor(t2b[0:64, :], rot, sinb[0:64, pos:pos + 512], ALU.mult)
                    v.tensor_add(tgt, t1b[0:64, :], t2b[0:64, :]).then_inc(arp, 1)
                for c in range(12):
                    v.wait_ge(anm, h * 12 + c + 1)
                    row = PS[0:1, 512 * (4 + c % 2):512 * (4 + c % 2) + 512]
                    if c < 4:
                        v.tensor_reduce(ntab[0:1, 4 * h + c:4 * h + c + 1], row,
                                        AX, ALU.max).then_inc(anr, 1)
                    elif c == 4:
                        v.tensor_reduce(ntab[0:1, 64 + h:64 + h + 1], row,
                                        AX, ALU.max).then_inc(anr, 1)
                    else:
                        F(v.tensor_reduce(ktmp, row, AX, ALU.max))
                        v.tensor_tensor(ntab[0:1, 64 + h:64 + h + 1],
                                        ntab[0:1, 64 + h:64 + h + 1], ktmp,
                                        ALU.max).then_inc(anr, 1)
                for kc in range(32):
                    gk = h * 32 + kc
                    v.wait_ge(avt, gk + 1)
                    tp = bank_bf(6 + (kc % 2), 128)
                    v.tensor_copy(vfb[:, kc * 128:(kc + 1) * 128], tp).then_inc(avc, 1)
            v.nop().then_inc(pb2a, 1)
            v.wait_ge(pb2a, 5)

            for h in range(H):
                ins = v.tensor_scalar(ztab[0:1, 4 * h:4 * h + 4],
                                      ntab[0:1, 4 * h:4 * h + 4],
                                      ntab[0:1, 64 + h:64 + h + 1], None, ALU.mult)
            ins.then_inc(bz1, 1)
            v.wait_ge(bz2, 1)
            v.tensor_scalar_mul(nbtab, ztab, -1.02).then_inc(bz2, 1)
            v.wait_ge(bt2, 1)
            v.tensor_copy(negbt, PS[:, 512 * 7:512 * 7 + 64]).then_inc(bv2, 1)
            for gj in range(64):
                h, j = gj // 4, gj % 4
                v.wait_ge(bpv, gj + 1)
                F(v.tensor_copy(lrow, PS[0:1, 512 * 7:512 * 7 + 512]))
                F(v.tensor_scalar_max(lrow, lrow, 1e-30))
                v.reciprocal(rrow, lrow).then_inc(blv, 1)
                v.wait_ge(bt3, gj + 1)
                v.tensor_copy(r_bc, bank(7)).then_inc(bv3, 1)
                v.tensor_tensor(Ctx[:, h * 2048 + j * 512: h * 2048 + (j + 1) * 512],
                                bank(6), r_bc, ALU.mult).then_inc(bcx, 1)
            v.nop().then_inc(pb2b, 1)
            v.wait_ge(pb2b, 5)

            for qt in range(NQT):
                xs = x0b[:, (qt % 2) * 2048:(qt % 2 + 1) * 2048]
                v.wait_ge(c3x2[qt % 2], 16 * (qt // 2 + 1))
                v.wait_ge(c3m, qt * 4 + 4)
                if qt >= 2:
                    v.wait_ge(c3st2[qt % 2], 16 * (qt // 2))
                pp = qt % 2
                xf = x1f[:, pp * 2048:(pp + 1) * 2048]
                v.scalar_tensor_tensor(xf, xs, col(48 + qt), PS[:, pp * 2048:(pp + 1) * 2048],
                                       ALU.mult, ALU.add)
                v.tensor_scalar_add(xf, xf, col(64 + qt)).then_inc(c3e, 1)
            v.nop().then_inc(pb3, 1)
            v.wait_ge(pb3, 5)

            for qt in range(NQT):
                xs = x1rb[:, (qt % 2) * 2048:(qt % 2 + 1) * 2048]
                v.wait_ge(d5x2[qt % 2], 16 * (qt // 2 + 1))
                v.reduce_sum(col(0), xs, axis=AX)
                F(v.scalar_tensor_tensor(jnk35, xs, 1.0, xs, ALU.bypass, ALU.mult,
                                         accum_out=col(1)))
                F(v.tensor_scalar_mul(col(2), col(0), 1.0 / D))
                F(v.tensor_tensor(col(46), col(2), col(2), ALU.mult))
                F(v.tensor_scalar(col(3), col(1), 1.0 / D, col(46), ALU.mult, ALU.subtract))
                v.tensor_scalar_add(col(7), col(3), EPS).then_inc(d5c, 1)
                v.wait_ge(d5a, qt + 1)
                v.tensor_scalar(col(11 + qt % 2), col(2), col(9 + qt % 2), -1.0,
                                ALU.mult, ALU.mult).then_inc(d5c, 1)

            v.nop().then_inc(pb35, 1)
            v.wait_ge(pb35, 5)

            for g4 in range(4):
                for oc in range(64):
                    gi = g4 * 64 + oc
                    v.wait_ge(e4u, gi + 1)
                    v.wait_ge(e4s, gi + 1)
                    v.tensor_tensor(HT[:, oc * 512:(oc + 1) * 512],
                                    bank(1 + 2 * (gi % 2)),
                                    sgb[:, (gi % 2) * 512:(gi % 2 + 1) * 512],
                                    ALU.mult).then_inc(e4h, 1)
                for p in range(2):
                    for qs in range(2):
                        qt = g4 * 4 + p * 2 + qs
                        v.wait_ge(e4bm, (g4 * 2 + p) * 64 + 64)
                        v.wait_ge(e4x2[qt % 2], 16 * (qt // 2 + 1))
                        if qt >= 2:
                            v.wait_ge(e4o2[qt % 2], 16 * (qt // 2))
                        x2slot = x2fB if (qt % 2 == 0) else x2fC
                        v.scalar_tensor_tensor(
                            x2slot,
                            PS[:, qs * 2048:(qs + 1) * 2048], 1.0,
                            x1r4[:, (qt % 2) * 2048:(qt % 2 + 1) * 2048],
                            ALU.mult, ALU.add).then_inc(e4e, 1)

    return nc


_NC_CACHE = None


def _get_nc():
    global _NC_CACHE
    if _NC_CACHE is None:
        _NC_CACHE = build_program()
    return _NC_CACHE


def _host_tables():
    import ml_dtypes
    rot = 64
    inv_freq = (1.0 / 10000.0 ** (np.arange(0, rot, 2, dtype=np.float32) / rot)).astype(np.float32)
    ang = np.arange(S, dtype=np.float32)[:, None] * inv_freq[None, :]
    ang = np.repeat(ang, 2, axis=-1)
    cos_g = np.cos(ang).T.astype(np.float32)
    sin_g = np.sin(ang).T.astype(np.float32)
    perm = np.zeros((64, 64), dtype=np.float32)
    for i in range(32):
        perm[2 * i + 1, 2 * i] = -1.0
        perm[2 * i, 2 * i + 1] = 1.0
    idb = np.eye(128, dtype=ml_dtypes.bfloat16)
    idf = np.eye(128, dtype=np.float32)
    ones_r = np.ones((1, 128), dtype=np.float32)
    ones_cb = np.ones((128, 1), dtype=ml_dtypes.bfloat16)
    return cos_g, sin_g, perm, idb, idf, ones_r, ones_cb


def kernel(L0_attn_X, L0_attn_ln_w, L0_attn_ln_b, L0_attn_W_in, L0_attn_W_out,
           L0_ffn_w_gate_up, L0_ffn_w_down):
    X = np.ascontiguousarray(np.asarray(L0_attn_X, dtype=np.float32))
    ln_w = np.asarray(L0_attn_ln_w, dtype=np.float32)
    ln_b = np.asarray(L0_attn_ln_b, dtype=np.float32)
    W_in = np.asarray(L0_attn_W_in, dtype=np.float32)
    W_out = np.ascontiguousarray(np.asarray(L0_attn_W_out, dtype=np.float32))
    w_gu = np.asarray(L0_ffn_w_gate_up, dtype=np.float32)
    w_dn = np.asarray(L0_ffn_w_down, dtype=np.float32)

    w_in_t = np.ascontiguousarray(
        W_in.reshape(DC, 128, 48, 128).transpose(2, 1, 0, 3).reshape(48, 128, DC * 128))
    wgu_t = np.ascontiguousarray(
        w_gu.T.reshape(DC, 128, 128, 128).transpose(2, 1, 0, 3).reshape(128, 128, DC * 128))
    wdn_t = np.ascontiguousarray(w_dn.T.reshape(64, 128, D))
    lnw_c = np.ascontiguousarray(ln_w.reshape(DC, 128).T)
    lnb_c = np.ascontiguousarray(ln_b.reshape(DC, 128).T)

    cos_g, sin_g, perm, idb, idf, ones_r, ones_cb = _host_tables()

    nc = _get_nc()
    in_maps = []
    for c in range(8):
        b, half = c // 2, c % 2
        off = half * SQ
        in_maps.append({
            "x": np.roll(X[b], -off, axis=0) if off else X[b],
            "w_in_t": w_in_t,
            "w_out": W_out,
            "wgu_t": wgu_t,
            "wdn_t": wdn_t,
            "lnw_c": lnw_c,
            "lnb_c": lnb_c,
            "cos_t": np.roll(cos_g, -off, axis=1) if off else cos_g,
            "sin_t": np.roll(sin_g, -off, axis=1) if off else sin_g,
            "perm": perm,
            "id_bf": idb,
            "id_f32": idf,
            "ones_r": ones_r,
            "ones_cb": ones_cb,
        })
    res = run_bass_kernel_spmd(nc, in_maps, core_ids=list(range(8)))
    out = np.empty((4, S, D), dtype=np.float32)
    for c in range(8):
        b, half = c // 2, c % 2
        out[b, half * SQ:(half + 1) * SQ, :] = np.asarray(res.results[c]["out"],
                                                          dtype=np.float32)
    return out
